# version 06
# baseline (speedup 1.0000x reference)
import numpy as np
import ml_dtypes
from contextlib import ExitStack

import concourse.bass as bass
import concourse.mybir as mybir
from concourse.bass import IndirectOffsetOnAxis
from concourse.bass_utils import run_bass_kernel_spmd

F32 = mybir.dt.float32
BF16 = mybir.dt.bfloat16
I32 = mybir.dt.int32
AF = mybir.ActivationFunctionType
ALU = mybir.AluOpType
AX = mybir.AxisListType
BF = ml_dtypes.bfloat16

NCORE = 8
SEQ = 4096
D = 1024
NT = SEQ // 128
NSLAB = SEQ // 512
SCALE_ATTN = float(128.0 ** -0.5)
ESTRIDE = 5248
CAP = 5120
NSLOT = 8 * ESTRIDE
MAPF = NSLOT // 128


def _chunks(w):
    K, F = w.shape
    nch = K // 128
    return np.ascontiguousarray(
        w.reshape(nch, 128, F).transpose(1, 0, 2).reshape(128, nch * F))


def _row(v):
    return np.ascontiguousarray(v.reshape(1, -1).astype(np.float32))


def host_prep(inp, core):
    f32 = lambda a: np.ascontiguousarray(np.asarray(a, dtype=np.float32))
    g = {}
    g["x"] = f32(inp["L0_attn_x"][core])
    g["gamma_r"] = _row(f32(inp["L0_attn_gamma"]))
    wqkvT = f32(inp["L0_attn_w_qkv"]).T
    w5 = wqkvT.reshape(8, 128, 3, 8, 128)
    g["wqkv_h"] = np.ascontiguousarray(
        w5.transpose(3, 1, 2, 0, 4).reshape(8 * 128, 3072))
    g["wout_c"] = _chunks(f32(inp["L0_attn_w_out"]).T)
    g["w1_c"] = _chunks(f32(inp["L0_ffn_W1"]))
    g["w2_c"] = _chunks(f32(inp["L0_ffn_W2"]))
    g["b1_t"] = np.ascontiguousarray(
        f32(inp["L0_ffn_b1"]).reshape(32, 128).T)
    g["b2_r"] = _row(f32(inp["L0_ffn_b2"]))
    g["inproj_c"] = _chunks(f32(inp["L1_attn_in_proj_w"]).T)
    for nm, key in (("qp", "L1_attn_q_proj_w"), ("kp", "L1_attn_k_proj_w"),
                    ("vp", "L1_attn_v_proj_w"), ("ip", "L1_attn_i_gate_w"),
                    ("fp", "L1_attn_f_gate_w"), ("op", "L1_attn_o_gate_w"),
                    ("outp", "L1_attn_out_proj_w")):
        g[nm + "_c"] = _chunks(f32(inp[key]).T)
    g["ib_r"] = _row(f32(inp["L1_attn_i_gate_b"]))
    g["fb_r"] = _row(f32(inp["L1_attn_f_gate_b"]))
    g["ob_r"] = _row(f32(inp["L1_attn_o_gate_b"]))
    g["qnw_r"] = _row(np.tile(f32(inp["L1_attn_q_norm_w"]), 16))
    g["knw_r"] = _row(np.tile(f32(inp["L1_attn_k_norm_w"]), 16))
    g["wg_c"] = _chunks(f32(inp["L1_ffn_wg"]))
    g["mw1_c"] = _chunks(f32(inp["L1_ffn_w1"][core]))
    g["mb1_t"] = np.ascontiguousarray(
        f32(inp["L1_ffn_b1"][core]).reshape(32, 128).T)
    g["mw2_c"] = _chunks(f32(inp["L1_ffn_w2"][core]))
    g["mb2_r"] = _row(f32(inp["L1_ffn_b2"][core]))
    g["svals"] = np.ascontiguousarray(
        (np.arange(SEQ, dtype=np.float32) + core * SEQ + 1.0)
        .reshape(NT, 128).T)
    oh8 = np.zeros((8, 1), np.float32); oh8[core, 0] = 1.0
    g["oh8"] = oh8
    sel = np.zeros((MAPF, 48), np.float32)
    for j in range(ESTRIDE // 128):
        sel[(core * ESTRIDE) // 128 + j, j] = 1.0
    g["sel_m"] = sel
    return g


INPUT_SPECS = [
    ("x", [SEQ, D]), ("gamma_r", [1, D]), ("wqkv_h", [8 * 128, 3072]),
    ("wout_c", [128, 8 * 1024]), ("w1_c", [128, 8 * 4096]),
    ("w2_c", [128, 32 * 1024]), ("b1_t", [128, 32]), ("b2_r", [1, D]),
    ("inproj_c", [128, 8 * 2048]), ("qp_c", [128, 8 * 1024]),
    ("kp_c", [128, 8 * 1024]), ("vp_c", [128, 8 * 1024]),
    ("ip_c", [128, 8 * 1024]), ("fp_c", [128, 8 * 1024]),
    ("op_c", [128, 8 * 1024]), ("outp_c", [128, 8 * 1024]),
    ("ib_r", [1, D]), ("fb_r", [1, D]), ("ob_r", [1, D]),
    ("qnw_r", [1, D]), ("knw_r", [1, D]), ("wg_c", [128, 64]),
    ("mw1_c", [128, 8 * 4096]), ("mb1_t", [128, 32]),
    ("mw2_c", [128, 32 * 1024]), ("mb2_r", [1, D]),
    ("svals", [128, NT]), ("oh8", [8, 1]), ("sel_m", [MAPF, 48]),
]


class Em:
    ENGS = ("sync", "gpsimd", "vector", "scalar", "tensor")

    def __init__(self, nc, ctx):
        self.nc = nc
        self.ctx = ctx
        self.q = {e: [] for e in self.ENGS}
        self.sem = {}
        self.cnt = {}
        self.dma_sems = set()
        for name in ("tv", "vv", "sv", "gv", "yv"):
            self.S(name)

    def S(self, name):
        if name not in self.sem:
            self.sem[name] = self.ctx.enter_context(self.nc.semaphore(name))
            self.cnt[name] = 0
        return self.sem[name]

    def emit(self, eng, fn):
        self.q[eng].append(fn)

    def wait(self, eng, sem, val):
        if val <= 0:
            return
        h = self.S(sem)
        self.emit(eng, lambda E: E.wait_ge(h, val))

    def bump(self, sem, k=1):
        self.S(sem)
        self.cnt[sem] += k
        return self.cnt[sem]

    def op(self, eng, fn, inc=None, k=1):
        if inc is None:
            self.emit(eng, lambda E: fn(E))
            return None
        h = self.S(inc)
        if k == 16:
            self.dma_sems.add(inc)
        v = self.bump(inc, k)
        self.emit(eng, lambda E: fn(E).then_inc(h, k))
        return v

    def drain(self, eng):
        self.emit(eng, lambda E: E.drain())

    def snap(self):
        return dict(self.cnt)

    def join(self, tag=None):
        sems = {"tensor": "jt", "vector": "jv", "scalar": "js",
                "gpsimd": "jg", "sync": "jy"}
        for name in sorted(self.dma_sems):
            self.wait("gpsimd", name, self.cnt[name])
        targets = {}
        for e in self.ENGS:
            targets[e] = self.op(e, lambda E: E.nop(), inc=sems[e])
        for e in self.ENGS:
            for o in self.ENGS:
                if o != e:
                    self.wait(e, sems[o], targets[o])


def build_program(debug_stream=None):
    nc = bass.Bass("TRN2", target_bir_lowering=False, num_devices=NCORE)
    ctx = ExitStack()

    dram_in = {}
    for name, shape in INPUT_SPECS:
        dram_in[name] = nc.dram_tensor(name, shape, F32, kind="ExternalInput").ap()
    y = nc.dram_tensor("y", [SEQ, D], F32, kind="ExternalOutput").ap()

    xr1 = nc.dram_tensor("xr1", [SEQ, D], F32, kind="Internal").ap()
    xr2 = nc.dram_tensor("xr2", [SEQ, D], F32, kind="Internal").ap()
    xr3 = nc.dram_tensor("xr3", [SEQ, D], F32, kind="Internal").ap()
    sp_q = nc.dram_tensor("sp_q", [SEQ, D], BF16, kind="Internal").ap()
    sp_k = nc.dram_tensor("sp_k", [SEQ, D], BF16, kind="Internal").ap()
    sp_v = nc.dram_tensor("sp_v", [SEQ, D], BF16, kind="Internal").ap()
    sp_i = nc.dram_tensor("sp_i", [SEQ, D], F32, kind="Internal").ap()
    sp_f = nc.dram_tensor("sp_f", [SEQ, D], F32, kind="Internal").ap()
    sp_g = nc.dram_tensor("sp_g", [SEQ, D], BF16, kind="Internal").ap()
    xn_my = nc.dram_tensor("xn_my", [SEQ, D], BF16, kind="Internal").ap()
    xn_all = nc.dram_tensor("xn_all", [NCORE * SEQ, D], BF16, kind="Internal").ap()
    cnts_in = nc.dram_tensor("cnts_in", [1, 8], F32, kind="Internal").ap()
    cnts_out = nc.dram_tensor("cnts_out", [8, 8], F32, kind="Internal").ap()
    smap_my = nc.dram_tensor("smap_my", [NSLOT + 128, 1], F32, kind="Internal").ap()
    smap_red = nc.dram_tensor("smap_red", [NSLOT + 128, 1], F32, kind="Internal").ap()
    eo_my = nc.dram_tensor("eo_my", [ESTRIDE, D], BF16, kind="Internal").ap()
    eo_all = nc.dram_tensor("eo_all", [NSLOT, D], BF16, kind="Internal").ap()

    ident_bf = nc.inline_tensor(np.eye(128, dtype=BF), name="ident_bf").ap()
    ident_f = nc.inline_tensor(np.eye(128, dtype=np.float32), name="ident_f").ap()
    ltincT = nc.inline_tensor(
        np.triu(np.ones((128, 128), np.float32)), name="ltincT").ap()
    lt8sT = nc.inline_tensor(
        np.triu(np.ones((8, 8), np.float32), 1), name="lt8sT").ap()
    ones_r = nc.inline_tensor(np.ones((1, 128), np.float32), name="ones_r").ap()
    ones_c = nc.inline_tensor(np.ones((128, 1), np.float32), name="ones_c").ap()
    ones_cb = nc.inline_tensor(np.ones((128, 1), dtype=BF), name="ones_cb").ap()
    ones_rb = nc.inline_tensor(np.ones((1, 128), dtype=BF), name="ones_rb").ap()
    iota8 = nc.inline_tensor(
        np.tile(np.arange(8, dtype=np.float32)[None, :], (128, 1)),
        name="iota8").ap()
    masks = []
    for j in range(4):
        m = (np.arange(512)[None, :] >= (128 * j + np.arange(128))[:, None])
        masks.append(nc.inline_tensor(m.astype(BF), name=f"mask{j}").ap())
    scanmask = nc.inline_tensor(
        np.tril(np.ones((128, 128), np.float32)).T.astype(np.float32),
        name="scanmask").ap()
    eye16r = nc.inline_tensor(
        np.eye(16, dtype=np.float32).reshape(1, 256), name="eye16r").ap()
    _c1 = np.full((128, 1), 0.0, np.float32); _c1[0] = 1.0; _c1[1:64] = 0.5
    _c2 = np.full((128, 1), 0.5, np.float32); _c2[0:64] = 1.0
    c1w = nc.inline_tensor(_c1, name="c1w").ap()
    c2w = nc.inline_tensor(_c2, name="c2w").ap()

    em = Em(nc, ctx)
    P = {}

    sb = lambda name, shape, dt=F32: ctx.enter_context(nc.sbuf_tensor(name, shape, dt))
    P["identb"] = sb("identb", [128, 128], BF16)
    P["identf"] = sb("identf", [128, 128], F32)
    P["ltincT"] = sb("s_ltincT", [128, 128], F32)
    P["onesr"] = sb("onesr", [1, 128], F32)
    P["onesc"] = sb("onesc", [128, 1], F32)
    P["onescb"] = sb("onescb", [128, 1], BF16)
    P["onesrb"] = sb("onesrb", [1, 128], BF16)
    P["gammab"] = sb("gammab", [128, D], F32)
    P["masks"] = sb("masksb", [128, 4 * 512], BF16)
    P["scanmask_sb"] = sb("scanmaskb", [128, 128], F32)
    P["c1w"] = sb("c1w_sb", [128, 1], F32)
    P["c2w"] = sb("c2w_sb", [128, 1], F32)
    P["consts"] = sb("constsb", [128, 8], F32)
    P["idx32"] = sb("idx32", [128, 4], I32)
    em_boot(em, nc, P, dram_in, dict(ident_bf=ident_bf, ident_f=ident_f,
            ltincT=ltincT, ones_r=ones_r, ones_c=ones_c, ones_cb=ones_cb,
            ones_rb=ones_rb, masks=masks, scanmask=scanmask, c1w=c1w, c2w=c2w))

    cvals = [1e-5, 1e-6, 1.0]
    P["cvals"] = cvals
    for i, v in enumerate(cvals):
        nc.const_aps.register(F32, v, P["consts"][:, i:i + 1])

    st_total = emit_all(em, nc, ctx, P, dram_in, dict(
        y=y, xr1=xr1, xr2=xr2, xr3=xr3, sp_q=sp_q, sp_k=sp_k, sp_v=sp_v,
        sp_i=sp_i, sp_f=sp_f, sp_g=sp_g, xn_my=xn_my, xn_all=xn_all,
        cnts_in=cnts_in, cnts_out=cnts_out, smap_my=smap_my, smap_red=smap_red,
        eo_my=eo_my, eo_all=eo_all,
        lt8sT=lt8sT, iota8=iota8, eye16r=eye16r),
        debug_stream=debug_stream)

    blk = ctx.enter_context(nc.Block())
    for eng_name, deco in (("sync", blk.sync), ("gpsimd", blk.gpsimd),
                           ("vector", blk.vector), ("scalar", blk.scalar),
                           ("tensor", blk.tensor)):
        thunks = em.q[eng_name]
        def body(E, thunks=thunks):
            for f in thunks:
                f(E)
        deco(body)
    ctx.close()
    return nc


class Arena:

    def __init__(self, nc, ctx, words):
        self.t = ctx.enter_context(nc.sbuf_tensor("ARENA", [128, words], F32))
        self.words = words
        self.off = 0

    def mark(self):
        return self.off

    def reset(self, off=0):
        self.off = off

    def al(self, nwords):
        a = self.off
        self.off += nwords
        assert self.off <= self.words, f"arena overflow {self.off}>{self.words}"
        return a

    def f32(self, n):
        a = self.al(n)
        return self.t[:, a:a + n]

    def bf16(self, n):
        a = self.al((n + 1) // 2)
        return self.t[:, a:a + (n + 1) // 2].bitcast(BF16)


def em_boot(em, nc, P, din, consts):
    def l(dst, src):
        em.op("sync", lambda E: E.dma_start(out=dst, in_=src), inc="bt", k=16)
    l(P["identb"][:], consts["ident_bf"])
    l(P["identf"][:], consts["ident_f"])
    l(P["ltincT"][:], consts["ltincT"])
    l(P["onesr"][:], consts["ones_r"])
    l(P["onesc"][:], consts["ones_c"])
    l(P["onescb"][:], consts["ones_cb"])
    l(P["onesrb"][:], consts["ones_rb"])
    for j in range(4):
        l(P["masks"][:, j * 512:(j + 1) * 512], consts["masks"][j])
    l(P["gammab"][:], din["gamma_r"].to_broadcast([128, D]))
    cinit = nc.inline_tensor(
        np.tile(np.array([[1e-5, 1e-6, 1.0, 0, 0, 0, 0, 0]], np.float32), (128, 1)),
        name="cinit").ap()
    l(P["consts"][:], cinit)
    l(P["scanmask_sb"][:], consts["scanmask"])
    l(P["c1w"][:], consts["c1w"])
    l(P["c2w"][:], consts["c2w"])
    P["boot_sd"] = em.cnt["bt"]


def emit_all(em, nc, ctx, P, din, dd, debug_stream=None):
    A = Arena(nc, ctx, nc.sbuf_bytes_remaining // 4 - 64)
    P["ps"] = [ctx.enter_context(nc.psum_tensor(f"ps{i}", [128, 512], F32))
               for i in range(6)]
    P["psR"] = ctx.enter_context(nc.psum_tensor("psR", [128, 512], F32))
    P["psT"] = ctx.enter_context(nc.psum_tensor("psT", [128, 1024], BF16))

    emit_A(em, nc, P, din, dd, A)
    if debug_stream != "xr1":
        emit_B(em, nc, P, din, dd, A)
    if debug_stream not in ("xr1", "xr2"):
        emit_C(em, nc, P, din, dd, A)
    if debug_stream not in ("xr1", "xr2", "xr3"):
        emit_D(em, nc, P, din, dd, A)

    if debug_stream is not None:
        src = dd[debug_stream]
        em.join()
        for i in range(4):
            em.op("gpsimd", lambda E, i=i: E.dma_start(
                out=dd["y"][i * 1024:(i + 1) * 1024, :],
                in_=src[i * 1024:(i + 1) * 1024, :]), inc="yout", k=16)
        em.wait("gpsimd", "yout", em.cnt["yout"])
    em.join()


def _ln_tile(em, P, A, bufs, xt_ap, out_bf_ap, l2gamma):
    bn, stat, xnf = bufs["bn"], bufs["stat"], bufs["xnf"]
    em.op("vector", lambda E: E.bn_stats(bn[:, 0:6], xt_ap[:, 0:512]))
    em.op("vector", lambda E: E.bn_stats(bn[:, 6:12], xt_ap[:, 512:1024]))
    em.drain("vector")
    vv = em.op("vector", lambda E: E.bn_aggr(stat[:, 0:2], bn[:, 0:12]), inc="vv")
    em.wait("scalar", "vv", vv)
    em.op("scalar", lambda E: E.activation(stat[:, 2:3], stat[:, 1:2], AF.Ln,
                                           bias=1e-5))
    em.drain("scalar")
    sv = em.op("scalar", lambda E: E.activation(stat[:, 3:4], stat[:, 2:3],
                                                AF.Exp, scale=-0.5), inc="sv")
    em.wait("vector", "sv", sv)
    em.op("vector", lambda E: E.scalar_tensor_tensor(
        stat[:, 4:5], stat[:, 0:1], -1.0, stat[:, 3:4], ALU.mult, ALU.mult))
    em.drain("vector")
    vv = em.op("vector", lambda E: E.tensor_scalar(
        xnf[:], xt_ap, stat[:, 3:4], stat[:, 4:5], ALU.mult, ALU.add), inc="vv")
    if not l2gamma:
        em.drain("vector")
        vv = em.op("vector", lambda E: E.tensor_copy(out_bf_ap, xnf[:]), inc="vv")
        return vv
    em.wait("scalar", "vv", vv)
    em.op("scalar", lambda E: E.activation(bufs["dead"], xnf[:], AF.Square,
                                           accum_out=stat[:, 5:6]))
    em.drain("scalar")
    em.op("scalar", lambda E: E.activation(stat[:, 6:7], stat[:, 5:6], AF.Ln,
                                           bias=1e-6))
    em.drain("scalar")
    sv = em.op("scalar", lambda E: E.activation(stat[:, 7:8], stat[:, 6:7],
                                                AF.Exp, scale=-0.5), inc="sv")
    em.wait("vector", "sv", sv)
    vv = em.op("vector", lambda E: E.scalar_tensor_tensor(
        out_bf_ap, xnf[:], stat[:, 7:8], P["gammab"][:], ALU.mult, ALU.mult),
        inc="vv")
    return vv


def _transpose_tile(em, P, xnb_ap, dst_cols, guard):
    if guard[1] > 0:
        em.wait("tensor", guard[0], guard[1])
    for c in range(8):
        f = lambda E, c=c: E.transpose(
            P["psT"][:, c * 128:(c + 1) * 128],
            xnb_ap[:, c * 128:(c + 1) * 128],
            P["identb"][:])
        if c < 7:
            em.op("tensor", f)
        else:
            tv = em.op("tensor", f, inc="tv")
    em.wait("vector", "tv", tv)
    for c in range(8):
        f = lambda E, c=c: E.tensor_copy(dst_cols(c), P["psT"][:, c * 128:(c + 1) * 128])
        if c < 7:
            em.op("vector", f)
        else:
            vv = em.op("vector", f, inc="vv")
    return tv, vv


def emit_A(em, nc, P, din, dd, A):
    ps = P["ps"]
    XNF = A.bf16(8 * 4096)
    OAL = A.bf16(8 * 4096)
    QF = A.bf16(4096)
    KF = A.bf16(4096)
    VT = A.bf16(4096)
    WH = [A.bf16(3072), A.bf16(3072)]
    XT = [A.f32(1024), A.f32(1024)]
    XNB = [A.bf16(1024), A.bf16(1024)]
    XNFW = A.f32(1024)
    BCS = XNFW[:, 0:512]
    BN = A.f32(12)
    STAT = A.f32(8)
    M4 = [A.f32(4), A.f32(4)]
    M8 = [A.f32(8), A.f32(8)]
    M4B = [A.bf16(4), A.bf16(4)]
    NEGMB = [A.bf16(512), A.bf16(512)]
    RECI = A.f32(512)
    PT = [A.bf16(512), A.bf16(512), A.bf16(512), A.bf16(512)]
    XO = [A.f32(1024), A.f32(1024)]
    DEAD = OAL[:, 0:2048].bitcast(F32)
    bufs = dict(bn=BN, stat=STAT, xnf=XNFW, dead=DEAD)

    em.wait("vector", "bt", P["boot_sd"])
    em.op("vector", lambda E: E.tensor_scalar_mul(P["gammab"][:], P["gammab"][:], 32.0))
    em.drain("vector")

    xt_guard = [0, 0]
    xnb_guard = [0, 0]
    psT_guard = 0
    for t in range(NT):
        b = t % 2
        em.wait("sync", "vv", xt_guard[b])
        sd = em.op("sync", lambda E, t=t, b=b: E.dma_start(
            out=XT[b][:], in_=din["x"][t * 128:(t + 1) * 128, :]),
            inc=f"xA{b}", k=16)
        em.wait("vector", f"xA{b}", sd)
        em.wait("vector", "tv", xnb_guard[b])
        ln_vv = _ln_tile(em, P, A, bufs, XT[b][:], XNB[b][:], l2gamma=True)
        xt_guard[b] = em.cnt["vv"]
        em.wait("tensor", "vv", ln_vv)
        tv, vv = _transpose_tile(
            em, P, XNB[b][:],
            lambda c, t=t: XNF[:, c * 4096 + t * 128: c * 4096 + (t + 1) * 128],
            ("vv", psT_guard))
        psT_guard = vv
        xnb_guard[b] = tv
    xnf_done_vv = em.cnt["vv"]

    wh_guard = [0, 0]
    ps_guard = [0, 0, 0, 0]
    ps_rel = ["vv", "vv", "vv", "vv"]
    pt_guard = [0, 0, 0, 0]
    ps4_guard = 0
    ps5_guard = 0
    psR_guard = 0
    oal_done = []

    em.wait("tensor", "vv", xnf_done_vv)
    for h in range(8):
        wb = h % 2
        if wh_guard[wb] > 0:
            em.wait("gpsimd", "tv", wh_guard[wb])
        gd = em.op("gpsimd", lambda E, h=h, wb=wb: E.dma_start(
            out=WH[wb][:], in_=din["wqkv_h"][h * 128:(h + 1) * 128, :]),
            inc=f"whA{wb}", k=16)
        em.wait("tensor", f"whA{wb}", gd)

        for which, dst, woff in (("q", QF, 0), ("k", KF, 1024)):
            for tt in range(8):
                pb = tt % 4
                if ps_guard[pb] > 0:
                    em.wait("tensor", ps_rel[pb], ps_guard[pb])
                for kc in range(8):
                    f = lambda E, kc=kc, tt=tt, wb=wb, woff=woff, pb=pb: E.matmul(
                        ps[pb][:], WH[wb][:, woff + kc * 128: woff + (kc + 1) * 128],
                        XNF[:, kc * 4096 + tt * 512: kc * 4096 + (tt + 1) * 512],
                        start=(kc == 0), stop=(kc == 7))
                    if kc < 7:
                        em.op("tensor", f)
                    else:
                        tv = em.op("tensor", f, inc="tv")
                em.wait("vector", "tv", tv)
                ps_guard[pb] = em.op("vector", lambda E, dst=dst, tt=tt, pb=pb:
                    E.tensor_copy(dst[:, tt * 512:(tt + 1) * 512], ps[pb][:]),
                    inc="vv")
                ps_rel[pb] = "vv"
        for rnd in range(8):
            pb = rnd % 4
            if ps_guard[pb] > 0:
                em.wait("tensor", ps_rel[pb], ps_guard[pb])
            for t4 in range(4):
                tk = rnd * 4 + t4
                for kc in range(8):
                    f = lambda E, kc=kc, tk=tk, t4=t4, wb=wb, pb=pb: E.matmul(
                        ps[pb][:, t4 * 128:(t4 + 1) * 128],
                        XNF[:, kc * 4096 + tk * 128: kc * 4096 + (tk + 1) * 128],
                        WH[wb][:, 2048 + kc * 128: 2048 + (kc + 1) * 128],
                        start=(kc == 0), stop=(kc == 7))
                    if kc < 7 or t4 < 3:
                        em.op("tensor", f)
                    else:
                        tv = em.op("tensor", f, inc="tv")
            em.wait("vector", "tv", tv)
            ps_guard[pb] = em.op("vector", lambda E, rnd=rnd, pb=pb:
                E.tensor_copy(VT[:, rnd * 512:(rnd + 1) * 512], ps[pb][:]),
                inc="vv")
            ps_rel[pb] = "vv"
        wh_guard[wb] = em.cnt["tv"]
        em.wait("tensor", "vv", em.cnt["vv"])

        def pass1_steps(g):
            gp = g % 2
            steps = []
            for qt in range(4):
                for j in range(g + 1):
                    def step(qt=qt, j=j, g=g, gp=gp):
                        pb = 2 + (j % 2)
                        if ps_guard[pb] > 0:
                            em.wait("tensor", ps_rel[pb], ps_guard[pb])
                        tv = em.op("tensor", lambda E, g=g, qt=qt, j=j, pb=pb:
                            E.matmul(ps[pb][:],
                                     QF[:, (g * 4 + qt) * 128:(g * 4 + qt + 1) * 128],
                                     KF[:, j * 512:(j + 1) * 512],
                                     start=True, stop=True), inc="tv")
                        em.wait("vector", "tv", tv)
                        ps_guard[pb] = em.op("vector", lambda E, j=j, pb=pb, gp=gp:
                            E.reduce_max(M8[gp][:, j:j + 1], ps[pb][:], axis=AX.X),
                            inc="vv")
                        ps_rel[pb] = "vv"
                        if j == g:
                            em.drain("vector")
                            em.op("vector", lambda E, qt=qt, g=g, gp=gp: E.reduce_max(
                                M4[gp][:, qt:qt + 1], M8[gp][:, 0:g + 1], axis=AX.X))
                            em.drain("vector")
                    steps.append(step)
            return steps

        def pass1_tail(g):
            nonlocal psR_guard
            gp = g % 2
            vv = em.op("vector", lambda E, gp=gp: E.tensor_copy(
                M4B[gp][:], M4[gp][:]), inc="vv")
            em.wait("tensor", "vv", max(vv, psR_guard))
            for qt in range(4):
                f = lambda E, qt=qt, gp=gp: E.matmul(
                    P["psR"][0:1, qt * 128:(qt + 1) * 128], M4B[gp][:, qt:qt + 1],
                    P["identb"][:], start=True, stop=True)
                if qt < 3:
                    em.op("tensor", f)
                else:
                    tv = em.op("tensor", f, inc="tv")
            em.wait("vector", "tv", tv)
            psR_guard = em.op("vector", lambda E, gp=gp: E.tensor_scalar_mul(
                NEGMB[gp][0:1, :], P["psR"][0:1, 0:512], -1.0), inc="vv")

        for s1 in pass1_steps(0):
            s1()
        pass1_tail(0)
        for g in range(8):
            gp = g % 2
            nxt = pass1_steps(g + 1) if g < 7 else []
            nch = 4 * g + 4
            negm_vv = psR_guard
            for c in range(nch):
                pb = c % 2
                if ps_guard[pb] > 0:
                    em.wait("tensor", ps_rel[pb], ps_guard[pb])
                em.wait("tensor", "vv", negm_vv)
                em.op("tensor", lambda E, c=c, pb=pb, g=g: E.matmul(
                    ps[pb][:], KF[:, c * 128:(c + 1) * 128],
                    QF[:, g * 512:(g + 1) * 512], start=True, stop=False))
                tv = em.op("tensor", lambda E, pb=pb, gp=gp: E.matmul(
                    ps[pb][:], P["onesrb"][0:1, :], NEGMB[gp][0:1, :],
                    start=False, stop=True), inc="tv")
                if pt_guard[pb] > 0:
                    em.wait("scalar", "tv", pt_guard[pb])
                em.wait("scalar", "tv", tv)
                sv = em.op("scalar", lambda E, pb=pb: E.activation(
                    PT[pb][:], ps[pb][:], AF.Exp, scale=SCALE_ATTN), inc="sv")
                ps_guard[pb] = sv
                ps_rel[pb] = "sv"
                dep = ("sv", sv)
                if c >= 4 * g:
                    j = c - 4 * g
                    em.wait("vector", "sv", sv)
                    vv = em.op("vector", lambda E, pb=pb, j=j: E.tensor_mul(
                        PT[pb][:], PT[pb][:], P["masks"][:, j * 512:(j + 1) * 512]),
                        inc="vv")
                    dep = ("vv", vv)
                em.wait("tensor", dep[0], dep[1])
                if c == 0:
                    em.wait("tensor", "vv", ps4_guard)
                    em.wait("tensor", "vv", ps5_guard)
                em.op("tensor", lambda E, c=c, pb=pb, first=(c == 0),
                      last=(c == nch - 1): E.matmul(
                    ps[4][:], VT[:, c * 128:(c + 1) * 128], PT[pb][:],
                    start=first, stop=last))
                tv = em.op("tensor", lambda E, pb=pb, first=(c == 0),
                           last=(c == nch - 1): E.matmul(
                    ps[5][0:1, :], P["onescb"][:, 0:1], PT[pb][:],
                    start=first, stop=last), inc="tv")
                pt_guard[pb] = tv
                if nxt:
                    nxt.pop(0)()
            em.wait("vector", "tv", tv)
            ps5_guard = em.op("vector", lambda E: E.reciprocal(
                RECI[0:1, :], ps[5][0:1, 0:512]), inc="vv")
            em.wait("tensor", "vv", ps5_guard)
            em.wait("tensor", "vv", psR_guard)
            tv = em.op("tensor", lambda E: E.matmul(
                P["psR"][:], P["onesr"][0:1, :], RECI[0:1, :],
                start=True, stop=True), inc="tv")
            em.wait("vector", "tv", tv)
            em.op("vector", lambda E: E.tensor_copy(BCS[:], P["psR"][:, 0:512]))
            em.drain("vector")
            psR_guard = em.cnt["vv"] + 1
            ps4_guard = em.op("vector", lambda E, h=h, g=g: E.tensor_mul(
                OAL[:, h * 4096 + g * 512: h * 4096 + (g + 1) * 512],
                ps[4][:], BCS[:]), inc="vv")
            for s1 in nxt:
                s1()
            if g < 7:
                pass1_tail(g + 1)
    oal_vv = em.cnt["vv"]

    WOUT = XNF[:, 0:8192]
    em.wait("gpsimd", "tv", em.cnt["tv"])
    gd = em.op("gpsimd", lambda E: E.dma_start(out=WOUT[:, 0:8192],
               in_=din["wout_c"]), inc="woA", k=16)
    em.wait("tensor", "woA", gd)
    em.wait("tensor", "vv", oal_vv)
    xo_guard = [0, 0]
    xt_guard = [em.cnt["vv"], em.cnt["vv"]]
    psout_guard = [0, 0]
    for t in range(NT):
        b = t % 2
        em.wait("sync", "vv", xt_guard[b])
        sd = em.op("sync", lambda E, t=t, b=b: E.dma_start(
            out=XT[b][:], in_=din["x"][t * 128:(t + 1) * 128, :]),
            inc=f"xA{b}", k=16)
        for half in range(2):
            if psout_guard[half] > 0:
                em.wait("tensor", "vv", psout_guard[half])
            for hh in range(8):
                f = lambda E, hh=hh, t=t, half=half: E.matmul(
                    ps[half][:], OAL[:, hh * 4096 + t * 128: hh * 4096 + (t + 1) * 128],
                    WOUT[:, hh * 1024 + half * 512: hh * 1024 + (half + 1) * 512],
                    start=(hh == 0), stop=(hh == 7))
                if hh < 7:
                    em.op("tensor", f)
                else:
                    tv = em.op("tensor", f, inc="tv")
            em.wait("vector", "tv", tv)
            em.wait("vector", f"xA{b}", sd)
            if half == 0 and xo_guard[b] > 0:
                em.wait("vector", f"xoS{b}", xo_guard[b])
            psout_guard[half] = em.op(
                "vector", lambda E, b=b, half=half: E.tensor_add(
                    XO[b][:, half * 512:(half + 1) * 512], ps[half][:],
                    XT[b][:, half * 512:(half + 1) * 512]), inc="vv")
        xt_guard[b] = em.cnt["vv"]
        em.wait("gpsimd", "vv", em.cnt["vv"])
        xo_guard[b] = em.op("gpsimd", lambda E, t=t, b=b: E.dma_start(
            out=dd["xr1"][t * 128:(t + 1) * 128, :], in_=XO[b][:]),
            inc=f"xoS{b}", k=16)
    A.reset()


def emit_B(em, nc, P, din, dd, A):
    em.join()
    ps = P["ps"]
    W1S = A.bf16(32768)
    W2S = A.bf16(32768)
    HFM = A.bf16(16384)
    XNFS = A.bf16(4096)
    XTS = [A.f32(1024) for _ in range(4)]
    XNB = [A.bf16(1024)]
    XNB.append(XNB[0])
    XNFW = A.f32(1024)
    BCS = XNFW[:, 0:512]
    BN = A.f32(12)
    STAT = A.f32(8)
    B1S = A.f32(32)
    B2B = A.bf16(1024)
    XO = [A.f32(1024)]
    XO.append(XO[0])
    DEAD = A.f32(4)
    bufs = dict(bn=BN, stat=STAT, xnf=XNFW, dead=DEAD)

    w1v = em.op("gpsimd", lambda E: E.dma_start(out=W1S[:], in_=din["w1_c"]),
                inc="w1B", k=16)
    w2v = em.op("gpsimd", lambda E: E.dma_start(out=W2S[:], in_=din["w2_c"]),
                inc="w2B", k=16)
    bbv = em.op("sync", lambda E: E.dma_start(out=B1S[:], in_=din["b1_t"]),
                inc="bB", k=16)
    b2v = em.op("gpsimd", lambda E: E.dma_start(
        out=B2B[:], in_=din["b2_r"].to_broadcast([128, D])), inc="b2B", k=16)
    em.wait("tensor", "w1B", w1v)
    em.wait("tensor", "w2B", w2v)
    em.wait("scalar", "bB", bbv)
    em.wait("vector", "b2B", b2v)

    xt_guard = [0, 0, 0, 0]
    xnb_guard = [0, 0]
    psT_guard = 0
    xnfs_guard = 0
    hfm_guard = 0
    psH_guard = [0, 0, 0, 0]
    psO_guard = [0, 0]
    xo_guard = [0, 0]
    for s in range(NSLAB):
        for tt in range(4):
            t = s * 4 + tt
            b = 0
            em.wait("sync", "vv", xt_guard[tt])
            sd = em.op("sync", lambda E, t=t, tt=tt: E.dma_start(
                out=XTS[tt][:], in_=dd["xr1"][t * 128:(t + 1) * 128, :]),
                inc=f"xB{tt}", k=16)
            em.wait("vector", f"xB{tt}", sd)
            em.wait("vector", "tv", xnb_guard[b])
            ln_vv = _ln_tile(em, P, A, bufs, XTS[tt][:], XNB[b][:], l2gamma=False)
            em.wait("tensor", "vv", ln_vv)
            if psT_guard:
                em.wait("tensor", "vv", psT_guard)
            for c in range(8):
                f = lambda E, c=c: E.transpose(
                    P["psT"][:, c * 128:(c + 1) * 128],
                    XNB[b][:, c * 128:(c + 1) * 128], P["identb"][:])
                if c < 7:
                    em.op("tensor", f)
                else:
                    tv = em.op("tensor", f, inc="tv")
            em.wait("vector", "tv", tv)
            em.wait("vector", "tv", xnfs_guard)
            for c in range(8):
                f = lambda E, c=c, tt=tt: E.tensor_copy(
                    XNFS[:, c * 512 + tt * 128: c * 512 + (tt + 1) * 128],
                    P["psT"][:, c * 128:(c + 1) * 128])
                if c < 7:
                    em.op("vector", f)
                else:
                    vv = em.op("vector", f, inc="vv")
            psT_guard = vv
            xnb_guard[0] = tv
        xnfs_vv = em.cnt["vv"]
        em.wait("tensor", "vv", xnfs_vv)
        em.wait("scalar", "tv", hfm_guard)
        for ic in range(32):
            pb = ic % 4
            if psH_guard[pb]:
                em.wait("tensor", "sv", psH_guard[pb])
            for kc in range(8):
                f = lambda E, kc=kc, ic=ic, pb=pb: E.matmul(
                    ps[pb][:], W1S[:, kc * 4096 + ic * 128: kc * 4096 + (ic + 1) * 128],
                    XNFS[:, kc * 512:(kc + 1) * 512], start=(kc == 0), stop=(kc == 7))
                if kc < 7:
                    em.op("tensor", f)
                else:
                    tv = em.op("tensor", f, inc="tv")
            em.wait("scalar", "tv", tv)
            psH_guard[pb] = em.op("scalar", lambda E, ic=ic, pb=pb: E.activation(
                HFM[:, ic * 512:(ic + 1) * 512], ps[pb][:], AF.Relu,
                bias=B1S[:, ic:ic + 1]), inc="sv")
        h_sv = em.cnt["sv"]
        xnfs_guard = em.cnt["tv"]
        em.wait("tensor", "sv", h_sv)
        for tt in range(4):
            t = s * 4 + tt
            b = 0
            for half in range(2):
                if psO_guard[half]:
                    em.wait("tensor", "vv", psO_guard[half])
                for ic in range(32):
                    f = lambda E, ic=ic, tt=tt, half=half: E.matmul(
                        ps[4 + half][:],
                        HFM[:, ic * 512 + tt * 128: ic * 512 + (tt + 1) * 128],
                        W2S[:, ic * 1024 + half * 512: ic * 1024 + (half + 1) * 512],
                        start=(ic == 0), stop=(ic == 31))
                    if ic < 31:
                        em.op("tensor", f)
                    else:
                        tv = em.op("tensor", f, inc="tv")
                em.wait("vector", "tv", tv)
                if xo_guard[b]:
                    em.wait("vector", f"xoB{b}", xo_guard[b])
                em.op("vector", lambda E, b=b, half=half: E.tensor_add(
                    XO[b][:, half * 512:(half + 1) * 512], ps[4 + half][:],
                    B2B[:, half * 512:(half + 1) * 512]))
                em.drain("vector")
                psO_guard[half] = em.op(
                    "vector", lambda E, b=b, half=half, tt=tt: E.tensor_add(
                        XO[b][:, half * 512:(half + 1) * 512],
                        XO[b][:, half * 512:(half + 1) * 512],
                        XTS[tt][:, half * 512:(half + 1) * 512]), inc="vv")
            xt_guard[tt] = em.cnt["vv"]
            em.wait("gpsimd", "vv", em.cnt["vv"])
            xo_guard[b] = em.op("gpsimd", lambda E, t=t, b=b: E.dma_start(
                out=dd["xr2"][t * 128:(t + 1) * 128, :], in_=XO[b][:]),
                inc=f"xoB{b}", k=16)
        hfm_guard = em.cnt["tv"]
    A.reset()


def emit_C(em, nc, P, din, dd, A):
    em.join()
    ps = P["ps"]

    INPC = A.bf16(16384)
    WPC = {}
    for nm in ("qp", "kp", "vp", "ip", "fp", "op"):
        WPC[nm] = A.bf16(8192)
    XNFS = A.bf16(4096)
    XIF = A.bf16(4096)
    XT = [A.f32(1024), A.f32(1024)]
    XNB = [A.bf16(1024)]
    XNFW = A.f32(1024)
    BCS = XNFW[:, 0:512]
    BN = A.f32(12)
    STAT = A.f32(8)
    DEAD = A.f32(4)
    IBB = A.f32(1024)
    FBB = A.f32(1024)
    OBB = A.f32(1024)
    PRJ = [A.bf16(1024), A.bf16(1024)]
    PRF = [A.f32(1024), A.f32(1024)]
    OGT = A.f32(1024)
    XGT = A.f32(1024)
    GT = [A.bf16(1024), A.bf16(1024)]
    bufs = dict(bn=BN, stat=STAT, xnf=XNFW, dead=DEAD)

    wv = {}
    wv["inp"] = em.op("gpsimd", lambda E: E.dma_start(out=INPC[:], in_=din["inproj_c"]),
                      inc="wC", k=16)
    for nm in ("qp", "kp", "vp", "ip", "fp", "op"):
        wv[nm] = em.op("gpsimd", lambda E, nm=nm: E.dma_start(
            out=WPC[nm][:], in_=din[nm + "_c"]), inc="wC", k=16)
    wC_all = em.cnt["wC"]
    bv = em.op("sync", lambda E: E.dma_start(
        out=IBB[:], in_=din["ib_r"].to_broadcast([128, D])), inc="bC", k=16)
    bv = em.op("sync", lambda E: E.dma_start(
        out=FBB[:], in_=din["fb_r"].to_broadcast([128, D])), inc="bC", k=16)
    bv = em.op("sync", lambda E: E.dma_start(
        out=OBB[:], in_=din["ob_r"].to_broadcast([128, D])), inc="bC", k=16)
    em.wait("tensor", "wC", wC_all)
    em.wait("vector", "bC", bv)

    xt_guard = [0, 0]
    xnb_guard = [0]
    psT_guard = 0
    xnfs_guard = 0
    prj_guard = [0, 0]
    prf_guard = [0, 0]
    gt_guard = [0, 0]
    psp_guard = [0, 0]
    for s in range(NSLAB):
        for tt in range(4):
            t = s * 4 + tt
            b = tt % 2
            em.wait("sync", "vv", xt_guard[b])
            sd = em.op("sync", lambda E, t=t, b=b: E.dma_start(
                out=XT[b][:], in_=dd["xr2"][t * 128:(t + 1) * 128, :]),
                inc=f"xC{b}", k=16)
            em.wait("vector", f"xC{b}", sd)
            em.wait("vector", "tv", xnb_guard[0])
            ln_vv = _ln_tile(em, P, A, bufs, XT[b][:], XNB[0][:], l2gamma=False)
            xt_guard[b] = em.cnt["vv"]
            em.wait("tensor", "vv", ln_vv)
            if psT_guard:
                em.wait("tensor", "vv", psT_guard)
            for c in range(8):
                f = lambda E, c=c: E.transpose(
                    P["psT"][:, c * 128:(c + 1) * 128],
                    XNB[0][:, c * 128:(c + 1) * 128], P["identb"][:])
                em.op("tensor", f) if c < 7 else None
                if c == 7:
                    tv = em.op("tensor", f, inc="tv")
            em.wait("vector", "tv", tv)
            em.wait("vector", "tv", xnfs_guard)
            for c in range(8):
                f = lambda E, c=c, tt=tt: E.tensor_copy(
                    XNFS[:, c * 512 + tt * 128: c * 512 + (tt + 1) * 128],
                    P["psT"][:, c * 128:(c + 1) * 128])
                if c < 7:
                    em.op("vector", f)
                else:
                    vv = em.op("vector", f, inc="vv")
            psT_guard = vv
            xnb_guard[0] = tv
        xnfs_vv = em.cnt["vv"]
        em.wait("tensor", "vv", xnfs_vv)
        for io in range(8):
            pb = io % 2
            if psp_guard[pb]:
                em.wait("tensor", "vv", psp_guard[pb])
            for kc in range(8):
                f = lambda E, kc=kc, io=io, pb=pb: E.matmul(
                    ps[pb][:], INPC[:, kc * 2048 + io * 128: kc * 2048 + (io + 1) * 128],
                    XNFS[:, kc * 512:(kc + 1) * 512], start=(kc == 0), stop=(kc == 7))
                if kc < 7:
                    em.op("tensor", f)
                else:
                    tv = em.op("tensor", f, inc="tv")
            em.wait("vector", "tv", tv)
            psp_guard[pb] = em.op("vector", lambda E, io=io, pb=pb: E.tensor_copy(
                XIF[:, io * 512:(io + 1) * 512], ps[pb][:]), inc="vv")
        xif_vv = em.cnt["vv"]
        em.wait("tensor", "vv", xif_vv)
        ps23e = 0
        for tt in range(4):
            t = s * 4 + tt
            for j, (nm, spname) in enumerate((("qp", "sp_q"), ("kp", "sp_k"),
                                              ("vp", "sp_v"))):
                pb = j % 2
                em.wait("tensor", "vv", ps23e)
                if prj_guard[pb]:
                    em.wait("vector", f"pC{pb}", prj_guard[pb])
                for half in range(2):
                    pp = 2 + half
                    for io in range(8):
                        f = lambda E, io=io, tt=tt, half=half, nm=nm, pp=pp: E.matmul(
                            ps[pp][:], XIF[:, io * 512 + tt * 128: io * 512 + (tt + 1) * 128],
                            WPC[nm][:, io * 1024 + half * 512: io * 1024 + (half + 1) * 512],
                            start=(io == 0), stop=(io == 7))
                        if io < 7 or half < 1:
                            em.op("tensor", f)
                        else:
                            tv = em.op("tensor", f, inc="tv")
                em.wait("vector", "tv", tv)
                em.op("vector", lambda E, pb=pb: E.tensor_copy(
                    PRJ[pb][:, 0:512], ps[2][:]))
                vv = em.op("vector", lambda E, pb=pb: E.tensor_copy(
                    PRJ[pb][:, 512:1024], ps[3][:]), inc="vv")
                ps23e = vv
                em.wait("gpsimd", "vv", vv)
                prj_guard[pb] = em.op("gpsimd", lambda E, t=t, pb=pb, spname=spname:
                    E.dma_start(out=dd[spname][t * 128:(t + 1) * 128, :],
                                in_=PRJ[pb][:]), inc=f"pC{pb}", k=16)
            for j, (nm, spname, BB) in enumerate((("ip", "sp_i", IBB),
                                                  ("fp", "sp_f", FBB))):
                pb = j % 2
                em.wait("tensor", "vv", ps23e)
                if prf_guard[pb]:
                    em.wait("vector", f"pF{pb}", prf_guard[pb])
                for half in range(2):
                    pp = 2 + half
                    for io in range(8):
                        f = lambda E, io=io, tt=tt, half=half, nm=nm, pp=pp: E.matmul(
                            ps[pp][:], XIF[:, io * 512 + tt * 128: io * 512 + (tt + 1) * 128],
                            WPC[nm][:, io * 1024 + half * 512: io * 1024 + (half + 1) * 512],
                            start=(io == 0), stop=(io == 7))
                        if io < 7 or half < 1:
                            em.op("tensor", f)
                        else:
                            tv = em.op("tensor", f, inc="tv")
                em.wait("vector", "tv", tv)
                em.op("vector", lambda E, pb=pb, BB=BB: E.tensor_add(
                    PRF[pb][:, 0:512], ps[2][:], BB[:, 0:512]))
                vv = em.op("vector", lambda E, pb=pb, BB=BB: E.tensor_add(
                    PRF[pb][:, 512:1024], ps[3][:], BB[:, 512:1024]), inc="vv")
                ps23e = vv
                em.wait("gpsimd", "vv", vv)
                prf_guard[pb] = em.op("gpsimd", lambda E, t=t, pb=pb, spname=spname:
                    E.dma_start(out=dd[spname][t * 128:(t + 1) * 128, :],
                                in_=PRF[pb][:]), inc=f"pF{pb}", k=16)
            gb = tt % 2
            em.wait("tensor", "vv", ps23e)
            if gt_guard[gb]:
                em.wait("vector", f"gC{gb}", gt_guard[gb])
            for half in range(2):
                pp = 2 + half
                for io in range(8):
                    f = lambda E, io=io, tt=tt, half=half, pp=pp: E.matmul(
                        ps[pp][:], XIF[:, io * 512 + tt * 128: io * 512 + (tt + 1) * 128],
                        WPC["op"][:, io * 1024 + half * 512: io * 1024 + (half + 1) * 512],
                        start=(io == 0), stop=(io == 7))
                    if io < 7 or half < 1:
                        em.op("tensor", f)
                    else:
                        tv = em.op("tensor", f, inc="tv")
            em.wait("vector", "tv", tv)
            em.op("vector", lambda E: E.tensor_add(OGT[:, 0:512], ps[2][:],
                                                   OBB[:, 0:512]))
            vv = em.op("vector", lambda E: E.tensor_add(OGT[:, 512:1024], ps[3][:],
                                                        OBB[:, 512:1024]), inc="vv")
            ps23e = vv
            for half in range(2):
                pp = 2 + half
                em.wait("tensor", "vv", vv)
                for kc in range(8):
                    f = lambda E, kc=kc, tt=tt, half=half, pp=pp: E.matmul(
                        ps[pp][:], XNFS[:, kc * 512 + tt * 128: kc * 512 + (tt + 1) * 128],
                        INPC[:, kc * 2048 + 1024 + half * 512: kc * 2048 + 1024 + (half + 1) * 512],
                        start=(kc == 0), stop=(kc == 7))
                    if kc < 7 or half < 1:
                        em.op("tensor", f)
                    else:
                        tv = em.op("tensor", f, inc="tv")
            em.wait("vector", "tv", tv)
            em.op("vector", lambda E: E.tensor_copy(XGT[:, 0:512], ps[2][:]))
            vv = em.op("vector", lambda E: E.tensor_copy(XGT[:, 512:1024],
                                                         ps[3][:]), inc="vv")
            ps23e = vv
            em.wait("scalar", "vv", vv)
            em.op("scalar", lambda E: E.activation(OGT[:], OGT[:], AF.Sigmoid))
            sv = em.op("scalar", lambda E: E.activation(XGT[:], XGT[:],
                                                        AF.Sigmoid), inc="sv")
            em.wait("vector", "sv", sv)
            vv = em.op("vector", lambda E, gb=gb: E.tensor_mul(
                GT[gb][:], OGT[:], XGT[:]), inc="vv")
            em.wait("gpsimd", "vv", vv)
            gt_guard[gb] = em.op("gpsimd", lambda E, t=t, gb=gb: E.dma_start(
                out=dd["sp_g"][t * 128:(t + 1) * 128, :], in_=GT[gb][:]),
                inc=f"gC{gb}", k=16)
            xnfs_guard = em.cnt["tv"]
    c1_done = dict(em.cnt)
    A.reset()

    em.join()
    OUTPC = A.bf16(8192)
    QNB = A.f32(1024)
    KNB = A.f32(1024)
    EYE16R = A.f32(256)
    LQ = [A.bf16(1024), A.bf16(1024)]
    LK = [A.bf16(1024), A.bf16(1024)]
    LV = [A.bf16(1024), A.bf16(1024)]
    LI = [A.f32(1024), A.f32(1024)]
    LF = [A.f32(1024), A.f32(1024)]
    LG = [A.bf16(1024), A.bf16(1024)]
    XT2 = [A.f32(1024), A.f32(1024)]
    NL = A.f32(1024)
    LROW = A.f32(1024)
    R1 = A.f32(1024)
    R2 = A.f32(1024)
    K1 = A.bf16(1024)
    K1F = A.bf16(1024)
    A1S = A.f32(1024)
    ARGQ = A.f32(1024)
    ARGK = A.f32(1024)
    ARGB = A.f32(1024)
    SQ = A.f32(1024)
    MS = A.f32(32)
    RSE = SQ
    QN = A.f32(1024)
    KN = A.f32(1024)
    QA = A.bf16(1024)
    QI = A.f32(1024)
    KT = A.bf16(1024)
    KB = A.bf16(1024)
    LFW = A.f32(1024)
    QAF = A.bf16(2048)
    QIF = A.f32(2048)
    KTF = A.bf16(2048)
    ASB = A.bf16(128)
    HT = A.f32(1024)
    HG = A.bf16(1024)
    HGF = A.bf16(1024)
    CAU = A.f32(1040)
    NLE = A.f32(16)
    PF16 = A.f32(16)
    DCOL = A.f32(4)
    XO2 = [A.f32(1024)]
    bv2 = em.op("sync", lambda E: E.dma_start(
        out=QNB[:], in_=din["qnw_r"].to_broadcast([128, D])), inc="bC2", k=16)
    bv2 = em.op("sync", lambda E: E.dma_start(
        out=KNB[:], in_=din["knw_r"].to_broadcast([128, D])), inc="bC2", k=16)
    wv2 = em.op("gpsimd", lambda E: E.dma_start(out=OUTPC[:], in_=din["outp_c"]),
                inc="wC2", k=16)
    eyev = em.op("sync", lambda E: E.dma_start(out=EYE16R[0:1, :],
                 in_=dd["eye16r"]), inc="bC2", k=16)
    em.wait("vector", "bC2", em.cnt["bC2"])
    em.wait("tensor", "wC2", wv2)
    em.wait("tensor", "bC2", em.cnt["bC2"])

    em.op("vector", lambda E: E.memset(CAU[:], 0.0))
    em.op("vector", lambda E: E.memset(ps[4][:, 0:128], 0.0))
    em.drain("vector")

    ld_guard = {}
    for st in ("q", "k", "v", "i", "f", "g", "x"):
        ld_guard[st] = [0, 0]
    LD = dict(q=(LQ, "sp_q"), k=(LK, "sp_k"), v=(LV, "sp_v"),
              i=(LI, "sp_i"), f=(LF, "sp_f"), g=(LG, "sp_g"))
    sdv = {}
    xo_guard2 = 0
    cau_vv = em.cnt["vv"]
    ps01_rel = 0
    psPT_guard = 0
    psPTb_guard = 0

    for t in range(NT):
        b = t % 2
        for st, (bufs2, spname) in LD.items():
            em.wait("sync", "vv", ld_guard[st][b])
            sdv[st] = em.op("sync", lambda E, t=t, b=b, bufs2=bufs2, spname=spname:
                E.dma_start(out=bufs2[b][:],
                            in_=dd[spname][t * 128:(t + 1) * 128, :]),
                inc=f"L{st}{b}", k=16)
        em.wait("sync", "vv", ld_guard["x"][b])
        sdv["x"] = em.op("sync", lambda E, t=t, b=b: E.dma_start(
            out=XT2[b][:], in_=dd["xr2"][t * 128:(t + 1) * 128, :]),
            inc=f"Lx{b}", k=16)

        em.wait("scalar", f"Lf{b}", sdv["f"])
        em.op("scalar", lambda E, b=b: E.activation(SQ[:], LF[b][:], AF.Exp,
                                                    scale=-1.0))
        em.drain("scalar")
        sv = em.op("scalar", lambda E: E.activation(LFW[:], SQ[:], AF.Ln,
                                                    bias=1.0), inc="sv")
        em.wait("tensor", "sv", sv)
        em.wait("tensor", "vv", max(cau_vv, ps01_rel))
        for half in range(2):
            tv = em.op("tensor", lambda E, half=half: E.matmul(
                ps[half][:], P["ltincT"][:],
                LFW[:, half * 512:(half + 1) * 512], start=True, stop=True),
                inc="tv")
        em.wait("vector", "tv", tv)
        em.op("vector", lambda E: E.tensor_copy(NL[:, 0:512], ps[0][:]))
        vv = em.op("vector", lambda E: E.tensor_copy(NL[:, 512:1024], ps[1][:]),
                   inc="vv")
        em.wait("tensor", "vv", em.cnt["vv"])
        for j, W in enumerate((P["c1w"], P["c2w"], P["onesc"])):
            for half in range(2):
                tvl = em.op("tensor", lambda E, j=j, half=half, W=W: E.matmul(
                    ps[2 * j + half][0:1, 0:512], W[:, 0:1],
                    LFW[:, half * 512:(half + 1) * 512], start=True, stop=True),
                    inc="tv")
        em.wait("vector", "tv", tvl)
        for j, R in enumerate((R1, R2, LROW)):
            em.op("vector", lambda E, j=j, R=R: E.tensor_copy(
                R[0:1, 0:512], ps[2 * j][0:1, 0:512]))
            vv = em.op("vector", lambda E, j=j, R=R: E.tensor_copy(
                R[0:1, 512:1024], ps[2 * j + 1][0:1, 0:512]), inc="vv")
        em.wait("vector", f"Li{b}", sdv["i"])
        em.wait("tensor", "vv", vv)
        for half in range(2):
            tvb = em.op("tensor", lambda E, half=half: E.matmul(
                ps[2 + half][:], P["onesr"][0:1, :],
                R1[0:1, half * 512:(half + 1) * 512], start=True, stop=True),
                inc="tv")
        em.wait("vector", "tv", tvb)
        for half in range(2):
            em.op("vector", lambda E, half=half: E.tensor_sub(
                ARGQ[0:64, half * 512:(half + 1) * 512],
                ps[2 + half][0:64, :], NL[0:64, half * 512:(half + 1) * 512]))
            em.op("vector", lambda E, half=half, b=b: E.tensor_sub(
                A1S[0:64, half * 512:(half + 1) * 512],
                LI[b][0:64, half * 512:(half + 1) * 512], ps[2 + half][0:64, :]))
        vv = em.op("vector", lambda E: E.engine_nop(), inc="vv")
        em.wait("tensor", "vv", vv)
        for half in range(2):
            tvb = em.op("tensor", lambda E, half=half: E.matmul(
                ps[2 + half][:], P["onesr"][0:1, :],
                R2[0:1, half * 512:(half + 1) * 512], start=True, stop=True),
                inc="tv")
        em.wait("vector", "tv", tvb)
        for half in range(2):
            em.op("vector", lambda E, half=half: E.tensor_sub(
                ARGQ[64:128, half * 512:(half + 1) * 512],
                ps[2 + half][64:128, :], NL[64:128, half * 512:(half + 1) * 512]))
            em.op("vector", lambda E, half=half, b=b: E.tensor_sub(
                ARGK[:, half * 512:(half + 1) * 512],
                LI[b][:, half * 512:(half + 1) * 512], ps[2 + half][:, :]))
        vv = em.op("vector", lambda E: E.engine_nop(), inc="vv")
        em.op("vector", lambda E, b=b: E.tensor_add(ARGB[:], LI[b][:], NL[:]))
        em.drain("vector")
        em.wait("tensor", "vv", vv)
        for half in range(2):
            tvb = em.op("tensor", lambda E, half=half: E.matmul(
                ps[2 + half][:], P["onesr"][0:1, :],
                LROW[0:1, half * 512:(half + 1) * 512], start=True, stop=True),
                inc="tv")
        em.wait("vector", "tv", tvb)
        for half in range(2):
            em.op("vector", lambda E, half=half: E.tensor_sub(
                ARGB[:, half * 512:(half + 1) * 512],
                ARGB[:, half * 512:(half + 1) * 512], ps[2 + half][:, :]))
        em.drain("vector")
        em.op("vector", lambda E: E.tensor_add(ARGK[:], ARGK[:], NL[:]))
        em.op("vector", lambda E: E.tensor_add(A1S[0:64, :], A1S[0:64, :],
                                               NL[0:64, :]))
        vv = em.op("vector", lambda E: E.engine_nop(), inc="vv")
        em.wait("vector", f"Lq{b}", sdv["q"])
        em.op("vector", lambda E, b=b: E.tensor_mul(SQ[:], LQ[b][:], LQ[b][:]))
        em.drain("vector")
        em.op("vector", lambda E: E.tensor_reduce(
            MS[:, 0:16].rearrange("p (h o) -> p h o", o=1),
            SQ[:].rearrange("p (h d) -> p h d", d=64), AX.X, ALU.add))
        em.drain("vector")
        em.wait("vector", f"Lk{b}", sdv["k"])
        em.op("vector", lambda E, b=b: E.tensor_mul(SQ[:], LK[b][:], LK[b][:]))
        em.drain("vector")
        vv = em.op("vector", lambda E: E.tensor_reduce(
            MS[:, 16:32].rearrange("p (h o) -> p h o", o=1),
            SQ[:].rearrange("p (h d) -> p h d", d=64), AX.X, ALU.add), inc="vv")
        em.wait("scalar", "vv", vv)
        em.op("scalar", lambda E: E.activation(MS[:, 0:32], MS[:, 0:32], AF.Ln,
                                               bias=1e-6, scale=1.0 / 64))
        em.drain("scalar")
        sv = em.op("scalar", lambda E: E.activation(MS[:, 0:32], MS[:, 0:32],
                                                    AF.Exp, scale=-0.5), inc="sv")
        em.wait("scalar", "vv", em.cnt["vv"])
        em.op("scalar", lambda E: E.activation(ARGQ[:], ARGQ[:], AF.Exp))
        em.op("scalar", lambda E: E.activation(ARGK[:], ARGK[:], AF.Exp))
        em.op("scalar", lambda E: E.activation(A1S[0:64, :], A1S[0:64, :], AF.Exp))
        sv2 = em.op("scalar", lambda E: E.activation(ARGB[:], ARGB[:], AF.Exp),
                    inc="sv")
        em.wait("vector", "sv", sv)
        em.op("vector", lambda E: E.tensor_copy(
            RSE[:].rearrange("p (h d) -> p h d", d=64),
            MS[:, 0:16].unsqueeze(2).to_broadcast([128, 16, 64])))
        em.drain("vector")
        em.op("vector", lambda E, b=b: E.tensor_mul(QN[:], LQ[b][:], RSE[:]))
        em.drain("vector")
        em.op("vector", lambda E: E.tensor_mul(QN[:], QN[:], QNB[:]))
        em.op("vector", lambda E: E.tensor_copy(
            RSE[:].rearrange("p (h d) -> p h d", d=64),
            MS[:, 16:32].unsqueeze(2).to_broadcast([128, 16, 64])))
        em.drain("vector")
        em.op("vector", lambda E, b=b: E.tensor_mul(KN[:], LK[b][:], RSE[:]))
        em.drain("vector")
        em.op("vector", lambda E: E.tensor_mul(KN[:], KN[:], KNB[:]))
        em.drain("vector")
        em.wait("vector", "sv", sv2)
        em.op("vector", lambda E: E.tensor_mul(QA[:], QN[:], ARGQ[:]))
        em.op("vector", lambda E: E.tensor_mul(KT[:], KN[:], ARGK[:]))
        em.op("vector", lambda E: E.tensor_mul(K1[0:64, :], KN[0:64, :],
                                               A1S[0:64, :]))
        kb_vv = em.op("vector", lambda E: E.tensor_mul(KB[:], KN[:], ARGB[:]),
                      inc="vv")
        em.wait("scalar", "vv", kb_vv)
        sv3 = em.op("scalar", lambda E: E.activation(SQ[:], NL[:], AF.Exp,
                                                     scale=-1.0), inc="sv")
        em.drain("vector")
        em.wait("vector", "sv", sv3)
        vv = em.op("vector", lambda E: E.tensor_mul(QI[:], QN[:], SQ[:]), inc="vv")
        ld_guard["q"][b] = vv
        ld_guard["k"][b] = vv
        em.wait("tensor", f"Lv{b}", sdv["v"])
        ld_guard["i"][b] = em.cnt["vv"]
        ld_guard["f"][b] = em.cnt["vv"]
        elem_vv = em.cnt["vv"]

        em.wait("tensor", "vv", elem_vv)
        for h in range(16):
            f = lambda E, h=h: E.matmul(
                P["psR"][0:64, 100:116], LROW[0:1, h * 64:(h + 1) * 64],
                EYE16R[0:1, h * 16:(h + 1) * 16], start=(h == 0), stop=(h == 15))
            if h < 15:
                em.op("tensor", f)
            else:
                tv = em.op("tensor", f, inc="tv")
        em.wait("vector", "tv", tv)
        vv = em.op("vector", lambda E: E.tensor_copy(NLE[0:64, :],
                                                     P["psR"][0:64, 100:116]),
                   inc="vv")
        em.wait("scalar", "vv", vv)
        svp = em.op("scalar", lambda E: E.activation(PF16[0:64, :], NLE[0:64, :],
                                                     AF.Exp, scale=-1.0), inc="sv")

        em.wait("tensor", "vv", em.cnt["vv"])
        for nmT, src_, dstF in ((0, QA, QAF), (2, KT, KTF)):
            for rnd in range(4):
                pb = rnd % 2
                if psPTb_guard:
                    em.wait("tensor", "vv", psPTb_guard)
                if psT_guard:
                    em.wait("tensor", "vv", psT_guard)
                for hh in range(4):
                    h = rnd * 4 + hh
                    f = lambda E, h=h, hh=hh, src_=src_, pb=pb: E.transpose(
                        P["psT"][0:64, pb * 512 + hh * 128: pb * 512 + (hh + 1) * 128],
                        src_[:, h * 64:(h + 1) * 64], P["identb"][:])
                    if hh < 3:
                        em.op("tensor", f)
                    else:
                        tv = em.op("tensor", f, inc="tv")
                em.wait("vector", "tv", tv)
                psPTb_guard = em.op("vector", lambda E, rnd=rnd, dstF=dstF, pb=pb:
                    E.tensor_copy(dstF[0:64, rnd * 512:(rnd + 1) * 512],
                                  P["psT"][0:64, pb * 512:(pb + 1) * 512]), inc="vv")
        for nmT, src_, dstF in ((1, QI, QIF),):
            for rnd in range(4):
                pb = rnd % 2
                if psPT_guard:
                    em.wait("tensor", "vv", psPT_guard)
                for hh in range(4):
                    h = rnd * 4 + hh
                    f = lambda E, h=h, hh=hh, src_=src_, pb=pb: E.transpose(
                        ps[2 + pb][0:64, hh * 128:(hh + 1) * 128],
                        src_[:, h * 64:(h + 1) * 64], P["identf"][:])
                    if hh < 3:
                        em.op("tensor", f)
                    else:
                        tv = em.op("tensor", f, inc="tv")
                em.wait("vector", "tv", tv)
                psPT_guard = em.op("vector", lambda E, rnd=rnd, dstF=dstF, pb=pb:
                    E.tensor_copy(dstF[0:64, rnd * 512:(rnd + 1) * 512],
                                  ps[2 + pb][0:64, 0:512]), inc="vv")
        for rnd in range(2):
            pb = rnd % 2
            if psPTb_guard:
                em.wait("tensor", "vv", psPTb_guard)
            for hh in range(8):
                h = rnd * 8 + hh
                f = lambda E, h=h, hh=hh, pb=pb: E.transpose(
                    P["psT"][0:64, pb * 512 + hh * 64: pb * 512 + (hh + 1) * 64],
                    K1[0:64, h * 64:(h + 1) * 64], P["identb"][0:64, 0:64])
                if hh < 7:
                    em.op("tensor", f)
                else:
                    tv = em.op("tensor", f, inc="tv")
            em.wait("vector", "tv", tv)
            psPTb_guard = em.op("vector", lambda E, rnd=rnd, pb=pb:
                E.tensor_copy(K1F[0:64, rnd * 512:(rnd + 1) * 512],
                              P["psT"][0:64, pb * 512:(pb + 1) * 512]), inc="vv")
        tr_vv = em.cnt["vv"]

        em.wait("tensor", "vv", tr_vv)
        asb_guard = 0
        hv_guard = [0, 0]
        st_guard = 0
        hcol_vv = 0
        for h in range(16):
            pbh = h % 2
            bb = b
            if asb_guard:
                em.wait("tensor", "vv", asb_guard)
            em.op("tensor", lambda E, h=h: E.matmul(
                ps[4][0:64, 0:64],
                K1F[0:64, h * 64:(h + 1) * 64],
                QAF[0:64, h * 128: h * 128 + 64], start=True, stop=True))
            tva = em.op("tensor", lambda E, h=h: E.matmul(
                ps[4][:, 64:128],
                KTF[0:64, h * 128:(h + 1) * 128],
                QAF[0:64, h * 128 + 64: h * 128 + 128], start=True, stop=True),
                inc="tv")
            em.wait("vector", "tv", tva)
            asb_guard = em.op("vector", lambda E: E.tensor_mul(
                ASB[:], ps[4][:, 0:128], P["scanmask_sb"][:]), inc="vv")
            em.wait("tensor", "vv", asb_guard)
            if hv_guard[pbh]:
                em.wait("tensor", "vv", hv_guard[pbh])
            em.op("tensor", lambda E, h=h, pbh=pbh: E.matmul(
                ps[2 + pbh][:, 0:65], QIF[0:64, h * 128:(h + 1) * 128],
                CAU[0:64, h * 65:(h + 1) * 65], start=True, stop=False))
            em.op("tensor", lambda E, h=h, pbh=pbh, b=b: E.matmul(
                ps[2 + pbh][:, 0:64], ASB[:], LV[b][:, h * 64:(h + 1) * 64],
                start=False, stop=False))
            tvh = em.op("tensor", lambda E, pbh=pbh: E.matmul(
                ps[2 + pbh][:, 64:65], ASB[:], P["onescb"][:, 0:1],
                start=False, stop=True), inc="tv")
            if st_guard:
                em.wait("tensor", "vv", st_guard)
            em.op("tensor", lambda E, h=h, b=b: E.matmul(
                P["psR"][0:64, 0:64], KB[:, h * 64:(h + 1) * 64],
                LV[b][:, h * 64:(h + 1) * 64], start=True, stop=True))
            tvs = em.op("tensor", lambda E, h=h: E.matmul(
                P["psR"][0:64, 64:65], KB[:, h * 64:(h + 1) * 64],
                P["onescb"][:, 0:1], start=True, stop=True), inc="tv")
            em.wait("vector", "tv", tvh)
            em.op("vector", lambda E, pbh=pbh: E.tensor_scalar_max(
                DCOL[:, 0:1], ps[2 + pbh][:, 64:65], 1.0))
            em.drain("vector")
            em.op("vector", lambda E: E.reciprocal(DCOL[:, 1:2], DCOL[:, 0:1]))
            em.drain("vector")
            hv_guard[pbh] = em.op("vector", lambda E, h=h, pbh=pbh: E.tensor_scalar_mul(
                HT[:, h * 64:(h + 1) * 64],
                ps[2 + pbh][:, 0:64], DCOL[:, 1:2]), inc="vv")
            em.wait("vector", "tv", tvs)
            em.wait("vector", "sv", svp)
            st_guard = em.op("vector", lambda E, h=h: E.scalar_tensor_tensor(
                CAU[0:64, h * 65:(h + 1) * 65], CAU[0:64, h * 65:(h + 1) * 65],
                PF16[0:64, h:h + 1], P["psR"][0:64, 0:65], ALU.mult, ALU.add),
                inc="vv")
            em.drain("vector")
        cau_vv = em.cnt["vv"]
        ld_guard["v"][b] = cau_vv
        hcol_vv = em.cnt["vv"]

        em.wait("vector", f"Lg{b}", sdv["g"])
        vv = em.op("vector", lambda E, b=b: E.tensor_mul(HG[:], HT[:], LG[b][:]),
                   inc="vv")
        ld_guard["g"][b] = vv
        em.wait("tensor", "vv", vv)
        if psT_guard:
            em.wait("tensor", "vv", psT_guard)
        if psPTb_guard:
            em.wait("tensor", "vv", psPTb_guard)
        for c in range(8):
            f = lambda E, c=c: E.transpose(
                P["psT"][:, c * 128:(c + 1) * 128], HG[:, c * 128:(c + 1) * 128],
                P["identb"][:])
            if c < 7:
                em.op("tensor", f)
            else:
                tv = em.op("tensor", f, inc="tv")
        em.wait("vector", "tv", tv)
        vv = None
        for c in range(8):
            f = lambda E, c=c: E.tensor_copy(HGF[:, c * 128:(c + 1) * 128],
                                             P["psT"][:, c * 128:(c + 1) * 128])
            if c < 7:
                em.op("vector", f)
            else:
                vv = em.op("vector", f, inc="vv")
        psT_guard = vv
        em.wait("tensor", "vv", vv)
        for half in range(2):
            for fc in range(8):
                f = lambda E, fc=fc, half=half: E.matmul(
                    ps[half][:], HGF[:, fc * 128:(fc + 1) * 128],
                    OUTPC[:, fc * 1024 + half * 512: fc * 1024 + (half + 1) * 512],
                    start=(fc == 0), stop=(fc == 7))
                if fc < 7:
                    em.op("tensor", f)
                else:
                    tv = em.op("tensor", f, inc="tv")
        em.wait("vector", "tv", tv)
        em.wait("vector", f"Lx{b}", sdv["x"])
        if xo_guard2:
            em.wait("vector", "xoC0", xo_guard2)
        em.op("vector", lambda E, b=b: E.tensor_add(XO2[0][:, 0:512], ps[0][:],
                                                    XT2[b][:, 0:512]))
        vv = em.op("vector", lambda E, b=b: E.tensor_add(
            XO2[0][:, 512:1024], ps[1][:], XT2[b][:, 512:1024]), inc="vv")
        ld_guard["x"][b] = vv
        ps01_rel = vv
        em.wait("gpsimd", "vv", vv)
        xo_guard2 = em.op("gpsimd", lambda E, t=t: E.dma_start(
            out=dd["xr3"][t * 128:(t + 1) * 128, :], in_=XO2[0][:]),
            inc="xoC0", k=16)
    A.reset()


def emit_D(em, nc, P, din, dd, A):
    em.join()
    ps = P["ps"]
    RG = [list(range(NCORE))]

    SLOT2 = A.f32(32)
    GK = A.f32(32)
    ESTAR = A.f32(32)
    POSL = A.f32(32)
    G1 = A.f32(32)
    MYMAPT = A.f32(48)
    IDX32 = P["idx32"]
    mark = A.mark()

    XT3 = [A.f32(1024), A.f32(1024)]
    XNB = [A.bf16(1024)]
    XNFW = A.f32(1024)
    BCS = XNFW[:, 0:512]
    BN = A.f32(12)
    STAT = A.f32(8)
    DEAD = A.f32(4)
    XNFSg = A.bf16(1024)
    WGC = A.bf16(64)
    G8 = A.f32(8)
    OH = A.f32(8)
    T1 = A.f32(8)
    T2 = A.f32(8)
    CNTE = A.f32(8)
    BASE_B = A.f32(8)
    TOTR = A.f32(8)
    IOT8 = A.f32(8)
    SVALS = A.f32(32)
    GCOL = A.f32(4)
    CNTS_SB = A.f32(8)
    BASEG = A.f32(8)
    BASER = A.f32(8)
    BASEB128 = A.f32(8)
    LT8S = A.f32(8)
    OH8S = A.f32(1)
    SELS = A.f32(144)
    MAPT = A.f32(384)
    MYMAP = A.f32(128)
    ZR = A.f32(336)
    bufs = dict(bn=BN, stat=STAT, xnf=XNFW, dead=DEAD)

    wgv = em.op("gpsimd", lambda E: E.dma_start(out=WGC[:], in_=din["wg_c"]),
                inc="wD", k=16)
    sv0 = em.op("sync", lambda E: E.dma_start(out=IOT8[:], in_=dd["iota8"]),
                inc="bD", k=16)
    sv0 = em.op("sync", lambda E: E.dma_start(out=SVALS[:], in_=din["svals"]),
                inc="bD", k=16)
    sv0 = em.op("sync", lambda E: E.dma_start(out=LT8S[0:8, :], in_=dd["lt8sT"]),
                inc="bD", k=16)
    sv0 = em.op("sync", lambda E: E.dma_start(out=OH8S[0:8, :], in_=din["oh8"]),
                inc="bD", k=16)
    sv0 = em.op("sync", lambda E: E.dma_start(out=SELS[0:128, 0:48],
                in_=din["sel_m"][0:128, :]), inc="bD", k=16)
    sv0 = em.op("sync", lambda E: E.dma_start(out=SELS[0:128, 48:96],
                in_=din["sel_m"][128:256, :]), inc="bD", k=16)
    sv0 = em.op("sync", lambda E: E.dma_start(out=SELS[0:72, 96:144],
                in_=din["sel_m"][256:328, :]), inc="bD", k=16)
    em.wait("vector", "bD", sv0)
    em.wait("tensor", "wD", wgv)
    em.wait("tensor", "bD", sv0)
    em.op("vector", lambda E: E.memset(BASE_B[:], 0.0))
    em.op("vector", lambda E: E.memset(ZR[:], 0.0))
    em.drain("vector")
    zr_vv = em.op("vector", lambda E: E.nop() if False else E.memset(DEAD[:], 0.0),
                  inc="vv")
    em.wait("gpsimd", "vv", zr_vv)
    zrg = em.op("gpsimd", lambda E: E.dma_start(
        out=dd["smap_my"].rearrange("(p f) x -> p (f x)", p=128),
        in_=ZR[:, 0:329]), inc="zD", k=16)

    xt_guard = [0, 0]
    xnb_guard = [0]
    psT_guard = 0
    xnfs_guard = 0
    xn_wr = 0
    for t in range(NT):
        b = t % 2
        em.wait("sync", "vv", xt_guard[b])
        sd = em.op("sync", lambda E, t=t, b=b: E.dma_start(
            out=XT3[b][:], in_=dd["xr3"][t * 128:(t + 1) * 128, :]),
            inc=f"xD{b}", k=16)
        em.wait("vector", f"xD{b}", sd)
        em.wait("vector", "tv", xnb_guard[0])
        if xn_wr:
            em.wait("vector", "xnS", xn_wr)
        ln_vv = _ln_tile(em, P, A, bufs, XT3[b][:], XNB[0][:], l2gamma=False)
        xt_guard[b] = em.cnt["vv"]
        em.wait("gpsimd", "vv", ln_vv)
        xn_wr = em.op("gpsimd", lambda E, t=t: E.dma_start(
            out=dd["xn_my"][t * 128:(t + 1) * 128, :], in_=XNB[0][:]),
            inc="xnS", k=16)
        em.wait("tensor", "vv", ln_vv)
        if psT_guard:
            em.wait("tensor", "vv", psT_guard)
        for c in range(8):
            f = lambda E, c=c: E.transpose(
                P["psT"][:, c * 128:(c + 1) * 128],
                XNB[0][:, c * 128:(c + 1) * 128], P["identb"][:])
            if c < 7:
                em.op("tensor", f)
            else:
                tv = em.op("tensor", f, inc="tv")
        em.wait("vector", "tv", tv)
        em.wait("vector", "tv", xnfs_guard)
        for c in range(8):
            f = lambda E, c=c: E.tensor_copy(
                XNFSg[:, c * 128:(c + 1) * 128], P["psT"][:, c * 128:(c + 1) * 128])
            if c < 7:
                em.op("vector", f)
            else:
                vv = em.op("vector", f, inc="vv")
        psT_guard = vv
        xnb_guard[0] = tv
        em.wait("tensor", "vv", vv)
        for kc in range(8):
            f = lambda E, kc=kc: E.matmul(
                ps[4][:, 0:8], XNFSg[:, kc * 128:(kc + 1) * 128],
                WGC[:, kc * 8:(kc + 1) * 8], start=(kc == 0), stop=(kc == 7))
            if kc < 7:
                em.op("tensor", f)
            else:
                tv = em.op("tensor", f, inc="tv")
        xnfs_guard = tv
        em.wait("vector", "tv", tv)
        vv = em.op("vector", lambda E: E.tensor_copy(G8[:], ps[4][:, 0:8]),
                   inc="vv")
        em.drain("vector")
        em.op("vector", lambda E: E.reduce_max(GCOL[:, 0:1], G8[:], axis=AX.X))
        em.drain("vector")
        em.op("vector", lambda E: E.tensor_single_scalar(OH[:], G8[:],
                                                         GCOL[:, 0:1], ALU.is_equal))
        em.op("vector", lambda E: E.tensor_scalar_mul(GCOL[:, 1:2], GCOL[:, 0:1],
                                                      -1.0))
        em.drain("vector")
        em.op("vector", lambda E: E.tensor_mul(T1[:], OH[:], IOT8[:]))
        em.op("vector", lambda E: E.tensor_scalar(T2[:], OH[:], -8.0, 8.0,
                                                  ALU.mult, ALU.add))
        em.drain("vector")
        em.op("vector", lambda E: E.tensor_add(T1[:], T1[:], T2[:]))
        em.drain("vector")
        em.op("vector", lambda E, t=t: E.tensor_reduce(ESTAR[:, t:t + 1], T1[:],
                                                       AX.X, ALU.min))
        em.drain("vector")
        vv = em.op("vector", lambda E, t=t: E.tensor_single_scalar(
            OH[:], IOT8[:], ESTAR[:, t:t + 1], ALU.is_equal), inc="vv")
        em.wait("scalar", "vv", vv)
        sv = em.op("scalar", lambda E: E.activation(T2[:], G8[:], AF.Exp,
                   bias=GCOL[:, 1:2], accum_out=GCOL[:, 2:3]), inc="sv")
        em.wait("vector", "sv", sv)
        em.op("vector", lambda E, t=t: E.reciprocal(G1[:, t:t + 1], GCOL[:, 2:3]))
        em.drain("vector")
        em.wait("tensor", "vv", em.cnt["vv"])
        tv = em.op("tensor", lambda E: E.matmul(ps[5][:, 0:8], P["ltincT"][:],
                   OH[:], start=True, stop=True), inc="tv")
        tv2 = em.op("tensor", lambda E: E.matmul(P["psR"][0:1, 0:8],
                    P["onesc"][:, 0:1], OH[:], start=True, stop=True), inc="tv")
        em.wait("vector", "tv", tv2)
        em.op("vector", lambda E: E.tensor_copy(CNTE[:], ps[5][:, 0:8]))
        em.op("vector", lambda E: E.tensor_copy(TOTR[0:1, :], P["psR"][0:1, 0:8]))
        em.drain("vector")
        em.op("vector", lambda E: E.tensor_add(T2[:], CNTE[:], BASE_B[:]))
        em.drain("vector")
        em.op("vector", lambda E: E.tensor_mul(T2[:], T2[:], OH[:]))
        em.drain("vector")
        vv = em.op("vector", lambda E, t=t: E.tensor_reduce(POSL[:, t:t + 1], T2[:],
                                                            AX.X, ALU.add), inc="vv")
        em.wait("tensor", "vv", vv)
        tv = em.op("tensor", lambda E: E.matmul(P["psR"][:, 8:16], P["onesr"][0:1, :],
                   TOTR[0:1, :], start=True, stop=True), inc="tv")
        em.wait("vector", "tv", tv)
        vv = em.op("vector", lambda E: E.tensor_add(BASE_B[:], BASE_B[:],
                   P["psR"][:, 8:16]), inc="vv")
        em.drain("vector")
    gat_vv = em.cnt["vv"]

    em.wait("gpsimd", "vv", gat_vv)
    cg = em.op("gpsimd", lambda E: E.dma_start(out=dd["cnts_in"],
               in_=BASE_B[0:1, 0:8]), inc="cD", k=16)
    em.wait("gpsimd", "cD", cg)
    cg = em.op("gpsimd", lambda E: E.collective_compute(
        "AllGather", ALU.bypass, RG, ins=[dd["cnts_in"]], outs=[dd["cnts_out"]]),
        inc="gv")
    em.wait("gpsimd", "gv", cg)
    cg = em.op("gpsimd", lambda E: E.dma_start(out=CNTS_SB[0:8, 0:8],
               in_=dd["cnts_out"]), inc="cD", k=16)
    em.wait("gpsimd", "xnS", em.cnt["xnS"])
    xag = em.op("gpsimd", lambda E: E.collective_compute(
        "AllGather", ALU.bypass, RG, ins=[dd["xn_my"]], outs=[dd["xn_all"]]),
        inc="gv")
    em.wait("tensor", "cD", cg)
    tv = em.op("tensor", lambda E: E.matmul(ps[4][0:8, 0:8], LT8S[0:8, 0:8],
               CNTS_SB[0:8, 0:8], start=True, stop=True), inc="tv")
    em.wait("vector", "tv", tv)
    vv = em.op("vector", lambda E: E.tensor_copy(BASEG[0:8, :], ps[4][0:8, 0:8]),
               inc="vv")
    em.wait("tensor", "vv", vv)
    tv = em.op("tensor", lambda E: E.matmul(ps[4][0:1, 0:8], OH8S[0:8, 0:1],
               BASEG[0:8, 0:8], start=True, stop=True), inc="tv")
    em.wait("vector", "tv", tv)
    vv = em.op("vector", lambda E: E.tensor_copy(BASER[0:1, :], ps[4][0:1, 0:8]),
               inc="vv")
    em.wait("tensor", "vv", vv)
    tv = em.op("tensor", lambda E: E.matmul(ps[4][:, 0:8], P["onesr"][0:1, :],
               BASER[0:1, 0:8], start=True, stop=True), inc="tv")
    em.wait("vector", "tv", tv)
    vv = em.op("vector", lambda E: E.tensor_copy(BASEB128[:], ps[4][:, 0:8]),
               inc="vv")

    em.wait("gpsimd", "zD", zrg)
    sc_wr = 0
    for t in range(NT):
        em.op("vector", lambda E, t=t: E.tensor_single_scalar(
            OH[:], IOT8[:], ESTAR[:, t:t + 1], ALU.is_equal))
        em.drain("vector")
        em.op("vector", lambda E: E.tensor_mul(T2[:], OH[:], BASEB128[:]))
        em.drain("vector")
        em.op("vector", lambda E: E.tensor_reduce(GCOL[:, 0:1], T2[:], AX.X,
                                                  ALU.add))
        em.drain("vector")
        em.op("vector", lambda E, t=t: E.tensor_add(GCOL[:, 1:2], GCOL[:, 0:1],
                                                    POSL[:, t:t + 1]))
        em.drain("vector")
        em.op("vector", lambda E: E.tensor_scalar_add(GCOL[:, 1:2], GCOL[:, 1:2],
                                                      -1.0))
        em.drain("vector")
        em.op("vector", lambda E: E.tensor_single_scalar(GCOL[:, 2:3],
              GCOL[:, 1:2], float(CAP), ALU.is_lt))
        em.drain("vector")
        em.op("vector", lambda E, t=t: E.tensor_mul(GK[:, t:t + 1], GCOL[:, 2:3],
                                                    G1[:, t:t + 1]))
        em.op("vector", lambda E, t=t: E.tensor_scalar(
            SLOT2[:, t:t + 1], ESTAR[:, t:t + 1], float(ESTRIDE), GCOL[:, 1:2],
            ALU.mult, ALU.add))
        em.drain("vector")
        em.op("vector", lambda E, t=t: E.tensor_mul(SLOT2[:, t:t + 1],
              SLOT2[:, t:t + 1], GCOL[:, 2:3]))
        em.op("vector", lambda E: E.tensor_scalar(GCOL[:, 3:4], GCOL[:, 2:3],
              -float(NSLOT), float(NSLOT), ALU.mult, ALU.add))
        em.drain("vector")
        em.op("vector", lambda E, t=t: E.tensor_add(GCOL[:, 3:4],
              SLOT2[:, t:t + 1], GCOL[:, 3:4]))
        em.drain("vector")
        if sc_wr:
            em.wait("vector", "scD", sc_wr)
        vv = em.op("vector", lambda E, t=t: E.tensor_copy(P["idx32"][:, 0:1],
                   GCOL[:, 3:4]), inc="vv")
        em.wait("gpsimd", "vv", vv)
        sc_wr = em.op("gpsimd", lambda E, t=t: E.indirect_dma_start(
            out=dd["smap_my"], out_offset=IndirectOffsetOnAxis(P["idx32"][:, 0:1], 0),
            in_=SVALS[:, t:t + 1], in_offset=None), inc="scD", k=16)
    em.wait("gpsimd", "scD", em.cnt["scD"])
    arv = em.op("gpsimd", lambda E: E.collective_compute(
        "AllReduce", ALU.add, RG, ins=[dd["smap_my"]], outs=[dd["smap_red"]]),
        inc="gv")
    em.wait("gpsimd", "gv", arv)
    mg = em.op("gpsimd", lambda E: E.dma_start(out=MAPT[:, 0:128],
               in_=dd["smap_red"].rearrange("(f p) x -> f (p x)", p=128)[0:128, :]),
               inc="cD", k=16)
    mg = em.op("gpsimd", lambda E: E.dma_start(out=MAPT[:, 128:256],
               in_=dd["smap_red"].rearrange("(f p) x -> f (p x)", p=128)[128:256, :]),
               inc="cD", k=16)
    mg = em.op("gpsimd", lambda E: E.dma_start(out=MAPT[0:72, 256:384],
               in_=dd["smap_red"].rearrange("(f p) x -> f (p x)", p=128)[256:328, :]),
               inc="cD", k=16)
    em.wait("tensor", "cD", mg)
    em.op("tensor", lambda E: E.matmul(ps[4][0:48, 0:128], SELS[0:128, 0:48],
          MAPT[0:128, 0:128], start=True, stop=False))
    em.op("tensor", lambda E: E.matmul(ps[4][0:48, 0:128], SELS[0:128, 48:96],
          MAPT[0:128, 128:256], start=False, stop=False))
    tv = em.op("tensor", lambda E: E.matmul(ps[4][0:48, 0:128], SELS[0:72, 96:144],
               MAPT[0:72, 256:384], start=False, stop=True), inc="tv")
    em.wait("vector", "tv", tv)
    vv = em.op("vector", lambda E: E.tensor_copy(MYMAP[0:48, :], ps[4][0:48, 0:128]),
               inc="vv")
    em.wait("tensor", "vv", vv)
    tv = em.op("tensor", lambda E: E.transpose(P["psR"][:, 0:48],
               MYMAP[0:48, 0:128], P["identf"][0:48, 0:48]), inc="tv")
    em.wait("vector", "tv", tv)
    mym_vv = em.op("vector", lambda E: E.tensor_copy(MYMAPT[:, 0:48],
                   P["psR"][:, 0:48]), inc="vv")

    em.join()
    A.reset(mark)
    MW1 = A.bf16(32768)
    MW2 = A.bf16(32768)
    HFM = A.bf16(16384)
    DGA = A.bf16(4096)
    DFM = A.bf16(4096)
    MB1S = A.f32(32)
    MB2B = A.bf16(1024)
    EOB = [A.f32(1024)]
    IDXD = A.f32(8)
    w1v = em.op("gpsimd", lambda E: E.dma_start(out=MW1[:], in_=din["mw1_c"]),
                inc="wD2", k=16)
    w2v = em.op("gpsimd", lambda E: E.dma_start(out=MW2[:], in_=din["mw2_c"]),
                inc="wD2", k=16)
    b1v = em.op("sync", lambda E: E.dma_start(out=MB1S[:], in_=din["mb1_t"]),
                inc="bD2", k=16)
    b2v = em.op("gpsimd", lambda E: E.dma_start(
        out=MB2B[:], in_=din["mb2_r"].to_broadcast([128, D])), inc="wD2", k=16)
    em.wait("tensor", "wD2", em.cnt["wD2"])
    em.wait("scalar", "bD2", b1v)
    em.wait("vector", "wD2", em.cnt["wD2"])
    em.op("vector", lambda E: E.memset(DGA[:], 0.0))
    em.drain("vector")
    em.wait("gpsimd", "gv", xag)
    nslab = ESTRIDE // 512 + 1
    dga_guard = 0
    gD_wr = 0
    psH_guard = [0, 0, 0, 0]
    psO_guard = [0, 0]
    hfm_guard = 0
    dfm_guard = 0
    eo_guard = 0
    for s in range(nslab):
        nsub = 4 if s < 10 else 1
        for sub in range(nsub):
            tcol = s * 4 + sub
            em.wait("vector", "vv", mym_vv)
            if dga_guard:
                em.wait("vector", "tv", dga_guard)
            em.op("vector", lambda E, tcol=tcol: E.tensor_scalar(
                IDXD[:, 0:1], MYMAPT[:, tcol:tcol + 1], -1.0, 0.0,
                ALU.add, ALU.max))
            em.drain("vector")
            if gD_wr:
                em.wait("vector", "gD", gD_wr)
            vv = em.op("vector", lambda E: E.tensor_copy(P["idx32"][:, 1:2],
                       IDXD[:, 0:1]), inc="vv")
            em.drain("vector")
            em.wait("gpsimd", "vv", vv)
            gD_wr = em.op("gpsimd", lambda E, sub=sub: E.indirect_dma_start(
                out=DGA[:, sub * 1024:(sub + 1) * 1024], out_offset=None,
                in_=dd["xn_all"],
                in_offset=IndirectOffsetOnAxis(P["idx32"][:, 1:2], 0)),
                inc="gD", k=16)
        em.wait("tensor", "gD", gD_wr)
        for sub in range(nsub):
            if psT_guard:
                em.wait("tensor", "vv", psT_guard)
            for c in range(8):
                f = lambda E, c=c, sub=sub: E.transpose(
                    P["psT"][:, c * 128:(c + 1) * 128],
                    DGA[:, sub * 1024 + c * 128: sub * 1024 + (c + 1) * 128],
                    P["identb"][:])
                if c < 7:
                    em.op("tensor", f)
                else:
                    tv = em.op("tensor", f, inc="tv")
            em.wait("vector", "tv", tv)
            em.wait("vector", "tv", hfm_guard)
            for c in range(8):
                f = lambda E, c=c, sub=sub: E.tensor_copy(
                    DFM[:, c * 512 + sub * 128: c * 512 + (sub + 1) * 128],
                    P["psT"][:, c * 128:(c + 1) * 128])
                if c < 7:
                    em.op("vector", f)
                else:
                    vv = em.op("vector", f, inc="vv")
            psT_guard = vv
        dga_guard = em.cnt["tv"]
        dfm_vv = em.cnt["vv"]
        em.wait("tensor", "vv", dfm_vv)
        em.wait("scalar", "tv", hfm_guard)
        for ic in range(32):
            pb = ic % 4
            if psH_guard[pb]:
                em.wait("tensor", "sv", psH_guard[pb])
            for kc in range(8):
                f = lambda E, kc=kc, ic=ic, pb=pb: E.matmul(
                    ps[pb][:], MW1[:, kc * 4096 + ic * 128: kc * 4096 + (ic + 1) * 128],
                    DFM[:, kc * 512:(kc + 1) * 512], start=(kc == 0), stop=(kc == 7))
                if kc < 7:
                    em.op("tensor", f)
                else:
                    tv = em.op("tensor", f, inc="tv")
            em.wait("scalar", "tv", tv)
            psH_guard[pb] = em.op("scalar", lambda E, ic=ic, pb=pb: E.activation(
                HFM[:, ic * 512:(ic + 1) * 512], ps[pb][:], AF.Relu,
                bias=MB1S[:, ic:ic + 1]), inc="sv")
        h_sv = em.cnt["sv"]
        dfm_guard = em.cnt["tv"]
        em.wait("tensor", "sv", h_sv)
        for sub in range(nsub):
            tcol = s * 4 + sub
            for half in range(2):
                if psO_guard[half]:
                    em.wait("tensor", "vv", psO_guard[half])
                for ic in range(32):
                    f = lambda E, ic=ic, sub=sub, half=half: E.matmul(
                        ps[4 + half][:],
                        HFM[:, ic * 512 + sub * 128: ic * 512 + (sub + 1) * 128],
                        MW2[:, ic * 1024 + half * 512: ic * 1024 + (half + 1) * 512],
                        start=(ic == 0), stop=(ic == 31))
                    if ic < 31:
                        em.op("tensor", f)
                    else:
                        tv = em.op("tensor", f, inc="tv")
                em.wait("vector", "tv", tv)
                if eo_guard:
                    em.wait("vector", "eoS", eo_guard)
                psO_guard[half] = em.op("vector", lambda E, half=half: E.tensor_add(
                    EOB[0][:, half * 512:(half + 1) * 512], ps[4 + half][:],
                    MB2B[:, half * 512:(half + 1) * 512]), inc="vv")
            em.wait("gpsimd", "vv", em.cnt["vv"])
            eo_guard = em.op("gpsimd", lambda E, tcol=tcol: E.dma_start(
                out=dd["eo_my"][tcol * 128:(tcol + 1) * 128, :], in_=EOB[0][:]),
                inc="eoS", k=16)
        hfm_guard = em.cnt["tv"]
    em.wait("gpsimd", "eoS", em.cnt["eoS"])
    eag = em.op("gpsimd", lambda E: E.collective_compute(
        "AllGather", ALU.bypass, RG, ins=[dd["eo_my"]], outs=[dd["eo_all"]]),
        inc="gv")

    em.join()
    A.reset(mark)
    FXT = [A.f32(1024), A.f32(1024)]
    FG = [A.bf16(1024), A.bf16(1024)]
    YO = [A.f32(1024), A.f32(1024)]
    em.wait("vector", "gv", eag)
    em.op("vector", lambda E: E.memset(FG[0][:], 0.0))
    vv0 = em.op("vector", lambda E: E.memset(FG[1][:], 0.0), inc="vv")
    fxt_guard = [0, 0]
    fg_guard = [0, 0]
    yo_guard = [0, 0]
    for t in range(NT):
        b = t % 2
        em.wait("sync", "vv", fxt_guard[b])
        sd = em.op("sync", lambda E, t=t, b=b: E.dma_start(
            out=FXT[b][:], in_=dd["xr3"][t * 128:(t + 1) * 128, :]),
            inc=f"fD{b}", k=16)
        em.wait("vector", "vv", vv0)
        if fg_guard[b]:
            em.wait("vector", f"gF{b}", fg_guard[b])
        vv = em.op("vector", lambda E, t=t, b=b: E.tensor_copy(
            P["idx32"][:, 2 + b:3 + b], SLOT2[:, t:t + 1]), inc="vv")
        em.wait("gpsimd", "vv", vv)
        fg_guard[b] = em.op("gpsimd", lambda E, b=b: E.indirect_dma_start(
            out=FG[b][:], out_offset=None,
            in_=dd["eo_all"],
            in_offset=IndirectOffsetOnAxis(P["idx32"][:, 2 + b:3 + b], 0)),
            inc=f"gF{b}", k=16)
        em.wait("vector", f"gF{b}", fg_guard[b])
        em.wait("vector", f"fD{b}", sd)
        if yo_guard[b]:
            em.wait("vector", f"yD{b}", yo_guard[b])
        vv = em.op("vector", lambda E, t=t, b=b: E.scalar_tensor_tensor(
            YO[b][:], FG[b][:], GK[:, t:t + 1], FXT[b][:], ALU.mult, ALU.add),
            inc="vv")
        fxt_guard[b] = vv
        em.wait("gpsimd", "vv", vv)
        yo_guard[b] = em.op("gpsimd", lambda E, t=t, b=b: E.dma_start(
            out=dd["y"][t * 128:(t + 1) * 128, :], in_=YO[b][:]),
            inc=f"yD{b}", k=16)
    A.reset()


_DEBUG_STREAM = None


def kernel(**inputs):
    maps = [host_prep(inputs, c) for c in range(NCORE)]
    nc = build_program(debug_stream=_DEBUG_STREAM)
    res = run_bass_kernel_spmd(nc, maps, core_ids=list(range(NCORE)))
    out = np.stack([np.asarray(res.results[c]["y"], dtype=np.float32)
                    for c in range(NCORE)], axis=0)
    return out


# version 07
# speedup vs baseline: 1.0250x; 1.0250x over previous
import numpy as np
import ml_dtypes
from contextlib import ExitStack

import concourse.bass as bass
import concourse.mybir as mybir
from concourse.bass import IndirectOffsetOnAxis
from concourse.bass_utils import run_bass_kernel_spmd

F32 = mybir.dt.float32
BF16 = mybir.dt.bfloat16
I32 = mybir.dt.int32
AF = mybir.ActivationFunctionType
ALU = mybir.AluOpType
AX = mybir.AxisListType
BF = ml_dtypes.bfloat16

NCORE = 8
SEQ = 4096
D = 1024
NT = SEQ // 128
NSLAB = SEQ // 512
SCALE_ATTN = float(128.0 ** -0.5)
ESTRIDE = 5248
CAP = 5120
NSLOT = 8 * ESTRIDE
MAPF = NSLOT // 128


def _chunks(w):
    K, F = w.shape
    nch = K // 128
    return np.ascontiguousarray(
        w.reshape(nch, 128, F).transpose(1, 0, 2).reshape(128, nch * F))


def _row(v):
    return np.ascontiguousarray(v.reshape(1, -1).astype(np.float32))


def host_prep(inp, core):
    f32 = lambda a: np.ascontiguousarray(np.asarray(a, dtype=np.float32))
    g = {}
    g["x"] = f32(inp["L0_attn_x"][core])
    g["gamma_r"] = _row(f32(inp["L0_attn_gamma"]))
    wqkvT = f32(inp["L0_attn_w_qkv"]).T
    w5 = wqkvT.reshape(8, 128, 3, 8, 128)
    g["wqkv_h"] = np.ascontiguousarray(
        w5.transpose(3, 1, 2, 0, 4).reshape(8 * 128, 3072))
    g["wout_c"] = _chunks(f32(inp["L0_attn_w_out"]).T)
    g["w1_c"] = _chunks(f32(inp["L0_ffn_W1"]))
    g["w2_c"] = _chunks(f32(inp["L0_ffn_W2"]))
    g["b1_t"] = np.ascontiguousarray(
        f32(inp["L0_ffn_b1"]).reshape(32, 128).T)
    g["b2_r"] = _row(f32(inp["L0_ffn_b2"]))
    g["inproj_c"] = _chunks(f32(inp["L1_attn_in_proj_w"]).T)
    for nm, key in (("qp", "L1_attn_q_proj_w"), ("kp", "L1_attn_k_proj_w"),
                    ("vp", "L1_attn_v_proj_w"), ("ip", "L1_attn_i_gate_w"),
                    ("fp", "L1_attn_f_gate_w"), ("op", "L1_attn_o_gate_w"),
                    ("outp", "L1_attn_out_proj_w")):
        g[nm + "_c"] = _chunks(f32(inp[key]).T)
    g["ib_r"] = _row(f32(inp["L1_attn_i_gate_b"]))
    g["fb_r"] = _row(f32(inp["L1_attn_f_gate_b"]))
    g["ob_r"] = _row(f32(inp["L1_attn_o_gate_b"]))
    g["qnw_r"] = _row(np.tile(f32(inp["L1_attn_q_norm_w"]), 16))
    g["knw_r"] = _row(np.tile(f32(inp["L1_attn_k_norm_w"]), 16))
    g["wg_c"] = _chunks(f32(inp["L1_ffn_wg"]))
    g["mw1_c"] = _chunks(f32(inp["L1_ffn_w1"][core]))
    g["mb1_t"] = np.ascontiguousarray(
        f32(inp["L1_ffn_b1"][core]).reshape(32, 128).T)
    g["mw2_c"] = _chunks(f32(inp["L1_ffn_w2"][core]))
    g["mb2_r"] = _row(f32(inp["L1_ffn_b2"][core]))
    g["svals"] = np.ascontiguousarray(
        (np.arange(SEQ, dtype=np.float32) + core * SEQ + 1.0)
        .reshape(NT, 128).T)
    oh8 = np.zeros((8, 1), np.float32); oh8[core, 0] = 1.0
    g["oh8"] = oh8
    sel = np.zeros((MAPF, 48), np.float32)
    for j in range(ESTRIDE // 128):
        sel[(core * ESTRIDE) // 128 + j, j] = 1.0
    g["sel_m"] = sel
    return g


INPUT_SPECS = [
    ("x", [SEQ, D]), ("gamma_r", [1, D]), ("wqkv_h", [8 * 128, 3072]),
    ("wout_c", [128, 8 * 1024]), ("w1_c", [128, 8 * 4096]),
    ("w2_c", [128, 32 * 1024]), ("b1_t", [128, 32]), ("b2_r", [1, D]),
    ("inproj_c", [128, 8 * 2048]), ("qp_c", [128, 8 * 1024]),
    ("kp_c", [128, 8 * 1024]), ("vp_c", [128, 8 * 1024]),
    ("ip_c", [128, 8 * 1024]), ("fp_c", [128, 8 * 1024]),
    ("op_c", [128, 8 * 1024]), ("outp_c", [128, 8 * 1024]),
    ("ib_r", [1, D]), ("fb_r", [1, D]), ("ob_r", [1, D]),
    ("qnw_r", [1, D]), ("knw_r", [1, D]), ("wg_c", [128, 64]),
    ("mw1_c", [128, 8 * 4096]), ("mb1_t", [128, 32]),
    ("mw2_c", [128, 32 * 1024]), ("mb2_r", [1, D]),
    ("svals", [128, NT]), ("oh8", [8, 1]), ("sel_m", [MAPF, 48]),
]


class Em:
    ENGS = ("sync", "gpsimd", "vector", "scalar", "tensor")

    def __init__(self, nc, ctx):
        self.nc = nc
        self.ctx = ctx
        self.q = {e: [] for e in self.ENGS}
        self.sem = {}
        self.cnt = {}
        self.dma_sems = set()
        for name in ("tv", "vv", "sv", "gv", "yv"):
            self.S(name)

    def S(self, name):
        if name not in self.sem:
            self.sem[name] = self.ctx.enter_context(self.nc.semaphore(name))
            self.cnt[name] = 0
        return self.sem[name]

    def emit(self, eng, fn):
        self.q[eng].append(fn)

    def wait(self, eng, sem, val):
        if val <= 0:
            return
        h = self.S(sem)
        self.emit(eng, lambda E: E.wait_ge(h, val))

    def bump(self, sem, k=1):
        self.S(sem)
        self.cnt[sem] += k
        return self.cnt[sem]

    def op(self, eng, fn, inc=None, k=1):
        if inc is None:
            self.emit(eng, lambda E: fn(E))
            return None
        h = self.S(inc)
        if k == 16:
            self.dma_sems.add(inc)
        v = self.bump(inc, k)
        self.emit(eng, lambda E: fn(E).then_inc(h, k))
        return v

    def drain(self, eng):
        self.emit(eng, lambda E: E.drain())

    def snap(self):
        return dict(self.cnt)

    def join(self, tag=None):
        sems = {"tensor": "jt", "vector": "jv", "scalar": "js",
                "gpsimd": "jg", "sync": "jy"}
        for name in sorted(self.dma_sems):
            self.wait("gpsimd", name, self.cnt[name])
        targets = {}
        for e in self.ENGS:
            targets[e] = self.op(e, lambda E: E.nop(), inc=sems[e])
        for e in self.ENGS:
            for o in self.ENGS:
                if o != e:
                    self.wait(e, sems[o], targets[o])


def build_program(debug_stream=None):
    nc = bass.Bass("TRN2", target_bir_lowering=False, num_devices=NCORE)
    ctx = ExitStack()

    dram_in = {}
    for name, shape in INPUT_SPECS:
        dram_in[name] = nc.dram_tensor(name, shape, F32, kind="ExternalInput").ap()
    y = nc.dram_tensor("y", [SEQ, D], F32, kind="ExternalOutput").ap()

    xr1 = nc.dram_tensor("xr1", [SEQ, D], F32, kind="Internal").ap()
    xr2 = nc.dram_tensor("xr2", [SEQ, D], F32, kind="Internal").ap()
    xr3 = nc.dram_tensor("xr3", [SEQ, D], F32, kind="Internal").ap()
    sp_q = nc.dram_tensor("sp_q", [SEQ, D], BF16, kind="Internal").ap()
    sp_k = nc.dram_tensor("sp_k", [SEQ, D], BF16, kind="Internal").ap()
    sp_v = nc.dram_tensor("sp_v", [SEQ, D], BF16, kind="Internal").ap()
    sp_i = nc.dram_tensor("sp_i", [SEQ, D], F32, kind="Internal").ap()
    sp_f = nc.dram_tensor("sp_f", [SEQ, D], F32, kind="Internal").ap()
    sp_g = nc.dram_tensor("sp_g", [SEQ, D], BF16, kind="Internal").ap()
    xn_my = nc.dram_tensor("xn_my", [SEQ, D], BF16, kind="Internal").ap()
    xn_all = nc.dram_tensor("xn_all", [NCORE * SEQ, D], BF16, kind="Internal").ap()
    cnts_in = nc.dram_tensor("cnts_in", [1, 8], F32, kind="Internal").ap()
    cnts_out = nc.dram_tensor("cnts_out", [8, 8], F32, kind="Internal").ap()
    smap_my = nc.dram_tensor("smap_my", [NSLOT + 128, 1], F32, kind="Internal").ap()
    smap_red = nc.dram_tensor("smap_red", [NSLOT + 128, 1], F32, kind="Internal").ap()
    eo_my = nc.dram_tensor("eo_my", [ESTRIDE, D], BF16, kind="Internal").ap()
    eo_all = nc.dram_tensor("eo_all", [NSLOT, D], BF16, kind="Internal").ap()

    ident_bf = nc.inline_tensor(np.eye(128, dtype=BF), name="ident_bf").ap()
    ident_f = nc.inline_tensor(np.eye(128, dtype=np.float32), name="ident_f").ap()
    ltincT = nc.inline_tensor(
        np.triu(np.ones((128, 128), np.float32)), name="ltincT").ap()
    lt8sT = nc.inline_tensor(
        np.triu(np.ones((8, 8), np.float32), 1), name="lt8sT").ap()
    ones_r = nc.inline_tensor(np.ones((1, 128), np.float32), name="ones_r").ap()
    ones_c = nc.inline_tensor(np.ones((128, 1), np.float32), name="ones_c").ap()
    ones_cb = nc.inline_tensor(np.ones((128, 1), dtype=BF), name="ones_cb").ap()
    ones_rb = nc.inline_tensor(np.ones((1, 128), dtype=BF), name="ones_rb").ap()
    iota8 = nc.inline_tensor(
        np.tile(np.arange(8, dtype=np.float32)[None, :], (128, 1)),
        name="iota8").ap()
    masks = []
    for j in range(4):
        m = (np.arange(512)[None, :] >= (128 * j + np.arange(128))[:, None])
        masks.append(nc.inline_tensor(m.astype(BF), name=f"mask{j}").ap())
    scanmask = nc.inline_tensor(
        np.tril(np.ones((128, 128), np.float32)).T.astype(np.float32),
        name="scanmask").ap()
    eye16r = nc.inline_tensor(
        np.eye(16, dtype=np.float32).reshape(1, 256), name="eye16r").ap()
    _c1 = np.full((128, 1), 0.0, np.float32); _c1[0] = 1.0; _c1[1:64] = 0.5
    _c2 = np.full((128, 1), 0.5, np.float32); _c2[0:64] = 1.0
    c1w = nc.inline_tensor(_c1, name="c1w").ap()
    c2w = nc.inline_tensor(_c2, name="c2w").ap()

    em = Em(nc, ctx)
    P = {}

    sb = lambda name, shape, dt=F32: ctx.enter_context(nc.sbuf_tensor(name, shape, dt))
    P["identb"] = sb("identb", [128, 128], BF16)
    P["identf"] = sb("identf", [128, 128], F32)
    P["ltincT"] = sb("s_ltincT", [128, 128], F32)
    P["onesr"] = sb("onesr", [1, 128], F32)
    P["onesc"] = sb("onesc", [128, 1], F32)
    P["onescb"] = sb("onescb", [128, 1], BF16)
    P["onesrb"] = sb("onesrb", [1, 128], BF16)
    P["gammab"] = sb("gammab", [128, D], F32)
    P["masks"] = sb("masksb", [128, 4 * 512], BF16)
    P["scanmask_sb"] = sb("scanmaskb", [128, 128], F32)
    P["c1w"] = sb("c1w_sb", [128, 1], F32)
    P["c2w"] = sb("c2w_sb", [128, 1], F32)
    P["consts"] = sb("constsb", [128, 8], F32)
    P["idx32"] = sb("idx32", [128, 4], I32)
    em_boot(em, nc, P, dram_in, dict(ident_bf=ident_bf, ident_f=ident_f,
            ltincT=ltincT, ones_r=ones_r, ones_c=ones_c, ones_cb=ones_cb,
            ones_rb=ones_rb, masks=masks, scanmask=scanmask, c1w=c1w, c2w=c2w))

    cvals = [1e-5, 1e-6, 1.0]
    P["cvals"] = cvals
    for i, v in enumerate(cvals):
        nc.const_aps.register(F32, v, P["consts"][:, i:i + 1])

    st_total = emit_all(em, nc, ctx, P, dram_in, dict(
        y=y, xr1=xr1, xr2=xr2, xr3=xr3, sp_q=sp_q, sp_k=sp_k, sp_v=sp_v,
        sp_i=sp_i, sp_f=sp_f, sp_g=sp_g, xn_my=xn_my, xn_all=xn_all,
        cnts_in=cnts_in, cnts_out=cnts_out, smap_my=smap_my, smap_red=smap_red,
        eo_my=eo_my, eo_all=eo_all,
        lt8sT=lt8sT, iota8=iota8, eye16r=eye16r),
        debug_stream=debug_stream)

    blk = ctx.enter_context(nc.Block())
    for eng_name, deco in (("sync", blk.sync), ("gpsimd", blk.gpsimd),
                           ("vector", blk.vector), ("scalar", blk.scalar),
                           ("tensor", blk.tensor)):
        thunks = em.q[eng_name]
        def body(E, thunks=thunks):
            for f in thunks:
                f(E)
        deco(body)
    ctx.close()
    return nc


class Arena:

    def __init__(self, nc, ctx, words):
        self.t = ctx.enter_context(nc.sbuf_tensor("ARENA", [128, words], F32))
        self.words = words
        self.off = 0

    def mark(self):
        return self.off

    def reset(self, off=0):
        self.off = off

    def al(self, nwords):
        a = self.off
        self.off += nwords
        assert self.off <= self.words, f"arena overflow {self.off}>{self.words}"
        return a

    def f32(self, n):
        a = self.al(n)
        return self.t[:, a:a + n]

    def bf16(self, n):
        a = self.al((n + 1) // 2)
        return self.t[:, a:a + (n + 1) // 2].bitcast(BF16)


def em_boot(em, nc, P, din, consts):
    def l(dst, src):
        em.op("sync", lambda E: E.dma_start(out=dst, in_=src), inc="bt", k=16)
    l(P["identb"][:], consts["ident_bf"])
    l(P["identf"][:], consts["ident_f"])
    l(P["ltincT"][:], consts["ltincT"])
    l(P["onesr"][:], consts["ones_r"])
    l(P["onesc"][:], consts["ones_c"])
    l(P["onescb"][:], consts["ones_cb"])
    l(P["onesrb"][:], consts["ones_rb"])
    for j in range(4):
        l(P["masks"][:, j * 512:(j + 1) * 512], consts["masks"][j])
    l(P["gammab"][:], din["gamma_r"].to_broadcast([128, D]))
    cinit = nc.inline_tensor(
        np.tile(np.array([[1e-5, 1e-6, 1.0, 0, 0, 0, 0, 0]], np.float32), (128, 1)),
        name="cinit").ap()
    l(P["consts"][:], cinit)
    l(P["scanmask_sb"][:], consts["scanmask"])
    l(P["c1w"][:], consts["c1w"])
    l(P["c2w"][:], consts["c2w"])
    P["boot_sd"] = em.cnt["bt"]


def emit_all(em, nc, ctx, P, din, dd, debug_stream=None):
    A = Arena(nc, ctx, nc.sbuf_bytes_remaining // 4 - 64)
    P["ps"] = [ctx.enter_context(nc.psum_tensor(f"ps{i}", [128, 512], F32))
               for i in range(6)]
    P["psR"] = ctx.enter_context(nc.psum_tensor("psR", [128, 512], F32))
    P["psT"] = ctx.enter_context(nc.psum_tensor("psT", [128, 1024], BF16))

    emit_A(em, nc, P, din, dd, A)
    if debug_stream != "xr1":
        emit_B(em, nc, P, din, dd, A)
    if debug_stream not in ("xr1", "xr2"):
        emit_C(em, nc, P, din, dd, A)
    if debug_stream not in ("xr1", "xr2", "xr3"):
        emit_D(em, nc, P, din, dd, A)

    if debug_stream is not None:
        src = dd[debug_stream]
        em.join()
        for i in range(4):
            em.op("gpsimd", lambda E, i=i: E.dma_start(
                out=dd["y"][i * 1024:(i + 1) * 1024, :],
                in_=src[i * 1024:(i + 1) * 1024, :]), inc="yout", k=16)
        em.wait("gpsimd", "yout", em.cnt["yout"])
    em.join()


def _ln_tile(em, P, A, bufs, xt_ap, out_bf_ap, l2gamma):
    bn, stat, xnf = bufs["bn"], bufs["stat"], bufs["xnf"]
    em.op("vector", lambda E: E.bn_stats(bn[:, 0:6], xt_ap[:, 0:512]))
    em.op("vector", lambda E: E.bn_stats(bn[:, 6:12], xt_ap[:, 512:1024]))
    em.drain("vector")
    vv = em.op("vector", lambda E: E.bn_aggr(stat[:, 0:2], bn[:, 0:12]), inc="vv")
    em.wait("scalar", "vv", vv)
    em.op("scalar", lambda E: E.activation(stat[:, 2:3], stat[:, 1:2], AF.Ln,
                                           bias=1e-5))
    em.drain("scalar")
    sv = em.op("scalar", lambda E: E.activation(stat[:, 3:4], stat[:, 2:3],
                                                AF.Exp, scale=-0.5), inc="sv")
    em.wait("vector", "sv", sv)
    em.op("vector", lambda E: E.scalar_tensor_tensor(
        stat[:, 4:5], stat[:, 0:1], -1.0, stat[:, 3:4], ALU.mult, ALU.mult))
    em.drain("vector")
    vv = em.op("vector", lambda E: E.tensor_scalar(
        xnf[:], xt_ap, stat[:, 3:4], stat[:, 4:5], ALU.mult, ALU.add), inc="vv")
    if not l2gamma:
        em.drain("vector")
        vv = em.op("vector", lambda E: E.tensor_copy(out_bf_ap, xnf[:]), inc="vv")
        return vv
    em.wait("scalar", "vv", vv)
    em.op("scalar", lambda E: E.activation(bufs["dead"], xnf[:], AF.Square,
                                           accum_out=stat[:, 5:6]))
    em.drain("scalar")
    em.op("scalar", lambda E: E.activation(stat[:, 6:7], stat[:, 5:6], AF.Ln,
                                           bias=1e-6))
    em.drain("scalar")
    sv = em.op("scalar", lambda E: E.activation(stat[:, 7:8], stat[:, 6:7],
                                                AF.Exp, scale=-0.5), inc="sv")
    em.wait("vector", "sv", sv)
    vv = em.op("vector", lambda E: E.scalar_tensor_tensor(
        out_bf_ap, xnf[:], stat[:, 7:8], P["gammab"][:], ALU.mult, ALU.mult),
        inc="vv")
    return vv


def _transpose_tile(em, P, xnb_ap, dst_cols, guard):
    if guard[1] > 0:
        em.wait("tensor", guard[0], guard[1])
    for c in range(8):
        f = lambda E, c=c: E.transpose(
            P["psT"][:, c * 128:(c + 1) * 128],
            xnb_ap[:, c * 128:(c + 1) * 128],
            P["identb"][:])
        if c < 7:
            em.op("tensor", f)
        else:
            tv = em.op("tensor", f, inc="tv")
    em.wait("vector", "tv", tv)
    for c in range(8):
        f = lambda E, c=c: E.tensor_copy(dst_cols(c), P["psT"][:, c * 128:(c + 1) * 128])
        if c < 7:
            em.op("vector", f)
        else:
            vv = em.op("vector", f, inc="vv")
    return tv, vv


def emit_A(em, nc, P, din, dd, A):
    ps = P["ps"]
    XNF = A.bf16(8 * 4096)
    OAL = A.bf16(8 * 4096)
    QF = A.bf16(4096)
    KF = A.bf16(4096)
    VT = A.bf16(4096)
    WH = [A.bf16(3072), A.bf16(3072)]
    XT = [A.f32(1024), A.f32(1024)]
    XNB = [A.bf16(1024), A.bf16(1024)]
    XNFW = A.f32(1024)
    BCS = XNFW[:, 0:512]
    BN = A.f32(12)
    STAT = A.f32(8)
    M4 = [A.f32(4), A.f32(4)]
    M8 = [A.f32(8), A.f32(8)]
    M4B = [A.bf16(4), A.bf16(4)]
    NEGMB = [A.bf16(512), A.bf16(512)]
    RECI = A.f32(512)
    PT = [A.bf16(512), A.bf16(512), A.bf16(512), A.bf16(512)]
    XO = [A.f32(1024), A.f32(1024)]
    DEAD = OAL[:, 0:2048].bitcast(F32)
    bufs = dict(bn=BN, stat=STAT, xnf=XNFW, dead=DEAD)

    em.wait("vector", "bt", P["boot_sd"])
    em.op("vector", lambda E: E.tensor_scalar_mul(P["gammab"][:], P["gammab"][:], 32.0))
    em.drain("vector")

    xt_guard = [0, 0]
    xnb_guard = [0, 0]
    psT_guard = 0
    for t in range(NT):
        b = t % 2
        em.wait("sync", "vv", xt_guard[b])
        sd = em.op("sync", lambda E, t=t, b=b: E.dma_start(
            out=XT[b][:], in_=din["x"][t * 128:(t + 1) * 128, :]),
            inc=f"xA{b}", k=16)
        em.wait("vector", f"xA{b}", sd)
        em.wait("vector", "tv", xnb_guard[b])
        ln_vv = _ln_tile(em, P, A, bufs, XT[b][:], XNB[b][:], l2gamma=True)
        xt_guard[b] = em.cnt["vv"]
        em.wait("tensor", "vv", ln_vv)
        tv, vv = _transpose_tile(
            em, P, XNB[b][:],
            lambda c, t=t: XNF[:, c * 4096 + t * 128: c * 4096 + (t + 1) * 128],
            ("vv", psT_guard))
        psT_guard = vv
        xnb_guard[b] = tv
    xnf_done_vv = em.cnt["vv"]

    wh_guard = [0, 0]
    ps_guard = [0, 0, 0, 0]
    ps_rel = ["vv", "vv", "vv", "vv"]
    pt_guard = [0, 0, 0, 0]
    ps4_guard = 0
    ps5_guard = 0
    psR_guard = 0
    oal_done = []

    em.wait("tensor", "vv", xnf_done_vv)
    for h in range(8):
        wb = h % 2
        if wh_guard[wb] > 0:
            em.wait("gpsimd", "tv", wh_guard[wb])
        gd = em.op("gpsimd", lambda E, h=h, wb=wb: E.dma_start(
            out=WH[wb][:], in_=din["wqkv_h"][h * 128:(h + 1) * 128, :]),
            inc=f"whA{wb}", k=16)
        em.wait("tensor", f"whA{wb}", gd)

        for which, dst, woff in (("q", QF, 0), ("k", KF, 1024)):
            for tt in range(8):
                pb = tt % 4
                if ps_guard[pb] > 0:
                    em.wait("tensor", ps_rel[pb], ps_guard[pb])
                for kc in range(8):
                    f = lambda E, kc=kc, tt=tt, wb=wb, woff=woff, pb=pb: E.matmul(
                        ps[pb][:], WH[wb][:, woff + kc * 128: woff + (kc + 1) * 128],
                        XNF[:, kc * 4096 + tt * 512: kc * 4096 + (tt + 1) * 512],
                        start=(kc == 0), stop=(kc == 7))
                    if kc < 7:
                        em.op("tensor", f)
                    else:
                        tv = em.op("tensor", f, inc="tv")
                em.wait("vector", "tv", tv)
                ps_guard[pb] = em.op("vector", lambda E, dst=dst, tt=tt, pb=pb:
                    E.tensor_copy(dst[:, tt * 512:(tt + 1) * 512], ps[pb][:]),
                    inc="vv")
                ps_rel[pb] = "vv"
        for rnd in range(8):
            pb = rnd % 4
            if ps_guard[pb] > 0:
                em.wait("tensor", ps_rel[pb], ps_guard[pb])
            for t4 in range(4):
                tk = rnd * 4 + t4
                for kc in range(8):
                    f = lambda E, kc=kc, tk=tk, t4=t4, wb=wb, pb=pb: E.matmul(
                        ps[pb][:, t4 * 128:(t4 + 1) * 128],
                        XNF[:, kc * 4096 + tk * 128: kc * 4096 + (tk + 1) * 128],
                        WH[wb][:, 2048 + kc * 128: 2048 + (kc + 1) * 128],
                        start=(kc == 0), stop=(kc == 7))
                    if kc < 7 or t4 < 3:
                        em.op("tensor", f)
                    else:
                        tv = em.op("tensor", f, inc="tv")
            em.wait("vector", "tv", tv)
            ps_guard[pb] = em.op("vector", lambda E, rnd=rnd, pb=pb:
                E.tensor_copy(VT[:, rnd * 512:(rnd + 1) * 512], ps[pb][:]),
                inc="vv")
            ps_rel[pb] = "vv"
        wh_guard[wb] = em.cnt["tv"]
        em.wait("tensor", "vv", em.cnt["vv"])

        def pass1_steps(g):
            gp = g % 2
            steps = []
            for qt in range(4):
                for j in range(g + 1):
                    def step(qt=qt, j=j, g=g, gp=gp):
                        pb = 2 + (j % 2)
                        if ps_guard[pb] > 0:
                            em.wait("tensor", ps_rel[pb], ps_guard[pb])
                        tv = em.op("tensor", lambda E, g=g, qt=qt, j=j, pb=pb:
                            E.matmul(ps[pb][:],
                                     QF[:, (g * 4 + qt) * 128:(g * 4 + qt + 1) * 128],
                                     KF[:, j * 512:(j + 1) * 512],
                                     start=True, stop=True), inc="tv")
                        em.wait("vector", "tv", tv)
                        ps_guard[pb] = em.op("vector", lambda E, j=j, pb=pb, gp=gp:
                            E.reduce_max(M8[gp][:, j:j + 1], ps[pb][:], axis=AX.X),
                            inc="vv")
                        ps_rel[pb] = "vv"
                        if j == g:
                            em.drain("vector")
                            em.op("vector", lambda E, qt=qt, g=g, gp=gp: E.reduce_max(
                                M4[gp][:, qt:qt + 1], M8[gp][:, 0:g + 1], axis=AX.X))
                            em.drain("vector")
                    steps.append(step)
            return steps

        def pass1_tail(g):
            nonlocal psR_guard
            gp = g % 2
            vv = em.op("vector", lambda E, gp=gp: E.tensor_copy(
                M4B[gp][:], M4[gp][:]), inc="vv")
            em.wait("tensor", "vv", max(vv, psR_guard))
            for qt in range(4):
                f = lambda E, qt=qt, gp=gp: E.matmul(
                    P["psR"][0:1, qt * 128:(qt + 1) * 128], M4B[gp][:, qt:qt + 1],
                    P["identb"][:], start=True, stop=True)
                if qt < 3:
                    em.op("tensor", f)
                else:
                    tv = em.op("tensor", f, inc="tv")
            em.wait("vector", "tv", tv)
            psR_guard = em.op("vector", lambda E, gp=gp: E.tensor_scalar_mul(
                NEGMB[gp][0:1, :], P["psR"][0:1, 0:512], -1.0), inc="vv")

        for s1 in pass1_steps(0):
            s1()
        pass1_tail(0)
        for g in range(8):
            gp = g % 2
            nxt = pass1_steps(g + 1) if g < 7 else []
            nch = 4 * g + 4
            negm_vv = psR_guard
            for c in range(nch):
                pb = c % 2
                if ps_guard[pb] > 0:
                    em.wait("tensor", ps_rel[pb], ps_guard[pb])
                em.wait("tensor", "vv", negm_vv)
                em.op("tensor", lambda E, c=c, pb=pb, g=g: E.matmul(
                    ps[pb][:], KF[:, c * 128:(c + 1) * 128],
                    QF[:, g * 512:(g + 1) * 512], start=True, stop=False))
                tv = em.op("tensor", lambda E, pb=pb, gp=gp: E.matmul(
                    ps[pb][:], P["onesrb"][0:1, :], NEGMB[gp][0:1, :],
                    start=False, stop=True), inc="tv")
                if pt_guard[pb] > 0:
                    em.wait("scalar", "tv", pt_guard[pb])
                em.wait("scalar", "tv", tv)
                sv = em.op("scalar", lambda E, pb=pb: E.activation(
                    PT[pb][:], ps[pb][:], AF.Exp, scale=SCALE_ATTN), inc="sv")
                ps_guard[pb] = sv
                ps_rel[pb] = "sv"
                dep = ("sv", sv)
                if c >= 4 * g:
                    j = c - 4 * g
                    em.wait("vector", "sv", sv)
                    vv = em.op("vector", lambda E, pb=pb, j=j: E.tensor_mul(
                        PT[pb][:], PT[pb][:], P["masks"][:, j * 512:(j + 1) * 512]),
                        inc="vv")
                    dep = ("vv", vv)
                em.wait("tensor", dep[0], dep[1])
                if c == 0:
                    em.wait("tensor", "vv", ps4_guard)
                    em.wait("tensor", "vv", ps5_guard)
                em.op("tensor", lambda E, c=c, pb=pb, first=(c == 0),
                      last=(c == nch - 1): E.matmul(
                    ps[4][:], VT[:, c * 128:(c + 1) * 128], PT[pb][:],
                    start=first, stop=last))
                tv = em.op("tensor", lambda E, pb=pb, first=(c == 0),
                           last=(c == nch - 1): E.matmul(
                    ps[5][0:1, :], P["onescb"][:, 0:1], PT[pb][:],
                    start=first, stop=last), inc="tv")
                pt_guard[pb] = tv
                if nxt:
                    nxt.pop(0)()
            em.wait("vector", "tv", tv)
            ps5_guard = em.op("vector", lambda E: E.reciprocal(
                RECI[0:1, :], ps[5][0:1, 0:512]), inc="vv")
            em.wait("tensor", "vv", ps5_guard)
            em.wait("tensor", "vv", psR_guard)
            tv = em.op("tensor", lambda E: E.matmul(
                P["psR"][:], P["onesr"][0:1, :], RECI[0:1, :],
                start=True, stop=True), inc="tv")
            em.wait("vector", "tv", tv)
            em.op("vector", lambda E: E.tensor_copy(BCS[:], P["psR"][:, 0:512]))
            em.drain("vector")
            psR_guard = em.cnt["vv"] + 1
            ps4_guard = em.op("vector", lambda E, h=h, g=g: E.tensor_mul(
                OAL[:, h * 4096 + g * 512: h * 4096 + (g + 1) * 512],
                ps[4][:], BCS[:]), inc="vv")
            for s1 in nxt:
                s1()
            if g < 7:
                pass1_tail(g + 1)
    oal_vv = em.cnt["vv"]

    WOUT = XNF[:, 0:8192]
    em.wait("gpsimd", "tv", em.cnt["tv"])
    gd = em.op("gpsimd", lambda E: E.dma_start(out=WOUT[:, 0:8192],
               in_=din["wout_c"]), inc="woA", k=16)
    em.wait("tensor", "woA", gd)
    em.wait("tensor", "vv", oal_vv)
    xo_guard = [0, 0]
    xt_guard = [em.cnt["vv"], em.cnt["vv"]]
    psout_guard = [0, 0]
    for t in range(NT):
        b = t % 2
        em.wait("sync", "vv", xt_guard[b])
        sd = em.op("sync", lambda E, t=t, b=b: E.dma_start(
            out=XT[b][:], in_=din["x"][t * 128:(t + 1) * 128, :]),
            inc=f"xA{b}", k=16)
        for half in range(2):
            if psout_guard[half] > 0:
                em.wait("tensor", "vv", psout_guard[half])
            for hh in range(8):
                f = lambda E, hh=hh, t=t, half=half: E.matmul(
                    ps[half][:], OAL[:, hh * 4096 + t * 128: hh * 4096 + (t + 1) * 128],
                    WOUT[:, hh * 1024 + half * 512: hh * 1024 + (half + 1) * 512],
                    start=(hh == 0), stop=(hh == 7))
                if hh < 7:
                    em.op("tensor", f)
                else:
                    tv = em.op("tensor", f, inc="tv")
            em.wait("vector", "tv", tv)
            em.wait("vector", f"xA{b}", sd)
            if half == 0 and xo_guard[b] > 0:
                em.wait("vector", f"xoS{b}", xo_guard[b])
            psout_guard[half] = em.op(
                "vector", lambda E, b=b, half=half: E.tensor_add(
                    XO[b][:, half * 512:(half + 1) * 512], ps[half][:],
                    XT[b][:, half * 512:(half + 1) * 512]), inc="vv")
        xt_guard[b] = em.cnt["vv"]
        em.wait("gpsimd", "vv", em.cnt["vv"])
        xo_guard[b] = em.op("gpsimd", lambda E, t=t, b=b: E.dma_start(
            out=dd["xr1"][t * 128:(t + 1) * 128, :], in_=XO[b][:]),
            inc=f"xoS{b}", k=16)
    A.reset()


def emit_B(em, nc, P, din, dd, A):
    em.join()
    ps = P["ps"]
    W1S = A.bf16(32768)
    W2S = A.bf16(32768)
    HFM = A.bf16(16384)
    XNFS = A.bf16(4096)
    XTS = [A.f32(1024) for _ in range(4)]
    XNB = [A.bf16(1024)]
    XNB.append(XNB[0])
    XNFW = A.f32(1024)
    BCS = XNFW[:, 0:512]
    BN = A.f32(12)
    STAT = A.f32(8)
    B1S = A.f32(32)
    B2B = A.bf16(1024)
    XO = [A.f32(1024)]
    XO.append(XO[0])
    DEAD = A.f32(4)
    bufs = dict(bn=BN, stat=STAT, xnf=XNFW, dead=DEAD)

    w1v = em.op("gpsimd", lambda E: E.dma_start(out=W1S[:], in_=din["w1_c"]),
                inc="w1B", k=16)
    w2v = em.op("gpsimd", lambda E: E.dma_start(out=W2S[:], in_=din["w2_c"]),
                inc="w2B", k=16)
    bbv = em.op("sync", lambda E: E.dma_start(out=B1S[:], in_=din["b1_t"]),
                inc="bB", k=16)
    b2v = em.op("gpsimd", lambda E: E.dma_start(
        out=B2B[:], in_=din["b2_r"].to_broadcast([128, D])), inc="b2B", k=16)
    em.wait("tensor", "w1B", w1v)
    em.wait("tensor", "w2B", w2v)
    em.wait("scalar", "bB", bbv)
    em.wait("vector", "b2B", b2v)

    xt_guard = [0, 0, 0, 0]
    xnb_guard = [0, 0]
    psT_guard = 0
    xnfs_guard = 0
    hfm_guard = 0
    psH_guard = [0, 0, 0, 0]
    psO_guard = [0, 0]
    xo_guard = [0, 0]
    for s in range(NSLAB):
        for tt in range(4):
            t = s * 4 + tt
            b = 0
            em.wait("sync", "vv", xt_guard[tt])
            sd = em.op("sync", lambda E, t=t, tt=tt: E.dma_start(
                out=XTS[tt][:], in_=dd["xr1"][t * 128:(t + 1) * 128, :]),
                inc=f"xB{tt}", k=16)
            em.wait("vector", f"xB{tt}", sd)
            em.wait("vector", "tv", xnb_guard[b])
            ln_vv = _ln_tile(em, P, A, bufs, XTS[tt][:], XNB[b][:], l2gamma=False)
            em.wait("tensor", "vv", ln_vv)
            if psT_guard:
                em.wait("tensor", "vv", psT_guard)
            for c in range(8):
                f = lambda E, c=c: E.transpose(
                    P["psT"][:, c * 128:(c + 1) * 128],
                    XNB[b][:, c * 128:(c + 1) * 128], P["identb"][:])
                if c < 7:
                    em.op("tensor", f)
                else:
                    tv = em.op("tensor", f, inc="tv")
            em.wait("vector", "tv", tv)
            em.wait("vector", "tv", xnfs_guard)
            for c in range(8):
                f = lambda E, c=c, tt=tt: E.tensor_copy(
                    XNFS[:, c * 512 + tt * 128: c * 512 + (tt + 1) * 128],
                    P["psT"][:, c * 128:(c + 1) * 128])
                if c < 7:
                    em.op("vector", f)
                else:
                    vv = em.op("vector", f, inc="vv")
            psT_guard = vv
            xnb_guard[0] = tv
        xnfs_vv = em.cnt["vv"]
        em.wait("tensor", "vv", xnfs_vv)
        em.wait("scalar", "tv", hfm_guard)
        for ic in range(32):
            pb = ic % 4
            if psH_guard[pb]:
                em.wait("tensor", "sv", psH_guard[pb])
            for kc in range(8):
                f = lambda E, kc=kc, ic=ic, pb=pb: E.matmul(
                    ps[pb][:], W1S[:, kc * 4096 + ic * 128: kc * 4096 + (ic + 1) * 128],
                    XNFS[:, kc * 512:(kc + 1) * 512], start=(kc == 0), stop=(kc == 7))
                if kc < 7:
                    em.op("tensor", f)
                else:
                    tv = em.op("tensor", f, inc="tv")
            em.wait("scalar", "tv", tv)
            psH_guard[pb] = em.op("scalar", lambda E, ic=ic, pb=pb: E.activation(
                HFM[:, ic * 512:(ic + 1) * 512], ps[pb][:], AF.Relu,
                bias=B1S[:, ic:ic + 1]), inc="sv")
        h_sv = em.cnt["sv"]
        xnfs_guard = em.cnt["tv"]
        em.wait("tensor", "sv", h_sv)
        for tt in range(4):
            t = s * 4 + tt
            b = 0
            for half in range(2):
                if psO_guard[half]:
                    em.wait("tensor", "vv", psO_guard[half])
                for ic in range(32):
                    f = lambda E, ic=ic, tt=tt, half=half: E.matmul(
                        ps[4 + half][:],
                        HFM[:, ic * 512 + tt * 128: ic * 512 + (tt + 1) * 128],
                        W2S[:, ic * 1024 + half * 512: ic * 1024 + (half + 1) * 512],
                        start=(ic == 0), stop=(ic == 31))
                    if ic < 31:
                        em.op("tensor", f)
                    else:
                        tv = em.op("tensor", f, inc="tv")
                em.wait("vector", "tv", tv)
                if xo_guard[b]:
                    em.wait("vector", f"xoB{b}", xo_guard[b])
                em.op("vector", lambda E, b=b, half=half: E.tensor_add(
                    XO[b][:, half * 512:(half + 1) * 512], ps[4 + half][:],
                    B2B[:, half * 512:(half + 1) * 512]))
                em.drain("vector")
                psO_guard[half] = em.op(
                    "vector", lambda E, b=b, half=half, tt=tt: E.tensor_add(
                        XO[b][:, half * 512:(half + 1) * 512],
                        XO[b][:, half * 512:(half + 1) * 512],
                        XTS[tt][:, half * 512:(half + 1) * 512]), inc="vv")
            xt_guard[tt] = em.cnt["vv"]
            em.wait("gpsimd", "vv", em.cnt["vv"])
            xo_guard[b] = em.op("gpsimd", lambda E, t=t, b=b: E.dma_start(
                out=dd["xr2"][t * 128:(t + 1) * 128, :], in_=XO[b][:]),
                inc=f"xoB{b}", k=16)
        hfm_guard = em.cnt["tv"]
    A.reset()


def emit_C(em, nc, P, din, dd, A):
    em.join()
    ps = P["ps"]

    INPC = A.bf16(16384)
    WPC = {}
    for nm in ("qp", "kp", "vp", "ip", "fp", "op"):
        WPC[nm] = A.bf16(8192)
    XNFS = A.bf16(4096)
    XIF = A.bf16(4096)
    XT = [A.f32(1024), A.f32(1024)]
    XNB = [A.bf16(1024)]
    XNFW = A.f32(1024)
    BCS = XNFW[:, 0:512]
    BN = A.f32(12)
    STAT = A.f32(8)
    DEAD = A.f32(4)
    IBB = A.f32(1024)
    FBB = A.f32(1024)
    OBB = A.f32(1024)
    PRJ = [A.bf16(1024), A.bf16(1024)]
    PRF = [A.f32(1024), A.f32(1024)]
    OGT = A.f32(1024)
    XGT = A.f32(1024)
    GT = [A.bf16(1024), A.bf16(1024)]
    bufs = dict(bn=BN, stat=STAT, xnf=XNFW, dead=DEAD)

    wv = {}
    wv["inp"] = em.op("gpsimd", lambda E: E.dma_start(out=INPC[:], in_=din["inproj_c"]),
                      inc="wC", k=16)
    for nm in ("qp", "kp", "vp", "ip", "fp", "op"):
        wv[nm] = em.op("gpsimd", lambda E, nm=nm: E.dma_start(
            out=WPC[nm][:], in_=din[nm + "_c"]), inc="wC", k=16)
    wC_all = em.cnt["wC"]
    bv = em.op("sync", lambda E: E.dma_start(
        out=IBB[:], in_=din["ib_r"].to_broadcast([128, D])), inc="bC", k=16)
    bv = em.op("sync", lambda E: E.dma_start(
        out=FBB[:], in_=din["fb_r"].to_broadcast([128, D])), inc="bC", k=16)
    bv = em.op("sync", lambda E: E.dma_start(
        out=OBB[:], in_=din["ob_r"].to_broadcast([128, D])), inc="bC", k=16)
    em.wait("tensor", "wC", wC_all)
    em.wait("vector", "bC", bv)

    xt_guard = [0, 0]
    xnb_guard = [0]
    psT_guard = 0
    xnfs_guard = 0
    prj_guard = [0, 0]
    prf_guard = [0, 0]
    gt_guard = [0, 0]
    psp_guard = [0, 0]
    for s in range(NSLAB):
        for tt in range(4):
            t = s * 4 + tt
            b = tt % 2
            em.wait("sync", "vv", xt_guard[b])
            sd = em.op("sync", lambda E, t=t, b=b: E.dma_start(
                out=XT[b][:], in_=dd["xr2"][t * 128:(t + 1) * 128, :]),
                inc=f"xC{b}", k=16)
            em.wait("vector", f"xC{b}", sd)
            em.wait("vector", "tv", xnb_guard[0])
            ln_vv = _ln_tile(em, P, A, bufs, XT[b][:], XNB[0][:], l2gamma=False)
            xt_guard[b] = em.cnt["vv"]
            em.wait("tensor", "vv", ln_vv)
            if psT_guard:
                em.wait("tensor", "vv", psT_guard)
            for c in range(8):
                f = lambda E, c=c: E.transpose(
                    P["psT"][:, c * 128:(c + 1) * 128],
                    XNB[0][:, c * 128:(c + 1) * 128], P["identb"][:])
                em.op("tensor", f) if c < 7 else None
                if c == 7:
                    tv = em.op("tensor", f, inc="tv")
            em.wait("vector", "tv", tv)
            em.wait("vector", "tv", xnfs_guard)
            for c in range(8):
                f = lambda E, c=c, tt=tt: E.tensor_copy(
                    XNFS[:, c * 512 + tt * 128: c * 512 + (tt + 1) * 128],
                    P["psT"][:, c * 128:(c + 1) * 128])
                if c < 7:
                    em.op("vector", f)
                else:
                    vv = em.op("vector", f, inc="vv")
            psT_guard = vv
            xnb_guard[0] = tv
        xnfs_vv = em.cnt["vv"]
        em.wait("tensor", "vv", xnfs_vv)
        for io in range(8):
            pb = io % 2
            if psp_guard[pb]:
                em.wait("tensor", "vv", psp_guard[pb])
            for kc in range(8):
                f = lambda E, kc=kc, io=io, pb=pb: E.matmul(
                    ps[pb][:], INPC[:, kc * 2048 + io * 128: kc * 2048 + (io + 1) * 128],
                    XNFS[:, kc * 512:(kc + 1) * 512], start=(kc == 0), stop=(kc == 7))
                if kc < 7:
                    em.op("tensor", f)
                else:
                    tv = em.op("tensor", f, inc="tv")
            em.wait("vector", "tv", tv)
            psp_guard[pb] = em.op("vector", lambda E, io=io, pb=pb: E.tensor_copy(
                XIF[:, io * 512:(io + 1) * 512], ps[pb][:]), inc="vv")
        xif_vv = em.cnt["vv"]
        em.wait("tensor", "vv", xif_vv)
        ps23e = 0
        for tt in range(4):
            t = s * 4 + tt
            for j, (nm, spname) in enumerate((("qp", "sp_q"), ("kp", "sp_k"),
                                              ("vp", "sp_v"))):
                pb = j % 2
                em.wait("tensor", "vv", ps23e)
                if prj_guard[pb]:
                    em.wait("vector", f"pC{pb}", prj_guard[pb])
                for half in range(2):
                    pp = 2 + half
                    for io in range(8):
                        f = lambda E, io=io, tt=tt, half=half, nm=nm, pp=pp: E.matmul(
                            ps[pp][:], XIF[:, io * 512 + tt * 128: io * 512 + (tt + 1) * 128],
                            WPC[nm][:, io * 1024 + half * 512: io * 1024 + (half + 1) * 512],
                            start=(io == 0), stop=(io == 7))
                        if io < 7 or half < 1:
                            em.op("tensor", f)
                        else:
                            tv = em.op("tensor", f, inc="tv")
                em.wait("vector", "tv", tv)
                em.op("vector", lambda E, pb=pb: E.tensor_copy(
                    PRJ[pb][:, 0:512], ps[2][:]))
                vv = em.op("vector", lambda E, pb=pb: E.tensor_copy(
                    PRJ[pb][:, 512:1024], ps[3][:]), inc="vv")
                ps23e = vv
                em.wait("gpsimd", "vv", vv)
                prj_guard[pb] = em.op("gpsimd", lambda E, t=t, pb=pb, spname=spname:
                    E.dma_start(out=dd[spname][t * 128:(t + 1) * 128, :],
                                in_=PRJ[pb][:]), inc=f"pC{pb}", k=16)
            for j, (nm, spname, BB) in enumerate((("ip", "sp_i", IBB),
                                                  ("fp", "sp_f", FBB))):
                pb = j % 2
                em.wait("tensor", "vv", ps23e)
                if prf_guard[pb]:
                    em.wait("vector", f"pF{pb}", prf_guard[pb])
                for half in range(2):
                    pp = 2 + half
                    for io in range(8):
                        f = lambda E, io=io, tt=tt, half=half, nm=nm, pp=pp: E.matmul(
                            ps[pp][:], XIF[:, io * 512 + tt * 128: io * 512 + (tt + 1) * 128],
                            WPC[nm][:, io * 1024 + half * 512: io * 1024 + (half + 1) * 512],
                            start=(io == 0), stop=(io == 7))
                        if io < 7 or half < 1:
                            em.op("tensor", f)
                        else:
                            tv = em.op("tensor", f, inc="tv")
                em.wait("vector", "tv", tv)
                em.op("vector", lambda E, pb=pb, BB=BB: E.tensor_add(
                    PRF[pb][:, 0:512], ps[2][:], BB[:, 0:512]))
                vv = em.op("vector", lambda E, pb=pb, BB=BB: E.tensor_add(
                    PRF[pb][:, 512:1024], ps[3][:], BB[:, 512:1024]), inc="vv")
                ps23e = vv
                em.wait("gpsimd", "vv", vv)
                prf_guard[pb] = em.op("gpsimd", lambda E, t=t, pb=pb, spname=spname:
                    E.dma_start(out=dd[spname][t * 128:(t + 1) * 128, :],
                                in_=PRF[pb][:]), inc=f"pF{pb}", k=16)
            gb = tt % 2
            em.wait("tensor", "vv", ps23e)
            if gt_guard[gb]:
                em.wait("vector", f"gC{gb}", gt_guard[gb])
            for half in range(2):
                pp = 2 + half
                for io in range(8):
                    f = lambda E, io=io, tt=tt, half=half, pp=pp: E.matmul(
                        ps[pp][:], XIF[:, io * 512 + tt * 128: io * 512 + (tt + 1) * 128],
                        WPC["op"][:, io * 1024 + half * 512: io * 1024 + (half + 1) * 512],
                        start=(io == 0), stop=(io == 7))
                    if io < 7 or half < 1:
                        em.op("tensor", f)
                    else:
                        tv = em.op("tensor", f, inc="tv")
            em.wait("vector", "tv", tv)
            em.op("vector", lambda E: E.tensor_add(OGT[:, 0:512], ps[2][:],
                                                   OBB[:, 0:512]))
            vv = em.op("vector", lambda E: E.tensor_add(OGT[:, 512:1024], ps[3][:],
                                                        OBB[:, 512:1024]), inc="vv")
            ps23e = vv
            for half in range(2):
                pp = 2 + half
                em.wait("tensor", "vv", vv)
                for kc in range(8):
                    f = lambda E, kc=kc, tt=tt, half=half, pp=pp: E.matmul(
                        ps[pp][:], XNFS[:, kc * 512 + tt * 128: kc * 512 + (tt + 1) * 128],
                        INPC[:, kc * 2048 + 1024 + half * 512: kc * 2048 + 1024 + (half + 1) * 512],
                        start=(kc == 0), stop=(kc == 7))
                    if kc < 7 or half < 1:
                        em.op("tensor", f)
                    else:
                        tv = em.op("tensor", f, inc="tv")
            em.wait("vector", "tv", tv)
            em.op("vector", lambda E: E.tensor_copy(XGT[:, 0:512], ps[2][:]))
            vv = em.op("vector", lambda E: E.tensor_copy(XGT[:, 512:1024],
                                                         ps[3][:]), inc="vv")
            ps23e = vv
            em.wait("scalar", "vv", vv)
            em.op("scalar", lambda E: E.activation(OGT[:], OGT[:], AF.Sigmoid))
            sv = em.op("scalar", lambda E: E.activation(XGT[:], XGT[:],
                                                        AF.Sigmoid), inc="sv")
            em.wait("vector", "sv", sv)
            vv = em.op("vector", lambda E, gb=gb: E.tensor_mul(
                GT[gb][:], OGT[:], XGT[:]), inc="vv")
            em.wait("gpsimd", "vv", vv)
            gt_guard[gb] = em.op("gpsimd", lambda E, t=t, gb=gb: E.dma_start(
                out=dd["sp_g"][t * 128:(t + 1) * 128, :], in_=GT[gb][:]),
                inc=f"gC{gb}", k=16)
            xnfs_guard = em.cnt["tv"]
    c1_done = dict(em.cnt)
    A.reset()

    em.join()
    OUTPC = A.bf16(8192)
    QNB = A.f32(1024)
    KNB = A.f32(1024)
    EYE16R = A.f32(256)
    LQ = [A.bf16(1024), A.bf16(1024)]
    LK = [A.bf16(1024), A.bf16(1024)]
    LV = [A.bf16(1024), A.bf16(1024)]
    LI = [A.f32(1024), A.f32(1024)]
    LF = [A.f32(1024), A.f32(1024)]
    LG = [A.bf16(1024), A.bf16(1024)]
    XT2 = [A.f32(1024), A.f32(1024)]
    NL = A.f32(1024)
    LROW = A.f32(1024)
    R1 = A.f32(1024)
    R2 = A.f32(1024)
    K1 = A.bf16(1024)
    K1F = A.bf16(1024)
    A1S = A.f32(1024)
    ARGQ = A.f32(1024)
    ARGK = A.f32(1024)
    ARGB = A.f32(1024)
    SQ = A.f32(1024)
    MS = A.f32(32)
    RSE = SQ
    QN = A.f32(1024)
    KN = A.f32(1024)
    QA = A.bf16(1024)
    QI = A.f32(1024)
    KT = A.bf16(1024)
    KB = A.bf16(1024)
    LFW = A.f32(1024)
    QAF = A.bf16(2048)
    QIF = A.f32(2048)
    KTF = A.bf16(2048)
    ASB_ALL = A.bf16(2048)
    HVS = A.f32(1040)
    DEN16 = A.f32(32)
    PFE = A.f32(1040)
    HT = A.f32(1024)
    HG = A.bf16(1024)
    HGF = A.bf16(1024)
    CAU = A.f32(1040)
    NLE = A.f32(16)
    PF16 = A.f32(16)
    DCOL = A.f32(4)
    XO2 = [A.f32(1024)]
    bv2 = em.op("sync", lambda E: E.dma_start(
        out=QNB[:], in_=din["qnw_r"].to_broadcast([128, D])), inc="bC2", k=16)
    bv2 = em.op("sync", lambda E: E.dma_start(
        out=KNB[:], in_=din["knw_r"].to_broadcast([128, D])), inc="bC2", k=16)
    wv2 = em.op("gpsimd", lambda E: E.dma_start(out=OUTPC[:], in_=din["outp_c"]),
                inc="wC2", k=16)
    eyev = em.op("sync", lambda E: E.dma_start(out=EYE16R[0:1, :],
                 in_=dd["eye16r"]), inc="bC2", k=16)
    em.wait("vector", "bC2", em.cnt["bC2"])
    em.wait("tensor", "wC2", wv2)
    em.wait("tensor", "bC2", em.cnt["bC2"])

    em.op("vector", lambda E: E.memset(CAU[:], 0.0))
    em.op("vector", lambda E: E.memset(ps[4][:, 0:128], 0.0))
    em.drain("vector")

    ld_guard = {}
    for st in ("q", "k", "v", "i", "f", "g", "x"):
        ld_guard[st] = [0, 0]
    LD = dict(q=(LQ, "sp_q"), k=(LK, "sp_k"), v=(LV, "sp_v"),
              i=(LI, "sp_i"), f=(LF, "sp_f"), g=(LG, "sp_g"))
    sdv = {}
    xo_guard2 = 0
    cau_vv = em.cnt["vv"]
    ps01_rel = 0
    prev_rse_rel = 0
    psPT_guard = 0
    psPTb_guard = 0

    for t in range(NT):
        b = t % 2
        for st, (bufs2, spname) in LD.items():
            em.wait("sync", "vv", ld_guard[st][b])
            sdv[st] = em.op("sync", lambda E, t=t, b=b, bufs2=bufs2, spname=spname:
                E.dma_start(out=bufs2[b][:],
                            in_=dd[spname][t * 128:(t + 1) * 128, :]),
                inc=f"L{st}{b}", k=16)
        em.wait("sync", "vv", ld_guard["x"][b])
        sdv["x"] = em.op("sync", lambda E, t=t, b=b: E.dma_start(
            out=XT2[b][:], in_=dd["xr2"][t * 128:(t + 1) * 128, :]),
            inc=f"Lx{b}", k=16)

        em.wait("scalar", f"Lf{b}", sdv["f"])
        em.wait("scalar", "vv", prev_rse_rel)
        em.op("scalar", lambda E, b=b: E.activation(SQ[:], LF[b][:], AF.Exp,
                                                    scale=-1.0))
        em.drain("scalar")
        sv = em.op("scalar", lambda E: E.activation(LFW[:], SQ[:], AF.Ln,
                                                    bias=1.0), inc="sv")
        em.wait("tensor", "sv", sv)
        em.wait("tensor", "vv", max(cau_vv, ps01_rel))
        for half in range(2):
            tv = em.op("tensor", lambda E, half=half: E.matmul(
                ps[half][:], P["ltincT"][:],
                LFW[:, half * 512:(half + 1) * 512], start=True, stop=True),
                inc="tv")
        em.wait("vector", "tv", tv)
        em.op("vector", lambda E: E.tensor_copy(NL[:, 0:512], ps[0][:]))
        vv = em.op("vector", lambda E: E.tensor_copy(NL[:, 512:1024], ps[1][:]),
                   inc="vv")
        em.wait("tensor", "vv", em.cnt["vv"])
        for j, W in enumerate((P["c1w"], P["c2w"], P["onesc"])):
            for half in range(2):
                tvl = em.op("tensor", lambda E, j=j, half=half, W=W: E.matmul(
                    ps[2 * j + half][0:1, 0:512], W[:, 0:1],
                    LFW[:, half * 512:(half + 1) * 512], start=True, stop=True),
                    inc="tv")
        em.wait("vector", "tv", tvl)
        for j, R in enumerate((R1, R2, LROW)):
            em.op("vector", lambda E, j=j, R=R: E.tensor_copy(
                R[0:1, 0:512], ps[2 * j][0:1, 0:512]))
            vv = em.op("vector", lambda E, j=j, R=R: E.tensor_copy(
                R[0:1, 512:1024], ps[2 * j + 1][0:1, 0:512]), inc="vv")
        em.wait("vector", f"Li{b}", sdv["i"])
        em.wait("tensor", "vv", vv)
        for half in range(2):
            tvb = em.op("tensor", lambda E, half=half: E.matmul(
                ps[2 + half][:], P["onesr"][0:1, :],
                R1[0:1, half * 512:(half + 1) * 512], start=True, stop=True),
                inc="tv")
        em.wait("vector", "tv", tvb)
        for half in range(2):
            em.op("vector", lambda E, half=half: E.tensor_sub(
                ARGQ[0:64, half * 512:(half + 1) * 512],
                ps[2 + half][0:64, :], NL[0:64, half * 512:(half + 1) * 512]))
            em.op("vector", lambda E, half=half, b=b: E.tensor_sub(
                A1S[0:64, half * 512:(half + 1) * 512],
                LI[b][0:64, half * 512:(half + 1) * 512], ps[2 + half][0:64, :]))
        vv = em.op("vector", lambda E: E.engine_nop(), inc="vv")
        em.wait("tensor", "vv", vv)
        for half in range(2):
            tvb = em.op("tensor", lambda E, half=half: E.matmul(
                ps[2 + half][:], P["onesr"][0:1, :],
                R2[0:1, half * 512:(half + 1) * 512], start=True, stop=True),
                inc="tv")
        em.wait("vector", "tv", tvb)
        for half in range(2):
            em.op("vector", lambda E, half=half: E.tensor_sub(
                ARGQ[64:128, half * 512:(half + 1) * 512],
                ps[2 + half][64:128, :], NL[64:128, half * 512:(half + 1) * 512]))
            em.op("vector", lambda E, half=half, b=b: E.tensor_sub(
                ARGK[:, half * 512:(half + 1) * 512],
                LI[b][:, half * 512:(half + 1) * 512], ps[2 + half][:, :]))
        vv = em.op("vector", lambda E: E.engine_nop(), inc="vv")
        em.op("vector", lambda E, b=b: E.tensor_add(ARGB[:], LI[b][:], NL[:]))
        em.drain("vector")
        em.wait("tensor", "vv", vv)
        for half in range(2):
            tvb = em.op("tensor", lambda E, half=half: E.matmul(
                ps[2 + half][:], P["onesr"][0:1, :],
                LROW[0:1, half * 512:(half + 1) * 512], start=True, stop=True),
                inc="tv")
        em.wait("vector", "tv", tvb)
        for half in range(2):
            em.op("vector", lambda E, half=half: E.tensor_sub(
                ARGB[:, half * 512:(half + 1) * 512],
                ARGB[:, half * 512:(half + 1) * 512], ps[2 + half][:, :]))
        em.drain("vector")
        em.op("vector", lambda E: E.tensor_add(ARGK[:], ARGK[:], NL[:]))
        em.op("vector", lambda E: E.tensor_add(A1S[0:64, :], A1S[0:64, :],
                                               NL[0:64, :]))
        vv = em.op("vector", lambda E: E.engine_nop(), inc="vv")
        em.wait("vector", f"Lq{b}", sdv["q"])
        em.op("vector", lambda E, b=b: E.tensor_mul(SQ[:], LQ[b][:], LQ[b][:]))
        em.drain("vector")
        em.op("vector", lambda E: E.tensor_reduce(
            MS[:, 0:16].rearrange("p (h o) -> p h o", o=1),
            SQ[:].rearrange("p (h d) -> p h d", d=64), AX.X, ALU.add))
        em.drain("vector")
        em.wait("vector", f"Lk{b}", sdv["k"])
        em.op("vector", lambda E, b=b: E.tensor_mul(SQ[:], LK[b][:], LK[b][:]))
        em.drain("vector")
        vv = em.op("vector", lambda E: E.tensor_reduce(
            MS[:, 16:32].rearrange("p (h o) -> p h o", o=1),
            SQ[:].rearrange("p (h d) -> p h d", d=64), AX.X, ALU.add), inc="vv")
        em.wait("scalar", "vv", vv)
        em.op("scalar", lambda E: E.activation(MS[:, 0:32], MS[:, 0:32], AF.Ln,
                                               bias=1e-6, scale=1.0 / 64))
        em.drain("scalar")
        sv = em.op("scalar", lambda E: E.activation(MS[:, 0:32], MS[:, 0:32],
                                                    AF.Exp, scale=-0.5), inc="sv")
        em.wait("scalar", "vv", em.cnt["vv"])
        em.op("scalar", lambda E: E.activation(ARGQ[:], ARGQ[:], AF.Exp))
        em.op("scalar", lambda E: E.activation(ARGK[:], ARGK[:], AF.Exp))
        em.op("scalar", lambda E: E.activation(A1S[0:64, :], A1S[0:64, :], AF.Exp))
        sv2 = em.op("scalar", lambda E: E.activation(ARGB[:], ARGB[:], AF.Exp),
                    inc="sv")
        em.wait("vector", "sv", sv)
        em.op("vector", lambda E: E.tensor_copy(
            RSE[:].rearrange("p (h d) -> p h d", d=64),
            MS[:, 0:16].unsqueeze(2).to_broadcast([128, 16, 64])))
        em.drain("vector")
        em.op("vector", lambda E, b=b: E.tensor_mul(QN[:], LQ[b][:], RSE[:]))
        em.drain("vector")
        em.op("vector", lambda E: E.tensor_mul(QN[:], QN[:], QNB[:]))
        em.op("vector", lambda E: E.tensor_copy(
            RSE[:].rearrange("p (h d) -> p h d", d=64),
            MS[:, 16:32].unsqueeze(2).to_broadcast([128, 16, 64])))
        em.drain("vector")
        em.op("vector", lambda E, b=b: E.tensor_mul(KN[:], LK[b][:], RSE[:]))
        em.drain("vector")
        em.op("vector", lambda E: E.tensor_mul(KN[:], KN[:], KNB[:]))
        em.drain("vector")
        em.wait("vector", "sv", sv2)
        em.op("vector", lambda E: E.tensor_mul(QA[:], QN[:], ARGQ[:]))
        em.op("vector", lambda E: E.tensor_mul(KT[:], KN[:], ARGK[:]))
        em.op("vector", lambda E: E.tensor_mul(K1[0:64, :], KN[0:64, :],
                                               A1S[0:64, :]))
        kb_vv = em.op("vector", lambda E: E.tensor_mul(KB[:], KN[:], ARGB[:]),
                      inc="vv")
        em.wait("scalar", "vv", kb_vv)
        sv3 = em.op("scalar", lambda E: E.activation(SQ[:], NL[:], AF.Exp,
                                                     scale=-1.0), inc="sv")
        em.drain("vector")
        em.wait("vector", "sv", sv3)
        vv = em.op("vector", lambda E: E.tensor_mul(QI[:], QN[:], SQ[:]), inc="vv")
        ld_guard["q"][b] = vv
        ld_guard["k"][b] = vv
        em.wait("tensor", f"Lv{b}", sdv["v"])
        ld_guard["i"][b] = em.cnt["vv"]
        ld_guard["f"][b] = em.cnt["vv"]
        elem_vv = em.cnt["vv"]

        em.wait("tensor", "vv", elem_vv)
        for h in range(16):
            f = lambda E, h=h: E.matmul(
                P["psR"][0:64, 100:116], LROW[0:1, h * 64:(h + 1) * 64],
                EYE16R[0:1, h * 16:(h + 1) * 16], start=(h == 0), stop=(h == 15))
            if h < 15:
                em.op("tensor", f)
            else:
                tv = em.op("tensor", f, inc="tv")
        em.wait("vector", "tv", tv)
        vv = em.op("vector", lambda E: E.tensor_copy(NLE[0:64, :],
                                                     P["psR"][0:64, 100:116]),
                   inc="vv")
        em.wait("scalar", "vv", vv)
        svp = em.op("scalar", lambda E: E.activation(PF16[0:64, :], NLE[0:64, :],
                                                     AF.Exp, scale=-1.0), inc="sv")

        em.wait("tensor", "vv", em.cnt["vv"])
        for nmT, src_, dstF in ((0, QA, QAF), (2, KT, KTF)):
            for rnd in range(4):
                pb = rnd % 2
                if psPTb_guard:
                    em.wait("tensor", "vv", psPTb_guard)
                if psT_guard:
                    em.wait("tensor", "vv", psT_guard)
                for hh in range(4):
                    h = rnd * 4 + hh
                    f = lambda E, h=h, hh=hh, src_=src_, pb=pb: E.transpose(
                        P["psT"][0:64, pb * 512 + hh * 128: pb * 512 + (hh + 1) * 128],
                        src_[:, h * 64:(h + 1) * 64], P["identb"][:])
                    if hh < 3:
                        em.op("tensor", f)
                    else:
                        tv = em.op("tensor", f, inc="tv")
                em.wait("vector", "tv", tv)
                psPTb_guard = em.op("vector", lambda E, rnd=rnd, dstF=dstF, pb=pb:
                    E.tensor_copy(dstF[0:64, rnd * 512:(rnd + 1) * 512],
                                  P["psT"][0:64, pb * 512:(pb + 1) * 512]), inc="vv")
        for nmT, src_, dstF in ((1, QI, QIF),):
            for rnd in range(4):
                pb = rnd % 2
                if psPT_guard:
                    em.wait("tensor", "vv", psPT_guard)
                for hh in range(4):
                    h = rnd * 4 + hh
                    f = lambda E, h=h, hh=hh, src_=src_, pb=pb: E.transpose(
                        ps[2 + pb][0:64, hh * 128:(hh + 1) * 128],
                        src_[:, h * 64:(h + 1) * 64], P["identf"][:])
                    if hh < 3:
                        em.op("tensor", f)
                    else:
                        tv = em.op("tensor", f, inc="tv")
                em.wait("vector", "tv", tv)
                psPT_guard = em.op("vector", lambda E, rnd=rnd, dstF=dstF, pb=pb:
                    E.tensor_copy(dstF[0:64, rnd * 512:(rnd + 1) * 512],
                                  ps[2 + pb][0:64, 0:512]), inc="vv")
        for rnd in range(2):
            pb = rnd % 2
            if psPTb_guard:
                em.wait("tensor", "vv", psPTb_guard)
            for hh in range(8):
                h = rnd * 8 + hh
                f = lambda E, h=h, hh=hh, pb=pb: E.transpose(
                    P["psT"][0:64, pb * 512 + hh * 64: pb * 512 + (hh + 1) * 64],
                    K1[0:64, h * 64:(h + 1) * 64], P["identb"][0:64, 0:64])
                if hh < 7:
                    em.op("tensor", f)
                else:
                    tv = em.op("tensor", f, inc="tv")
            em.wait("vector", "tv", tv)
            psPTb_guard = em.op("vector", lambda E, rnd=rnd, pb=pb:
                E.tensor_copy(K1F[0:64, rnd * 512:(rnd + 1) * 512],
                              P["psT"][0:64, pb * 512:(pb + 1) * 512]), inc="vv")
        tr_vv = em.cnt["vv"]

        em.wait("tensor", "vv", tr_vv)
        at_guard = [0, 0]
        for h in range(16):
            pb = h % 2
            if at_guard[pb]:
                em.wait("tensor", "vv", at_guard[pb])
            em.op("tensor", lambda E, h=h, pb=pb: E.matmul(
                ps[4 + pb][0:64, 0:64],
                K1F[0:64, h * 64:(h + 1) * 64],
                QAF[0:64, h * 128: h * 128 + 64], start=True, stop=True))
            tva = em.op("tensor", lambda E, h=h, pb=pb: E.matmul(
                ps[4 + pb][:, 64:128],
                KTF[0:64, h * 128:(h + 1) * 128],
                QAF[0:64, h * 128 + 64: h * 128 + 128], start=True, stop=True),
                inc="tv")
            em.wait("vector", "tv", tva)
            at_guard[pb] = em.op("vector", lambda E, h=h, pb=pb: E.tensor_mul(
                ASB_ALL[:, h * 128:(h + 1) * 128], ps[4 + pb][:, 0:128],
                P["scanmask_sb"][:]), inc="vv")
        asb_vv = em.cnt["vv"]
        em.wait("tensor", "vv", asb_vv)
        hv_guard4 = [0, 0, 0, 0]
        for h in range(16):
            pb = h % 4
            if hv_guard4[pb]:
                em.wait("tensor", "vv", hv_guard4[pb])
            em.op("tensor", lambda E, h=h, pb=pb: E.matmul(
                ps[pb][:, 0:65], QIF[0:64, h * 128:(h + 1) * 128],
                CAU[0:64, h * 65:(h + 1) * 65], start=True, stop=False))
            em.op("tensor", lambda E, h=h, pb=pb, b=b: E.matmul(
                ps[pb][:, 0:64], ASB_ALL[:, h * 128:(h + 1) * 128],
                LV[b][:, h * 64:(h + 1) * 64], start=False, stop=False))
            tvh = em.op("tensor", lambda E, h=h, pb=pb: E.matmul(
                ps[pb][:, 64:65], ASB_ALL[:, h * 128:(h + 1) * 128],
                P["onescb"][:, 0:1], start=False, stop=True), inc="tv")
            em.wait("vector", "tv", tvh)
            hv_guard4[pb] = em.op("vector", lambda E, h=h, pb=pb: E.tensor_copy(
                HVS[:, h * 65:(h + 1) * 65], ps[pb][:, 0:65]), inc="vv")
        em.drain("vector")
        hv3 = HVS[:].rearrange("p (h c) -> p h c", c=65)
        em.op("vector", lambda E: E.tensor_scalar_max(
            DEN16[:, 0:16], hv3[:, :, 64:65].squeeze(2), 1.0))
        em.drain("vector")
        em.op("vector", lambda E: E.reciprocal(DEN16[:, 16:32], DEN16[:, 0:16]))
        em.drain("vector")
        em.op("vector", lambda E: E.tensor_copy(
            RSE[:].rearrange("p (h d) -> p h d", d=64),
            DEN16[:, 16:32].unsqueeze(2).to_broadcast([128, 16, 64])))
        em.drain("vector")
        prev_rse = em.op("vector", lambda E: E.tensor_mul(
            HT[:].rearrange("p (h d) -> p h d", d=64),
            hv3[:, :, 0:64], RSE[:].rearrange("p (h d) -> p h d", d=64)), inc="vv")
        em.drain("vector")
        em.wait("vector", "sv", svp)
        def state_slot(h):
            if h < 7:
                return P["psR"], h * 65
            if h < 14:
                return ps[4], (h - 7) * 65
            return ps[5], (h - 14) * 65
        em.wait("tensor", "vv", em.cnt["vv"])
        for h in range(16):
            bank, off = state_slot(h)
            em.op("tensor", lambda E, h=h, bank=bank, off=off, b=b: E.matmul(
                bank[0:64, off:off + 64], KB[:, h * 64:(h + 1) * 64],
                LV[b][:, h * 64:(h + 1) * 64], start=True, stop=True))
            f = lambda E, h=h, bank=bank, off=off: E.matmul(
                bank[0:64, off + 64:off + 65], KB[:, h * 64:(h + 1) * 64],
                P["onescb"][:, 0:1], start=True, stop=True)
            if h < 15:
                em.op("tensor", f)
            else:
                tvs = em.op("tensor", f, inc="tv")
        em.wait("vector", "tv", tvs)
        em.op("vector", lambda E: E.tensor_copy(
            PFE[0:64, :].rearrange("p (h d) -> p h d", d=65),
            PF16[0:64, 0:16].unsqueeze(2).to_broadcast([64, 16, 65])))
        em.drain("vector")
        em.op("vector", lambda E: E.tensor_mul(CAU[0:64, :], CAU[0:64, :],
                                               PFE[0:64, 0:1040]))
        em.drain("vector")
        em.op("vector", lambda E: E.tensor_add(CAU[0:64, 0:455], CAU[0:64, 0:455],
                                               P["psR"][0:64, 0:455]))
        em.op("vector", lambda E: E.tensor_add(CAU[0:64, 455:910],
                                               CAU[0:64, 455:910], ps[4][0:64, 0:455]))
        st_vv = em.op("vector", lambda E: E.tensor_add(
            CAU[0:64, 910:1040], CAU[0:64, 910:1040], ps[5][0:64, 0:130]),
            inc="vv")
        em.drain("vector")
        cau_vv = em.cnt["vv"]
        prev_rse_rel = prev_rse
        ld_guard["v"][b] = cau_vv
        hcol_vv = em.cnt["vv"]

        em.wait("vector", f"Lg{b}", sdv["g"])
        vv = em.op("vector", lambda E, b=b: E.tensor_mul(HG[:], HT[:], LG[b][:]),
                   inc="vv")
        ld_guard["g"][b] = vv
        em.wait("tensor", "vv", vv)
        if psT_guard:
            em.wait("tensor", "vv", psT_guard)
        if psPTb_guard:
            em.wait("tensor", "vv", psPTb_guard)
        for c in range(8):
            f = lambda E, c=c: E.transpose(
                P["psT"][:, c * 128:(c + 1) * 128], HG[:, c * 128:(c + 1) * 128],
                P["identb"][:])
            if c < 7:
                em.op("tensor", f)
            else:
                tv = em.op("tensor", f, inc="tv")
        em.wait("vector", "tv", tv)
        vv = None
        for c in range(8):
            f = lambda E, c=c: E.tensor_copy(HGF[:, c * 128:(c + 1) * 128],
                                             P["psT"][:, c * 128:(c + 1) * 128])
            if c < 7:
                em.op("vector", f)
            else:
                vv = em.op("vector", f, inc="vv")
        psT_guard = vv
        em.wait("tensor", "vv", vv)
        for half in range(2):
            for fc in range(8):
                f = lambda E, fc=fc, half=half: E.matmul(
                    ps[half][:], HGF[:, fc * 128:(fc + 1) * 128],
                    OUTPC[:, fc * 1024 + half * 512: fc * 1024 + (half + 1) * 512],
                    start=(fc == 0), stop=(fc == 7))
                if fc < 7:
                    em.op("tensor", f)
                else:
                    tv = em.op("tensor", f, inc="tv")
        em.wait("vector", "tv", tv)
        em.wait("vector", f"Lx{b}", sdv["x"])
        if xo_guard2:
            em.wait("vector", "xoC0", xo_guard2)
        em.op("vector", lambda E, b=b: E.tensor_add(XO2[0][:, 0:512], ps[0][:],
                                                    XT2[b][:, 0:512]))
        vv = em.op("vector", lambda E, b=b: E.tensor_add(
            XO2[0][:, 512:1024], ps[1][:], XT2[b][:, 512:1024]), inc="vv")
        ld_guard["x"][b] = vv
        ps01_rel = vv
        em.wait("gpsimd", "vv", vv)
        xo_guard2 = em.op("gpsimd", lambda E, t=t: E.dma_start(
            out=dd["xr3"][t * 128:(t + 1) * 128, :], in_=XO2[0][:]),
            inc="xoC0", k=16)
    A.reset()


def emit_D(em, nc, P, din, dd, A):
    em.join()
    ps = P["ps"]
    RG = [list(range(NCORE))]

    SLOT2 = A.f32(32)
    GK = A.f32(32)
    ESTAR = A.f32(32)
    POSL = A.f32(32)
    G1 = A.f32(32)
    MYMAPT = A.f32(48)
    IDX32 = P["idx32"]
    mark = A.mark()

    XT3 = [A.f32(1024), A.f32(1024)]
    XNB = [A.bf16(1024)]
    XNFW = A.f32(1024)
    BCS = XNFW[:, 0:512]
    BN = A.f32(12)
    STAT = A.f32(8)
    DEAD = A.f32(4)
    XNFSg = A.bf16(1024)
    WGC = A.bf16(64)
    G8 = A.f32(8)
    OH = A.f32(8)
    T1 = A.f32(8)
    T2 = A.f32(8)
    CNTE = A.f32(8)
    BASE_B = A.f32(8)
    TOTR = A.f32(8)
    IOT8 = A.f32(8)
    SVALS = A.f32(32)
    GCOL = A.f32(4)
    CNTS_SB = A.f32(8)
    BASEG = A.f32(8)
    BASER = A.f32(8)
    BASEB128 = A.f32(8)
    LT8S = A.f32(8)
    OH8S = A.f32(1)
    SELS = A.f32(144)
    MAPT = A.f32(384)
    MYMAP = A.f32(128)
    ZR = A.f32(336)
    bufs = dict(bn=BN, stat=STAT, xnf=XNFW, dead=DEAD)

    wgv = em.op("gpsimd", lambda E: E.dma_start(out=WGC[:], in_=din["wg_c"]),
                inc="wD", k=16)
    sv0 = em.op("sync", lambda E: E.dma_start(out=IOT8[:], in_=dd["iota8"]),
                inc="bD", k=16)
    sv0 = em.op("sync", lambda E: E.dma_start(out=SVALS[:], in_=din["svals"]),
                inc="bD", k=16)
    sv0 = em.op("sync", lambda E: E.dma_start(out=LT8S[0:8, :], in_=dd["lt8sT"]),
                inc="bD", k=16)
    sv0 = em.op("sync", lambda E: E.dma_start(out=OH8S[0:8, :], in_=din["oh8"]),
                inc="bD", k=16)
    sv0 = em.op("sync", lambda E: E.dma_start(out=SELS[0:128, 0:48],
                in_=din["sel_m"][0:128, :]), inc="bD", k=16)
    sv0 = em.op("sync", lambda E: E.dma_start(out=SELS[0:128, 48:96],
                in_=din["sel_m"][128:256, :]), inc="bD", k=16)
    sv0 = em.op("sync", lambda E: E.dma_start(out=SELS[0:72, 96:144],
                in_=din["sel_m"][256:328, :]), inc="bD", k=16)
    em.wait("vector", "bD", sv0)
    em.wait("tensor", "wD", wgv)
    em.wait("tensor", "bD", sv0)
    em.op("vector", lambda E: E.memset(BASE_B[:], 0.0))
    em.op("vector", lambda E: E.memset(ZR[:], 0.0))
    em.drain("vector")
    zr_vv = em.op("vector", lambda E: E.nop() if False else E.memset(DEAD[:], 0.0),
                  inc="vv")
    em.wait("gpsimd", "vv", zr_vv)
    zrg = em.op("gpsimd", lambda E: E.dma_start(
        out=dd["smap_my"].rearrange("(p f) x -> p (f x)", p=128),
        in_=ZR[:, 0:329]), inc="zD", k=16)

    xt_guard = [0, 0]
    xnb_guard = [0]
    psT_guard = 0
    xnfs_guard = 0
    xn_wr = 0
    for t in range(NT):
        b = t % 2
        em.wait("sync", "vv", xt_guard[b])
        sd = em.op("sync", lambda E, t=t, b=b: E.dma_start(
            out=XT3[b][:], in_=dd["xr3"][t * 128:(t + 1) * 128, :]),
            inc=f"xD{b}", k=16)
        em.wait("vector", f"xD{b}", sd)
        em.wait("vector", "tv", xnb_guard[0])
        if xn_wr:
            em.wait("vector", "xnS", xn_wr)
        ln_vv = _ln_tile(em, P, A, bufs, XT3[b][:], XNB[0][:], l2gamma=False)
        xt_guard[b] = em.cnt["vv"]
        em.wait("gpsimd", "vv", ln_vv)
        xn_wr = em.op("gpsimd", lambda E, t=t: E.dma_start(
            out=dd["xn_my"][t * 128:(t + 1) * 128, :], in_=XNB[0][:]),
            inc="xnS", k=16)
        em.wait("tensor", "vv", ln_vv)
        if psT_guard:
            em.wait("tensor", "vv", psT_guard)
        for c in range(8):
            f = lambda E, c=c: E.transpose(
                P["psT"][:, c * 128:(c + 1) * 128],
                XNB[0][:, c * 128:(c + 1) * 128], P["identb"][:])
            if c < 7:
                em.op("tensor", f)
            else:
                tv = em.op("tensor", f, inc="tv")
        em.wait("vector", "tv", tv)
        em.wait("vector", "tv", xnfs_guard)
        for c in range(8):
            f = lambda E, c=c: E.tensor_copy(
                XNFSg[:, c * 128:(c + 1) * 128], P["psT"][:, c * 128:(c + 1) * 128])
            if c < 7:
                em.op("vector", f)
            else:
                vv = em.op("vector", f, inc="vv")
        psT_guard = vv
        xnb_guard[0] = tv
        em.wait("tensor", "vv", vv)
        for kc in range(8):
            f = lambda E, kc=kc: E.matmul(
                ps[4][:, 0:8], XNFSg[:, kc * 128:(kc + 1) * 128],
                WGC[:, kc * 8:(kc + 1) * 8], start=(kc == 0), stop=(kc == 7))
            if kc < 7:
                em.op("tensor", f)
            else:
                tv = em.op("tensor", f, inc="tv")
        xnfs_guard = tv
        em.wait("vector", "tv", tv)
        vv = em.op("vector", lambda E: E.tensor_copy(G8[:], ps[4][:, 0:8]),
                   inc="vv")
        em.drain("vector")
        em.op("vector", lambda E: E.reduce_max(GCOL[:, 0:1], G8[:], axis=AX.X))
        em.drain("vector")
        em.op("vector", lambda E: E.tensor_single_scalar(OH[:], G8[:],
                                                         GCOL[:, 0:1], ALU.is_equal))
        em.op("vector", lambda E: E.tensor_scalar_mul(GCOL[:, 1:2], GCOL[:, 0:1],
                                                      -1.0))
        em.drain("vector")
        em.op("vector", lambda E: E.tensor_mul(T1[:], OH[:], IOT8[:]))
        em.op("vector", lambda E: E.tensor_scalar(T2[:], OH[:], -8.0, 8.0,
                                                  ALU.mult, ALU.add))
        em.drain("vector")
        em.op("vector", lambda E: E.tensor_add(T1[:], T1[:], T2[:]))
        em.drain("vector")
        em.op("vector", lambda E, t=t: E.tensor_reduce(ESTAR[:, t:t + 1], T1[:],
                                                       AX.X, ALU.min))
        em.drain("vector")
        vv = em.op("vector", lambda E, t=t: E.tensor_single_scalar(
            OH[:], IOT8[:], ESTAR[:, t:t + 1], ALU.is_equal), inc="vv")
        em.wait("scalar", "vv", vv)
        sv = em.op("scalar", lambda E: E.activation(T2[:], G8[:], AF.Exp,
                   bias=GCOL[:, 1:2], accum_out=GCOL[:, 2:3]), inc="sv")
        em.wait("vector", "sv", sv)
        em.op("vector", lambda E, t=t: E.reciprocal(G1[:, t:t + 1], GCOL[:, 2:3]))
        em.drain("vector")
        em.wait("tensor", "vv", em.cnt["vv"])
        tv = em.op("tensor", lambda E: E.matmul(ps[5][:, 0:8], P["ltincT"][:],
                   OH[:], start=True, stop=True), inc="tv")
        tv2 = em.op("tensor", lambda E: E.matmul(P["psR"][0:1, 0:8],
                    P["onesc"][:, 0:1], OH[:], start=True, stop=True), inc="tv")
        em.wait("vector", "tv", tv2)
        em.op("vector", lambda E: E.tensor_copy(CNTE[:], ps[5][:, 0:8]))
        em.op("vector", lambda E: E.tensor_copy(TOTR[0:1, :], P["psR"][0:1, 0:8]))
        em.drain("vector")
        em.op("vector", lambda E: E.tensor_add(T2[:], CNTE[:], BASE_B[:]))
        em.drain("vector")
        em.op("vector", lambda E: E.tensor_mul(T2[:], T2[:], OH[:]))
        em.drain("vector")
        vv = em.op("vector", lambda E, t=t: E.tensor_reduce(POSL[:, t:t + 1], T2[:],
                                                            AX.X, ALU.add), inc="vv")
        em.wait("tensor", "vv", vv)
        tv = em.op("tensor", lambda E: E.matmul(P["psR"][:, 8:16], P["onesr"][0:1, :],
                   TOTR[0:1, :], start=True, stop=True), inc="tv")
        em.wait("vector", "tv", tv)
        vv = em.op("vector", lambda E: E.tensor_add(BASE_B[:], BASE_B[:],
                   P["psR"][:, 8:16]), inc="vv")
        em.drain("vector")
    gat_vv = em.cnt["vv"]

    em.wait("gpsimd", "vv", gat_vv)
    cg = em.op("gpsimd", lambda E: E.dma_start(out=dd["cnts_in"],
               in_=BASE_B[0:1, 0:8]), inc="cD", k=16)
    em.wait("gpsimd", "cD", cg)
    cg = em.op("gpsimd", lambda E: E.collective_compute(
        "AllGather", ALU.bypass, RG, ins=[dd["cnts_in"]], outs=[dd["cnts_out"]]),
        inc="gv")
    em.wait("gpsimd", "gv", cg)
    cg = em.op("gpsimd", lambda E: E.dma_start(out=CNTS_SB[0:8, 0:8],
               in_=dd["cnts_out"]), inc="cD", k=16)
    em.wait("gpsimd", "xnS", em.cnt["xnS"])
    xag = em.op("gpsimd", lambda E: E.collective_compute(
        "AllGather", ALU.bypass, RG, ins=[dd["xn_my"]], outs=[dd["xn_all"]]),
        inc="gv")
    em.wait("tensor", "cD", cg)
    tv = em.op("tensor", lambda E: E.matmul(ps[4][0:8, 0:8], LT8S[0:8, 0:8],
               CNTS_SB[0:8, 0:8], start=True, stop=True), inc="tv")
    em.wait("vector", "tv", tv)
    vv = em.op("vector", lambda E: E.tensor_copy(BASEG[0:8, :], ps[4][0:8, 0:8]),
               inc="vv")
    em.wait("tensor", "vv", vv)
    tv = em.op("tensor", lambda E: E.matmul(ps[4][0:1, 0:8], OH8S[0:8, 0:1],
               BASEG[0:8, 0:8], start=True, stop=True), inc="tv")
    em.wait("vector", "tv", tv)
    vv = em.op("vector", lambda E: E.tensor_copy(BASER[0:1, :], ps[4][0:1, 0:8]),
               inc="vv")
    em.wait("tensor", "vv", vv)
    tv = em.op("tensor", lambda E: E.matmul(ps[4][:, 0:8], P["onesr"][0:1, :],
               BASER[0:1, 0:8], start=True, stop=True), inc="tv")
    em.wait("vector", "tv", tv)
    vv = em.op("vector", lambda E: E.tensor_copy(BASEB128[:], ps[4][:, 0:8]),
               inc="vv")

    em.wait("gpsimd", "zD", zrg)
    sc_wr = 0
    for t in range(NT):
        em.op("vector", lambda E, t=t: E.tensor_single_scalar(
            OH[:], IOT8[:], ESTAR[:, t:t + 1], ALU.is_equal))
        em.drain("vector")
        em.op("vector", lambda E: E.tensor_mul(T2[:], OH[:], BASEB128[:]))
        em.drain("vector")
        em.op("vector", lambda E: E.tensor_reduce(GCOL[:, 0:1], T2[:], AX.X,
                                                  ALU.add))
        em.drain("vector")
        em.op("vector", lambda E, t=t: E.tensor_add(GCOL[:, 1:2], GCOL[:, 0:1],
                                                    POSL[:, t:t + 1]))
        em.drain("vector")
        em.op("vector", lambda E: E.tensor_scalar_add(GCOL[:, 1:2], GCOL[:, 1:2],
                                                      -1.0))
        em.drain("vector")
        em.op("vector", lambda E: E.tensor_single_scalar(GCOL[:, 2:3],
              GCOL[:, 1:2], float(CAP), ALU.is_lt))
        em.drain("vector")
        em.op("vector", lambda E, t=t: E.tensor_mul(GK[:, t:t + 1], GCOL[:, 2:3],
                                                    G1[:, t:t + 1]))
        em.op("vector", lambda E, t=t: E.tensor_scalar(
            SLOT2[:, t:t + 1], ESTAR[:, t:t + 1], float(ESTRIDE), GCOL[:, 1:2],
            ALU.mult, ALU.add))
        em.drain("vector")
        em.op("vector", lambda E, t=t: E.tensor_mul(SLOT2[:, t:t + 1],
              SLOT2[:, t:t + 1], GCOL[:, 2:3]))
        em.op("vector", lambda E: E.tensor_scalar(GCOL[:, 3:4], GCOL[:, 2:3],
              -float(NSLOT), float(NSLOT), ALU.mult, ALU.add))
        em.drain("vector")
        em.op("vector", lambda E, t=t: E.tensor_add(GCOL[:, 3:4],
              SLOT2[:, t:t + 1], GCOL[:, 3:4]))
        em.drain("vector")
        if sc_wr:
            em.wait("vector", "scD", sc_wr)
        vv = em.op("vector", lambda E, t=t: E.tensor_copy(P["idx32"][:, 0:1],
                   GCOL[:, 3:4]), inc="vv")
        em.wait("gpsimd", "vv", vv)
        sc_wr = em.op("gpsimd", lambda E, t=t: E.indirect_dma_start(
            out=dd["smap_my"], out_offset=IndirectOffsetOnAxis(P["idx32"][:, 0:1], 0),
            in_=SVALS[:, t:t + 1], in_offset=None), inc="scD", k=16)
    em.wait("gpsimd", "scD", em.cnt["scD"])
    arv = em.op("gpsimd", lambda E: E.collective_compute(
        "AllReduce", ALU.add, RG, ins=[dd["smap_my"]], outs=[dd["smap_red"]]),
        inc="gv")
    em.wait("gpsimd", "gv", arv)
    mg = em.op("gpsimd", lambda E: E.dma_start(out=MAPT[:, 0:128],
               in_=dd["smap_red"].rearrange("(f p) x -> f (p x)", p=128)[0:128, :]),
               inc="cD", k=16)
    mg = em.op("gpsimd", lambda E: E.dma_start(out=MAPT[:, 128:256],
               in_=dd["smap_red"].rearrange("(f p) x -> f (p x)", p=128)[128:256, :]),
               inc="cD", k=16)
    mg = em.op("gpsimd", lambda E: E.dma_start(out=MAPT[0:72, 256:384],
               in_=dd["smap_red"].rearrange("(f p) x -> f (p x)", p=128)[256:328, :]),
               inc="cD", k=16)
    em.wait("tensor", "cD", mg)
    em.op("tensor", lambda E: E.matmul(ps[4][0:48, 0:128], SELS[0:128, 0:48],
          MAPT[0:128, 0:128], start=True, stop=False))
    em.op("tensor", lambda E: E.matmul(ps[4][0:48, 0:128], SELS[0:128, 48:96],
          MAPT[0:128, 128:256], start=False, stop=False))
    tv = em.op("tensor", lambda E: E.matmul(ps[4][0:48, 0:128], SELS[0:72, 96:144],
               MAPT[0:72, 256:384], start=False, stop=True), inc="tv")
    em.wait("vector", "tv", tv)
    vv = em.op("vector", lambda E: E.tensor_copy(MYMAP[0:48, :], ps[4][0:48, 0:128]),
               inc="vv")
    em.wait("tensor", "vv", vv)
    tv = em.op("tensor", lambda E: E.transpose(P["psR"][:, 0:48],
               MYMAP[0:48, 0:128], P["identf"][0:48, 0:48]), inc="tv")
    em.wait("vector", "tv", tv)
    mym_vv = em.op("vector", lambda E: E.tensor_copy(MYMAPT[:, 0:48],
                   P["psR"][:, 0:48]), inc="vv")

    em.join()
    A.reset(mark)
    MW1 = A.bf16(32768)
    MW2 = A.bf16(32768)
    HFM = A.bf16(16384)
    DGA = A.bf16(4096)
    DFM = A.bf16(4096)
    MB1S = A.f32(32)
    MB2B = A.bf16(1024)
    EOB = [A.f32(1024)]
    IDXD = A.f32(8)
    w1v = em.op("gpsimd", lambda E: E.dma_start(out=MW1[:], in_=din["mw1_c"]),
                inc="wD2", k=16)
    w2v = em.op("gpsimd", lambda E: E.dma_start(out=MW2[:], in_=din["mw2_c"]),
                inc="wD2", k=16)
    b1v = em.op("sync", lambda E: E.dma_start(out=MB1S[:], in_=din["mb1_t"]),
                inc="bD2", k=16)
    b2v = em.op("gpsimd", lambda E: E.dma_start(
        out=MB2B[:], in_=din["mb2_r"].to_broadcast([128, D])), inc="wD2", k=16)
    em.wait("tensor", "wD2", em.cnt["wD2"])
    em.wait("scalar", "bD2", b1v)
    em.wait("vector", "wD2", em.cnt["wD2"])
    em.op("vector", lambda E: E.memset(DGA[:], 0.0))
    em.drain("vector")
    em.wait("gpsimd", "gv", xag)
    nslab = ESTRIDE // 512 + 1
    dga_guard = 0
    gD_wr = 0
    psH_guard = [0, 0, 0, 0]
    psO_guard = [0, 0]
    hfm_guard = 0
    dfm_guard = 0
    eo_guard = 0
    for s in range(nslab):
        nsub = 4 if s < 10 else 1
        for sub in range(nsub):
            tcol = s * 4 + sub
            em.wait("vector", "vv", mym_vv)
            if dga_guard:
                em.wait("vector", "tv", dga_guard)
            em.op("vector", lambda E, tcol=tcol: E.tensor_scalar(
                IDXD[:, 0:1], MYMAPT[:, tcol:tcol + 1], -1.0, 0.0,
                ALU.add, ALU.max))
            em.drain("vector")
            if gD_wr:
                em.wait("vector", "gD", gD_wr)
            vv = em.op("vector", lambda E: E.tensor_copy(P["idx32"][:, 1:2],
                       IDXD[:, 0:1]), inc="vv")
            em.drain("vector")
            em.wait("gpsimd", "vv", vv)
            gD_wr = em.op("gpsimd", lambda E, sub=sub: E.indirect_dma_start(
                out=DGA[:, sub * 1024:(sub + 1) * 1024], out_offset=None,
                in_=dd["xn_all"],
                in_offset=IndirectOffsetOnAxis(P["idx32"][:, 1:2], 0)),
                inc="gD", k=16)
        em.wait("tensor", "gD", gD_wr)
        for sub in range(nsub):
            if psT_guard:
                em.wait("tensor", "vv", psT_guard)
            for c in range(8):
                f = lambda E, c=c, sub=sub: E.transpose(
                    P["psT"][:, c * 128:(c + 1) * 128],
                    DGA[:, sub * 1024 + c * 128: sub * 1024 + (c + 1) * 128],
                    P["identb"][:])
                if c < 7:
                    em.op("tensor", f)
                else:
                    tv = em.op("tensor", f, inc="tv")
            em.wait("vector", "tv", tv)
            em.wait("vector", "tv", hfm_guard)
            for c in range(8):
                f = lambda E, c=c, sub=sub: E.tensor_copy(
                    DFM[:, c * 512 + sub * 128: c * 512 + (sub + 1) * 128],
                    P["psT"][:, c * 128:(c + 1) * 128])
                if c < 7:
                    em.op("vector", f)
                else:
                    vv = em.op("vector", f, inc="vv")
            psT_guard = vv
        dga_guard = em.cnt["tv"]
        dfm_vv = em.cnt["vv"]
        em.wait("tensor", "vv", dfm_vv)
        em.wait("scalar", "tv", hfm_guard)
        for ic in range(32):
            pb = ic % 4
            if psH_guard[pb]:
                em.wait("tensor", "sv", psH_guard[pb])
            for kc in range(8):
                f = lambda E, kc=kc, ic=ic, pb=pb: E.matmul(
                    ps[pb][:], MW1[:, kc * 4096 + ic * 128: kc * 4096 + (ic + 1) * 128],
                    DFM[:, kc * 512:(kc + 1) * 512], start=(kc == 0), stop=(kc == 7))
                if kc < 7:
                    em.op("tensor", f)
                else:
                    tv = em.op("tensor", f, inc="tv")
            em.wait("scalar", "tv", tv)
            psH_guard[pb] = em.op("scalar", lambda E, ic=ic, pb=pb: E.activation(
                HFM[:, ic * 512:(ic + 1) * 512], ps[pb][:], AF.Relu,
                bias=MB1S[:, ic:ic + 1]), inc="sv")
        h_sv = em.cnt["sv"]
        dfm_guard = em.cnt["tv"]
        em.wait("tensor", "sv", h_sv)
        for sub in range(nsub):
            tcol = s * 4 + sub
            for half in range(2):
                if psO_guard[half]:
                    em.wait("tensor", "vv", psO_guard[half])
                for ic in range(32):
                    f = lambda E, ic=ic, sub=sub, half=half: E.matmul(
                        ps[4 + half][:],
                        HFM[:, ic * 512 + sub * 128: ic * 512 + (sub + 1) * 128],
                        MW2[:, ic * 1024 + half * 512: ic * 1024 + (half + 1) * 512],
                        start=(ic == 0), stop=(ic == 31))
                    if ic < 31:
                        em.op("tensor", f)
                    else:
                        tv = em.op("tensor", f, inc="tv")
                em.wait("vector", "tv", tv)
                if eo_guard:
                    em.wait("vector", "eoS", eo_guard)
                psO_guard[half] = em.op("vector", lambda E, half=half: E.tensor_add(
                    EOB[0][:, half * 512:(half + 1) * 512], ps[4 + half][:],
                    MB2B[:, half * 512:(half + 1) * 512]), inc="vv")
            em.wait("gpsimd", "vv", em.cnt["vv"])
            eo_guard = em.op("gpsimd", lambda E, tcol=tcol: E.dma_start(
                out=dd["eo_my"][tcol * 128:(tcol + 1) * 128, :], in_=EOB[0][:]),
                inc="eoS", k=16)
        hfm_guard = em.cnt["tv"]
    em.wait("gpsimd", "eoS", em.cnt["eoS"])
    eag = em.op("gpsimd", lambda E: E.collective_compute(
        "AllGather", ALU.bypass, RG, ins=[dd["eo_my"]], outs=[dd["eo_all"]]),
        inc="gv")

    em.join()
    A.reset(mark)
    FXT = [A.f32(1024), A.f32(1024)]
    FG = [A.bf16(1024), A.bf16(1024)]
    YO = [A.f32(1024), A.f32(1024)]
    em.wait("vector", "gv", eag)
    em.op("vector", lambda E: E.memset(FG[0][:], 0.0))
    vv0 = em.op("vector", lambda E: E.memset(FG[1][:], 0.0), inc="vv")
    fxt_guard = [0, 0]
    fg_guard = [0, 0]
    yo_guard = [0, 0]
    for t in range(NT):
        b = t % 2
        em.wait("sync", "vv", fxt_guard[b])
        sd = em.op("sync", lambda E, t=t, b=b: E.dma_start(
            out=FXT[b][:], in_=dd["xr3"][t * 128:(t + 1) * 128, :]),
            inc=f"fD{b}", k=16)
        em.wait("vector", "vv", vv0)
        if fg_guard[b]:
            em.wait("vector", f"gF{b}", fg_guard[b])
        vv = em.op("vector", lambda E, t=t, b=b: E.tensor_copy(
            P["idx32"][:, 2 + b:3 + b], SLOT2[:, t:t + 1]), inc="vv")
        em.wait("gpsimd", "vv", vv)
        fg_guard[b] = em.op("gpsimd", lambda E, b=b: E.indirect_dma_start(
            out=FG[b][:], out_offset=None,
            in_=dd["eo_all"],
            in_offset=IndirectOffsetOnAxis(P["idx32"][:, 2 + b:3 + b], 0)),
            inc=f"gF{b}", k=16)
        em.wait("vector", f"gF{b}", fg_guard[b])
        em.wait("vector", f"fD{b}", sd)
        if yo_guard[b]:
            em.wait("vector", f"yD{b}", yo_guard[b])
        vv = em.op("vector", lambda E, t=t, b=b: E.scalar_tensor_tensor(
            YO[b][:], FG[b][:], GK[:, t:t + 1], FXT[b][:], ALU.mult, ALU.add),
            inc="vv")
        fxt_guard[b] = vv
        em.wait("gpsimd", "vv", vv)
        yo_guard[b] = em.op("gpsimd", lambda E, t=t, b=b: E.dma_start(
            out=dd["y"][t * 128:(t + 1) * 128, :], in_=YO[b][:]),
            inc=f"yD{b}", k=16)
    A.reset()


_DEBUG_STREAM = None


def kernel(**inputs):
    maps = [host_prep(inputs, c) for c in range(NCORE)]
    nc = build_program(debug_stream=_DEBUG_STREAM)
    res = run_bass_kernel_spmd(nc, maps, core_ids=list(range(NCORE)))
    out = np.stack([np.asarray(res.results[c]["y"], dtype=np.float32)
                    for c in range(NCORE)], axis=0)
    return out


# version 08
# speedup vs baseline: 1.0388x; 1.0135x over previous
import numpy as np
import ml_dtypes
from contextlib import ExitStack

import concourse.bass as bass
import concourse.mybir as mybir
from concourse.bass import IndirectOffsetOnAxis
from concourse.bass_utils import run_bass_kernel_spmd

F32 = mybir.dt.float32
BF16 = mybir.dt.bfloat16
I32 = mybir.dt.int32
AF = mybir.ActivationFunctionType
ALU = mybir.AluOpType
AX = mybir.AxisListType
BF = ml_dtypes.bfloat16

NCORE = 8
SEQ = 4096
D = 1024
NT = SEQ // 128
NSLAB = SEQ // 512
SCALE_ATTN = float(128.0 ** -0.5)
ESTRIDE = 5248
CAP = 5120
NSLOT = 8 * ESTRIDE
MAPF = NSLOT // 128


def _chunks(w):
    K, F = w.shape
    nch = K // 128
    return np.ascontiguousarray(
        w.reshape(nch, 128, F).transpose(1, 0, 2).reshape(128, nch * F))


def _row(v):
    return np.ascontiguousarray(v.reshape(1, -1).astype(np.float32))


def host_prep(inp, core):
    f32 = lambda a: np.ascontiguousarray(np.asarray(a, dtype=np.float32))
    g = {}
    g["x"] = f32(inp["L0_attn_x"][core])
    g["gamma_r"] = _row(f32(inp["L0_attn_gamma"]))
    wqkvT = f32(inp["L0_attn_w_qkv"]).T
    w5 = wqkvT.reshape(8, 128, 3, 8, 128)
    g["wqkv_h"] = np.ascontiguousarray(
        w5.transpose(3, 1, 2, 0, 4).reshape(8 * 128, 3072))
    g["wout_c"] = _chunks(f32(inp["L0_attn_w_out"]).T)
    g["w1_c"] = _chunks(f32(inp["L0_ffn_W1"]))
    g["w2_c"] = _chunks(f32(inp["L0_ffn_W2"]))
    g["b1_t"] = np.ascontiguousarray(
        f32(inp["L0_ffn_b1"]).reshape(32, 128).T)
    g["b2_r"] = _row(f32(inp["L0_ffn_b2"]))
    g["inproj_c"] = _chunks(f32(inp["L1_attn_in_proj_w"]).T)
    for nm, key in (("qp", "L1_attn_q_proj_w"), ("kp", "L1_attn_k_proj_w"),
                    ("vp", "L1_attn_v_proj_w"), ("ip", "L1_attn_i_gate_w"),
                    ("fp", "L1_attn_f_gate_w"), ("op", "L1_attn_o_gate_w"),
                    ("outp", "L1_attn_out_proj_w")):
        g[nm + "_c"] = _chunks(f32(inp[key]).T)
    g["ib_r"] = _row(f32(inp["L1_attn_i_gate_b"]))
    g["fb_r"] = _row(f32(inp["L1_attn_f_gate_b"]))
    g["ob_r"] = _row(f32(inp["L1_attn_o_gate_b"]))
    g["qnw_r"] = _row(np.tile(f32(inp["L1_attn_q_norm_w"]), 16))
    g["knw_r"] = _row(np.tile(f32(inp["L1_attn_k_norm_w"]), 16))
    g["wg_c"] = _chunks(f32(inp["L1_ffn_wg"]))
    g["mw1_c"] = _chunks(f32(inp["L1_ffn_w1"][core]))
    g["mb1_t"] = np.ascontiguousarray(
        f32(inp["L1_ffn_b1"][core]).reshape(32, 128).T)
    g["mw2_c"] = _chunks(f32(inp["L1_ffn_w2"][core]))
    g["mb2_r"] = _row(f32(inp["L1_ffn_b2"][core]))
    g["svals"] = np.ascontiguousarray(
        (np.arange(SEQ, dtype=np.float32) + core * SEQ + 1.0)
        .reshape(NT, 128).T)
    oh8 = np.zeros((8, 1), np.float32); oh8[core, 0] = 1.0
    g["oh8"] = oh8
    sel = np.zeros((MAPF, 48), np.float32)
    for j in range(ESTRIDE // 128):
        sel[(core * ESTRIDE) // 128 + j, j] = 1.0
    g["sel_m"] = sel
    return g


INPUT_SPECS = [
    ("x", [SEQ, D]), ("gamma_r", [1, D]), ("wqkv_h", [8 * 128, 3072]),
    ("wout_c", [128, 8 * 1024]), ("w1_c", [128, 8 * 4096]),
    ("w2_c", [128, 32 * 1024]), ("b1_t", [128, 32]), ("b2_r", [1, D]),
    ("inproj_c", [128, 8 * 2048]), ("qp_c", [128, 8 * 1024]),
    ("kp_c", [128, 8 * 1024]), ("vp_c", [128, 8 * 1024]),
    ("ip_c", [128, 8 * 1024]), ("fp_c", [128, 8 * 1024]),
    ("op_c", [128, 8 * 1024]), ("outp_c", [128, 8 * 1024]),
    ("ib_r", [1, D]), ("fb_r", [1, D]), ("ob_r", [1, D]),
    ("qnw_r", [1, D]), ("knw_r", [1, D]), ("wg_c", [128, 64]),
    ("mw1_c", [128, 8 * 4096]), ("mb1_t", [128, 32]),
    ("mw2_c", [128, 32 * 1024]), ("mb2_r", [1, D]),
    ("svals", [128, NT]), ("oh8", [8, 1]), ("sel_m", [MAPF, 48]),
]


class Em:
    ENGS = ("sync", "gpsimd", "vector", "scalar", "tensor")

    def __init__(self, nc, ctx):
        self.nc = nc
        self.ctx = ctx
        self.q = {e: [] for e in self.ENGS}
        self.sem = {}
        self.cnt = {}
        self.dma_sems = set()
        for name in ("tv", "vv", "sv", "gv", "yv"):
            self.S(name)

    def S(self, name):
        if name not in self.sem:
            self.sem[name] = self.ctx.enter_context(self.nc.semaphore(name))
            self.cnt[name] = 0
        return self.sem[name]

    def emit(self, eng, fn):
        self.q[eng].append(fn)

    def wait(self, eng, sem, val):
        if val <= 0:
            return
        h = self.S(sem)
        self.emit(eng, lambda E: E.wait_ge(h, val))

    def bump(self, sem, k=1):
        self.S(sem)
        self.cnt[sem] += k
        return self.cnt[sem]

    def op(self, eng, fn, inc=None, k=1):
        if inc is None:
            self.emit(eng, lambda E: fn(E))
            return None
        h = self.S(inc)
        if k == 16:
            self.dma_sems.add(inc)
        v = self.bump(inc, k)
        self.emit(eng, lambda E: fn(E).then_inc(h, k))
        return v

    def drain(self, eng):
        self.emit(eng, lambda E: E.drain())

    def snap(self):
        return dict(self.cnt)

    def join(self, tag=None):
        sems = {"tensor": "jt", "vector": "jv", "scalar": "js",
                "gpsimd": "jg", "sync": "jy"}
        for name in sorted(self.dma_sems):
            self.wait("gpsimd", name, self.cnt[name])
        targets = {}
        for e in self.ENGS:
            targets[e] = self.op(e, lambda E: E.nop(), inc=sems[e])
        for e in self.ENGS:
            for o in self.ENGS:
                if o != e:
                    self.wait(e, sems[o], targets[o])


def build_program(debug_stream=None):
    nc = bass.Bass("TRN2", target_bir_lowering=False, num_devices=NCORE)
    ctx = ExitStack()

    dram_in = {}
    for name, shape in INPUT_SPECS:
        dram_in[name] = nc.dram_tensor(name, shape, F32, kind="ExternalInput").ap()
    y = nc.dram_tensor("y", [SEQ, D], F32, kind="ExternalOutput").ap()

    xr1 = nc.dram_tensor("xr1", [SEQ, D], F32, kind="Internal").ap()
    xr2 = nc.dram_tensor("xr2", [SEQ, D], F32, kind="Internal").ap()
    xr3 = nc.dram_tensor("xr3", [SEQ, D], F32, kind="Internal").ap()
    sp_q = nc.dram_tensor("sp_q", [SEQ, D], BF16, kind="Internal").ap()
    sp_k = nc.dram_tensor("sp_k", [SEQ, D], BF16, kind="Internal").ap()
    sp_v = nc.dram_tensor("sp_v", [SEQ, D], BF16, kind="Internal").ap()
    sp_i = nc.dram_tensor("sp_i", [SEQ, D], F32, kind="Internal").ap()
    sp_f = nc.dram_tensor("sp_f", [SEQ, D], F32, kind="Internal").ap()
    sp_g = nc.dram_tensor("sp_g", [SEQ, D], BF16, kind="Internal").ap()
    xn_my = nc.dram_tensor("xn_my", [SEQ, D], BF16, kind="Internal").ap()
    xn_all = nc.dram_tensor("xn_all", [NCORE * SEQ, D], BF16, kind="Internal").ap()
    cnts_in = nc.dram_tensor("cnts_in", [1, 8], F32, kind="Internal").ap()
    cnts_out = nc.dram_tensor("cnts_out", [8, 8], F32, kind="Internal").ap()
    smap_my = nc.dram_tensor("smap_my", [NSLOT + 128, 1], F32, kind="Internal").ap()
    smap_red = nc.dram_tensor("smap_red", [NSLOT + 128, 1], F32, kind="Internal").ap()
    eo_my = nc.dram_tensor("eo_my", [ESTRIDE, D], BF16, kind="Internal").ap()
    eo_all = nc.dram_tensor("eo_all", [NSLOT, D], BF16, kind="Internal").ap()

    ident_bf = nc.inline_tensor(np.eye(128, dtype=BF), name="ident_bf").ap()
    ident_f = nc.inline_tensor(np.eye(128, dtype=np.float32), name="ident_f").ap()
    ltincT = nc.inline_tensor(
        np.triu(np.ones((128, 128), np.float32)), name="ltincT").ap()
    lt8sT = nc.inline_tensor(
        np.triu(np.ones((8, 8), np.float32), 1), name="lt8sT").ap()
    ones_r = nc.inline_tensor(np.ones((1, 128), np.float32), name="ones_r").ap()
    ones_c = nc.inline_tensor(np.ones((128, 1), np.float32), name="ones_c").ap()
    ones_cb = nc.inline_tensor(np.ones((128, 1), dtype=BF), name="ones_cb").ap()
    ones_rb = nc.inline_tensor(np.ones((1, 128), dtype=BF), name="ones_rb").ap()
    iota8 = nc.inline_tensor(
        np.tile(np.arange(8, dtype=np.float32)[None, :], (128, 1)),
        name="iota8").ap()
    masks = []
    for j in range(4):
        m = (np.arange(512)[None, :] >= (128 * j + np.arange(128))[:, None])
        masks.append(nc.inline_tensor(m.astype(BF), name=f"mask{j}").ap())
    scanmask = nc.inline_tensor(
        np.tril(np.ones((128, 128), np.float32)).T.astype(np.float32),
        name="scanmask").ap()
    eye16r = nc.inline_tensor(
        np.eye(16, dtype=np.float32).reshape(1, 256), name="eye16r").ap()
    _c1 = np.full((128, 1), 0.0, np.float32); _c1[0] = 1.0; _c1[1:64] = 0.5
    _c2 = np.full((128, 1), 0.5, np.float32); _c2[0:64] = 1.0
    c1w = nc.inline_tensor(_c1, name="c1w").ap()
    c2w = nc.inline_tensor(_c2, name="c2w").ap()

    em = Em(nc, ctx)
    P = {}

    sb = lambda name, shape, dt=F32: ctx.enter_context(nc.sbuf_tensor(name, shape, dt))
    P["identb"] = sb("identb", [128, 128], BF16)
    P["identf"] = sb("identf", [128, 128], F32)
    P["ltincT"] = sb("s_ltincT", [128, 128], F32)
    P["onesr"] = sb("onesr", [1, 128], F32)
    P["onesc"] = sb("onesc", [128, 1], F32)
    P["onescb"] = sb("onescb", [128, 1], BF16)
    P["onesrb"] = sb("onesrb", [1, 128], BF16)
    P["gammab"] = sb("gammab", [128, D], F32)
    P["masks"] = sb("masksb", [128, 4 * 512], BF16)
    P["scanmask_sb"] = sb("scanmaskb", [128, 128], F32)
    P["c1w"] = sb("c1w_sb", [128, 1], F32)
    P["c2w"] = sb("c2w_sb", [128, 1], F32)
    P["consts"] = sb("constsb", [128, 8], F32)
    P["idx32"] = sb("idx32", [128, 4], I32)
    em_boot(em, nc, P, dram_in, dict(ident_bf=ident_bf, ident_f=ident_f,
            ltincT=ltincT, ones_r=ones_r, ones_c=ones_c, ones_cb=ones_cb,
            ones_rb=ones_rb, masks=masks, scanmask=scanmask, c1w=c1w, c2w=c2w))

    cvals = [1e-5, 1e-6, 1.0]
    P["cvals"] = cvals
    for i, v in enumerate(cvals):
        nc.const_aps.register(F32, v, P["consts"][:, i:i + 1])

    st_total = emit_all(em, nc, ctx, P, dram_in, dict(
        y=y, xr1=xr1, xr2=xr2, xr3=xr3, sp_q=sp_q, sp_k=sp_k, sp_v=sp_v,
        sp_i=sp_i, sp_f=sp_f, sp_g=sp_g, xn_my=xn_my, xn_all=xn_all,
        cnts_in=cnts_in, cnts_out=cnts_out, smap_my=smap_my, smap_red=smap_red,
        eo_my=eo_my, eo_all=eo_all,
        lt8sT=lt8sT, iota8=iota8, eye16r=eye16r),
        debug_stream=debug_stream)

    blk = ctx.enter_context(nc.Block())
    for eng_name, deco in (("sync", blk.sync), ("gpsimd", blk.gpsimd),
                           ("vector", blk.vector), ("scalar", blk.scalar),
                           ("tensor", blk.tensor)):
        thunks = em.q[eng_name]
        def body(E, thunks=thunks):
            for f in thunks:
                f(E)
        deco(body)
    ctx.close()
    return nc


class Arena:

    def __init__(self, nc, ctx, words):
        self.t = ctx.enter_context(nc.sbuf_tensor("ARENA", [128, words], F32))
        self.words = words
        self.off = 0

    def mark(self):
        return self.off

    def reset(self, off=0):
        self.off = off

    def al(self, nwords):
        a = self.off
        self.off += nwords
        assert self.off <= self.words, f"arena overflow {self.off}>{self.words}"
        return a

    def f32(self, n):
        a = self.al(n)
        return self.t[:, a:a + n]

    def bf16(self, n):
        a = self.al((n + 1) // 2)
        return self.t[:, a:a + (n + 1) // 2].bitcast(BF16)


def em_boot(em, nc, P, din, consts):
    def l(dst, src):
        em.op("sync", lambda E: E.dma_start(out=dst, in_=src), inc="bt", k=16)
    l(P["identb"][:], consts["ident_bf"])
    l(P["identf"][:], consts["ident_f"])
    l(P["ltincT"][:], consts["ltincT"])
    l(P["onesr"][:], consts["ones_r"])
    l(P["onesc"][:], consts["ones_c"])
    l(P["onescb"][:], consts["ones_cb"])
    l(P["onesrb"][:], consts["ones_rb"])
    for j in range(4):
        l(P["masks"][:, j * 512:(j + 1) * 512], consts["masks"][j])
    l(P["gammab"][:], din["gamma_r"].to_broadcast([128, D]))
    cinit = nc.inline_tensor(
        np.tile(np.array([[1e-5, 1e-6, 1.0, 0, 0, 0, 0, 0]], np.float32), (128, 1)),
        name="cinit").ap()
    l(P["consts"][:], cinit)
    l(P["scanmask_sb"][:], consts["scanmask"])
    l(P["c1w"][:], consts["c1w"])
    l(P["c2w"][:], consts["c2w"])
    P["boot_sd"] = em.cnt["bt"]


def emit_all(em, nc, ctx, P, din, dd, debug_stream=None):
    A = Arena(nc, ctx, nc.sbuf_bytes_remaining // 4 - 64)
    P["ps"] = [ctx.enter_context(nc.psum_tensor(f"ps{i}", [128, 512], F32))
               for i in range(6)]
    P["psR"] = ctx.enter_context(nc.psum_tensor("psR", [128, 512], F32))
    P["psT"] = ctx.enter_context(nc.psum_tensor("psT", [128, 1024], BF16))

    emit_A(em, nc, P, din, dd, A)
    if debug_stream != "xr1":
        emit_B(em, nc, P, din, dd, A)
    if debug_stream not in ("xr1", "xr2"):
        emit_C(em, nc, P, din, dd, A)
    if debug_stream not in ("xr1", "xr2", "xr3"):
        emit_D(em, nc, P, din, dd, A)

    if debug_stream is not None:
        src = dd[debug_stream]
        em.join()
        for i in range(4):
            em.op("gpsimd", lambda E, i=i: E.dma_start(
                out=dd["y"][i * 1024:(i + 1) * 1024, :],
                in_=src[i * 1024:(i + 1) * 1024, :]), inc="yout", k=16)
        em.wait("gpsimd", "yout", em.cnt["yout"])
    em.join()


def _ln_tile(em, P, A, bufs, xt_ap, out_bf_ap, l2gamma):
    bn, stat, xnf = bufs["bn"], bufs["stat"], bufs["xnf"]
    em.op("vector", lambda E: E.bn_stats(bn[:, 0:6], xt_ap[:, 0:512]))
    em.op("vector", lambda E: E.bn_stats(bn[:, 6:12], xt_ap[:, 512:1024]))
    em.drain("vector")
    vv = em.op("vector", lambda E: E.bn_aggr(stat[:, 0:2], bn[:, 0:12]), inc="vv")
    em.wait("scalar", "vv", vv)
    em.op("scalar", lambda E: E.activation(stat[:, 2:3], stat[:, 1:2], AF.Ln,
                                           bias=1e-5))
    em.drain("scalar")
    sv = em.op("scalar", lambda E: E.activation(stat[:, 3:4], stat[:, 2:3],
                                                AF.Exp, scale=-0.5), inc="sv")
    em.wait("vector", "sv", sv)
    em.op("vector", lambda E: E.scalar_tensor_tensor(
        stat[:, 4:5], stat[:, 0:1], -1.0, stat[:, 3:4], ALU.mult, ALU.mult))
    em.drain("vector")
    vv = em.op("vector", lambda E: E.tensor_scalar(
        xnf[:], xt_ap, stat[:, 3:4], stat[:, 4:5], ALU.mult, ALU.add), inc="vv")
    if not l2gamma:
        em.drain("vector")
        vv = em.op("vector", lambda E: E.tensor_copy(out_bf_ap, xnf[:]), inc="vv")
        return vv
    em.wait("scalar", "vv", vv)
    em.op("scalar", lambda E: E.activation(bufs["dead"], xnf[:], AF.Square,
                                           accum_out=stat[:, 5:6]))
    em.drain("scalar")
    em.op("scalar", lambda E: E.activation(stat[:, 6:7], stat[:, 5:6], AF.Ln,
                                           bias=1e-6))
    em.drain("scalar")
    sv = em.op("scalar", lambda E: E.activation(stat[:, 7:8], stat[:, 6:7],
                                                AF.Exp, scale=-0.5), inc="sv")
    em.wait("vector", "sv", sv)
    vv = em.op("vector", lambda E: E.scalar_tensor_tensor(
        out_bf_ap, xnf[:], stat[:, 7:8], P["gammab"][:], ALU.mult, ALU.mult),
        inc="vv")
    return vv


def _transpose_tile(em, P, xnb_ap, dst_cols, guard):
    if guard[1] > 0:
        em.wait("tensor", guard[0], guard[1])
    for c in range(8):
        f = lambda E, c=c: E.transpose(
            P["psT"][:, c * 128:(c + 1) * 128],
            xnb_ap[:, c * 128:(c + 1) * 128],
            P["identb"][:])
        if c < 7:
            em.op("tensor", f)
        else:
            tv = em.op("tensor", f, inc="tv")
    em.wait("vector", "tv", tv)
    for c in range(8):
        f = lambda E, c=c: E.tensor_copy(dst_cols(c), P["psT"][:, c * 128:(c + 1) * 128])
        if c < 7:
            em.op("vector", f)
        else:
            vv = em.op("vector", f, inc="vv")
    return tv, vv


def emit_A(em, nc, P, din, dd, A):
    ps = P["ps"]
    XNF = A.bf16(8 * 4096)
    OAL = A.bf16(8 * 4096)
    QF = A.bf16(4096)
    KF = A.bf16(4096)
    VT = A.bf16(4096)
    WH = [A.bf16(3072), A.bf16(3072)]
    XT = [A.f32(1024), A.f32(1024)]
    XNB = [A.bf16(1024), A.bf16(1024)]
    XNFW = A.f32(1024)
    BCS = XNFW[:, 0:512]
    BN = A.f32(12)
    STAT = A.f32(8)
    M4 = [A.f32(4), A.f32(4)]
    M8 = [A.f32(8), A.f32(8)]
    M4B = [A.bf16(4), A.bf16(4)]
    NEGMB = [A.bf16(512), A.bf16(512)]
    RECI = A.f32(512)
    PT = [A.bf16(512), A.bf16(512), A.bf16(512), A.bf16(512)]
    XO = [A.f32(1024), A.f32(1024)]
    DEAD = OAL[:, 0:2048].bitcast(F32)
    bufs = dict(bn=BN, stat=STAT, xnf=XNFW, dead=DEAD)

    em.wait("vector", "bt", P["boot_sd"])
    em.op("vector", lambda E: E.tensor_scalar_mul(P["gammab"][:], P["gammab"][:], 32.0))
    em.drain("vector")

    xt_guard = [0, 0]
    xnb_guard = [0, 0]
    psT_guard = 0
    for t in range(NT):
        b = t % 2
        em.wait("sync", "vv", xt_guard[b])
        sd = em.op("sync", lambda E, t=t, b=b: E.dma_start(
            out=XT[b][:], in_=din["x"][t * 128:(t + 1) * 128, :]),
            inc=f"xA{b}", k=16)
        em.wait("vector", f"xA{b}", sd)
        em.wait("vector", "tv", xnb_guard[b])
        ln_vv = _ln_tile(em, P, A, bufs, XT[b][:], XNB[b][:], l2gamma=True)
        xt_guard[b] = em.cnt["vv"]
        em.wait("tensor", "vv", ln_vv)
        tv, vv = _transpose_tile(
            em, P, XNB[b][:],
            lambda c, t=t: XNF[:, c * 4096 + t * 128: c * 4096 + (t + 1) * 128],
            ("vv", psT_guard))
        psT_guard = vv
        xnb_guard[b] = tv
    xnf_done_vv = em.cnt["vv"]

    wh_guard = [0, 0]
    ps_guard = [0, 0, 0, 0]
    ps_rel = ["vv", "vv", "vv", "vv"]
    pt_guard = [0, 0, 0, 0]
    ps4_guard = 0
    ps5_guard = 0
    psR_guard = 0
    oal_done = []

    em.wait("tensor", "vv", xnf_done_vv)
    for h in range(8):
        wb = h % 2
        if wh_guard[wb] > 0:
            em.wait("gpsimd", "tv", wh_guard[wb])
        gd = em.op("gpsimd", lambda E, h=h, wb=wb: E.dma_start(
            out=WH[wb][:], in_=din["wqkv_h"][h * 128:(h + 1) * 128, :]),
            inc=f"whA{wb}", k=16)
        em.wait("tensor", f"whA{wb}", gd)

        for which, dst, woff in (("q", QF, 0), ("k", KF, 1024)):
            for tt in range(8):
                pb = tt % 4
                if ps_guard[pb] > 0:
                    em.wait("tensor", ps_rel[pb], ps_guard[pb])
                for kc in range(8):
                    f = lambda E, kc=kc, tt=tt, wb=wb, woff=woff, pb=pb: E.matmul(
                        ps[pb][:], WH[wb][:, woff + kc * 128: woff + (kc + 1) * 128],
                        XNF[:, kc * 4096 + tt * 512: kc * 4096 + (tt + 1) * 512],
                        start=(kc == 0), stop=(kc == 7))
                    if kc < 7:
                        em.op("tensor", f)
                    else:
                        tv = em.op("tensor", f, inc="tv")
                em.wait("vector", "tv", tv)
                ps_guard[pb] = em.op("vector", lambda E, dst=dst, tt=tt, pb=pb:
                    E.tensor_copy(dst[:, tt * 512:(tt + 1) * 512], ps[pb][:]),
                    inc="vv")
                ps_rel[pb] = "vv"
        for rnd in range(8):
            pb = rnd % 4
            if ps_guard[pb] > 0:
                em.wait("tensor", ps_rel[pb], ps_guard[pb])
            for t4 in range(4):
                tk = rnd * 4 + t4
                for kc in range(8):
                    f = lambda E, kc=kc, tk=tk, t4=t4, wb=wb, pb=pb: E.matmul(
                        ps[pb][:, t4 * 128:(t4 + 1) * 128],
                        XNF[:, kc * 4096 + tk * 128: kc * 4096 + (tk + 1) * 128],
                        WH[wb][:, 2048 + kc * 128: 2048 + (kc + 1) * 128],
                        start=(kc == 0), stop=(kc == 7))
                    if kc < 7 or t4 < 3:
                        em.op("tensor", f)
                    else:
                        tv = em.op("tensor", f, inc="tv")
            em.wait("vector", "tv", tv)
            ps_guard[pb] = em.op("vector", lambda E, rnd=rnd, pb=pb:
                E.tensor_copy(VT[:, rnd * 512:(rnd + 1) * 512], ps[pb][:]),
                inc="vv")
            ps_rel[pb] = "vv"
        wh_guard[wb] = em.cnt["tv"]
        em.wait("tensor", "vv", em.cnt["vv"])

        def pass1_steps(g):
            gp = g % 2
            steps = []
            for qt in range(4):
                for j in range(g + 1):
                    def step(qt=qt, j=j, g=g, gp=gp):
                        pb = 2 + (j % 2)
                        if ps_guard[pb] > 0:
                            em.wait("tensor", ps_rel[pb], ps_guard[pb])
                        tv = em.op("tensor", lambda E, g=g, qt=qt, j=j, pb=pb:
                            E.matmul(ps[pb][:],
                                     QF[:, (g * 4 + qt) * 128:(g * 4 + qt + 1) * 128],
                                     KF[:, j * 512:(j + 1) * 512],
                                     start=True, stop=True), inc="tv")
                        em.wait("vector", "tv", tv)
                        ps_guard[pb] = em.op("vector", lambda E, j=j, pb=pb, gp=gp:
                            E.reduce_max(M8[gp][:, j:j + 1], ps[pb][:], axis=AX.X),
                            inc="vv")
                        ps_rel[pb] = "vv"
                        if j == g:
                            em.drain("vector")
                            em.op("vector", lambda E, qt=qt, g=g, gp=gp: E.reduce_max(
                                M4[gp][:, qt:qt + 1], M8[gp][:, 0:g + 1], axis=AX.X))
                            em.drain("vector")
                    steps.append(step)
            return steps

        def pass1_tail(g):
            nonlocal psR_guard
            gp = g % 2
            vv = em.op("vector", lambda E, gp=gp: E.tensor_copy(
                M4B[gp][:], M4[gp][:]), inc="vv")
            em.wait("tensor", "vv", max(vv, psR_guard))
            for qt in range(4):
                f = lambda E, qt=qt, gp=gp: E.matmul(
                    P["psR"][0:1, qt * 128:(qt + 1) * 128], M4B[gp][:, qt:qt + 1],
                    P["identb"][:], start=True, stop=True)
                if qt < 3:
                    em.op("tensor", f)
                else:
                    tv = em.op("tensor", f, inc="tv")
            em.wait("vector", "tv", tv)
            psR_guard = em.op("vector", lambda E, gp=gp: E.tensor_scalar_mul(
                NEGMB[gp][0:1, :], P["psR"][0:1, 0:512], -1.0), inc="vv")

        for s1 in pass1_steps(0):
            s1()
        pass1_tail(0)
        for g in range(8):
            gp = g % 2
            nxt = pass1_steps(g + 1) if g < 7 else []
            nch = 4 * g + 4
            negm_vv = psR_guard
            for c in range(nch):
                pb = c % 2
                if ps_guard[pb] > 0:
                    em.wait("tensor", ps_rel[pb], ps_guard[pb])
                em.wait("tensor", "vv", negm_vv)
                em.op("tensor", lambda E, c=c, pb=pb, g=g: E.matmul(
                    ps[pb][:], KF[:, c * 128:(c + 1) * 128],
                    QF[:, g * 512:(g + 1) * 512], start=True, stop=False))
                tv = em.op("tensor", lambda E, pb=pb, gp=gp: E.matmul(
                    ps[pb][:], P["onesrb"][0:1, :], NEGMB[gp][0:1, :],
                    start=False, stop=True), inc="tv")
                if pt_guard[pb] > 0:
                    em.wait("scalar", "tv", pt_guard[pb])
                em.wait("scalar", "tv", tv)
                sv = em.op("scalar", lambda E, pb=pb: E.activation(
                    PT[pb][:], ps[pb][:], AF.Exp, scale=SCALE_ATTN), inc="sv")
                ps_guard[pb] = sv
                ps_rel[pb] = "sv"
                dep = ("sv", sv)
                if c >= 4 * g:
                    j = c - 4 * g
                    em.wait("vector", "sv", sv)
                    vv = em.op("vector", lambda E, pb=pb, j=j: E.tensor_mul(
                        PT[pb][:], PT[pb][:], P["masks"][:, j * 512:(j + 1) * 512]),
                        inc="vv")
                    dep = ("vv", vv)
                em.wait("tensor", dep[0], dep[1])
                if c == 0:
                    em.wait("tensor", "vv", ps4_guard)
                    em.wait("tensor", "vv", ps5_guard)
                em.op("tensor", lambda E, c=c, pb=pb, first=(c == 0),
                      last=(c == nch - 1): E.matmul(
                    ps[4][:], VT[:, c * 128:(c + 1) * 128], PT[pb][:],
                    start=first, stop=last))
                tv = em.op("tensor", lambda E, pb=pb, first=(c == 0),
                           last=(c == nch - 1): E.matmul(
                    ps[5][0:1, :], P["onescb"][:, 0:1], PT[pb][:],
                    start=first, stop=last), inc="tv")
                pt_guard[pb] = tv
                if nxt:
                    nxt.pop(0)()
            em.wait("vector", "tv", tv)
            ps5_guard = em.op("vector", lambda E: E.reciprocal(
                RECI[0:1, :], ps[5][0:1, 0:512]), inc="vv")
            em.wait("tensor", "vv", ps5_guard)
            em.wait("tensor", "vv", psR_guard)
            tv = em.op("tensor", lambda E: E.matmul(
                P["psR"][:], P["onesr"][0:1, :], RECI[0:1, :],
                start=True, stop=True), inc="tv")
            em.wait("vector", "tv", tv)
            em.op("vector", lambda E: E.tensor_copy(BCS[:], P["psR"][:, 0:512]))
            em.drain("vector")
            psR_guard = em.cnt["vv"] + 1
            ps4_guard = em.op("vector", lambda E, h=h, g=g: E.tensor_mul(
                OAL[:, h * 4096 + g * 512: h * 4096 + (g + 1) * 512],
                ps[4][:], BCS[:]), inc="vv")
            for s1 in nxt:
                s1()
            if g < 7:
                pass1_tail(g + 1)
    oal_vv = em.cnt["vv"]

    WOUT = XNF[:, 0:8192]
    em.wait("gpsimd", "tv", em.cnt["tv"])
    gd = em.op("gpsimd", lambda E: E.dma_start(out=WOUT[:, 0:8192],
               in_=din["wout_c"]), inc="woA", k=16)
    em.wait("tensor", "woA", gd)
    em.wait("tensor", "vv", oal_vv)
    xo_guard = [0, 0]
    xt_guard = [em.cnt["vv"], em.cnt["vv"]]
    psout_guard = [0, 0]
    for t in range(NT):
        b = t % 2
        em.wait("sync", "vv", xt_guard[b])
        sd = em.op("sync", lambda E, t=t, b=b: E.dma_start(
            out=XT[b][:], in_=din["x"][t * 128:(t + 1) * 128, :]),
            inc=f"xA{b}", k=16)
        for half in range(2):
            if psout_guard[half] > 0:
                em.wait("tensor", "vv", psout_guard[half])
            for hh in range(8):
                f = lambda E, hh=hh, t=t, half=half: E.matmul(
                    ps[half][:], OAL[:, hh * 4096 + t * 128: hh * 4096 + (t + 1) * 128],
                    WOUT[:, hh * 1024 + half * 512: hh * 1024 + (half + 1) * 512],
                    start=(hh == 0), stop=(hh == 7))
                if hh < 7:
                    em.op("tensor", f)
                else:
                    tv = em.op("tensor", f, inc="tv")
            em.wait("vector", "tv", tv)
            em.wait("vector", f"xA{b}", sd)
            if half == 0 and xo_guard[b] > 0:
                em.wait("vector", f"xoS{b}", xo_guard[b])
            psout_guard[half] = em.op(
                "vector", lambda E, b=b, half=half: E.tensor_add(
                    XO[b][:, half * 512:(half + 1) * 512], ps[half][:],
                    XT[b][:, half * 512:(half + 1) * 512]), inc="vv")
        xt_guard[b] = em.cnt["vv"]
        em.wait("gpsimd", "vv", em.cnt["vv"])
        xo_guard[b] = em.op("gpsimd", lambda E, t=t, b=b: E.dma_start(
            out=dd["xr1"][t * 128:(t + 1) * 128, :], in_=XO[b][:]),
            inc=f"xoS{b}", k=16)
    A.reset()


def emit_B(em, nc, P, din, dd, A):
    em.join()
    ps = P["ps"]
    W1S = A.bf16(32768)
    W2S = A.bf16(32768)
    HFM = A.bf16(16384)
    XNFS = A.bf16(4096)
    XTS = [A.f32(1024) for _ in range(4)]
    XNB = [A.bf16(1024)]
    XNB.append(XNB[0])
    XNFW = A.f32(1024)
    BCS = XNFW[:, 0:512]
    BN = A.f32(12)
    STAT = A.f32(8)
    B1S = A.f32(32)
    B2B = A.bf16(1024)
    XO = [A.f32(1024)]
    XO.append(XO[0])
    DEAD = A.f32(4)
    bufs = dict(bn=BN, stat=STAT, xnf=XNFW, dead=DEAD)

    w1v = em.op("gpsimd", lambda E: E.dma_start(out=W1S[:], in_=din["w1_c"]),
                inc="w1B", k=16)
    w2v = em.op("gpsimd", lambda E: E.dma_start(out=W2S[:], in_=din["w2_c"]),
                inc="w2B", k=16)
    bbv = em.op("sync", lambda E: E.dma_start(out=B1S[:], in_=din["b1_t"]),
                inc="bB", k=16)
    b2v = em.op("gpsimd", lambda E: E.dma_start(
        out=B2B[:], in_=din["b2_r"].to_broadcast([128, D])), inc="b2B", k=16)
    em.wait("tensor", "w1B", w1v)
    em.wait("tensor", "w2B", w2v)
    em.wait("scalar", "bB", bbv)
    em.wait("vector", "b2B", b2v)

    xt_guard = [0, 0, 0, 0]
    xnb_guard = [0, 0]
    psT_guard = 0
    xnfs_guard = 0
    hfm_guard = 0
    psH_guard = [0, 0, 0, 0]
    psO_guard = [0, 0]
    xo_guard = [0, 0]
    for s in range(NSLAB):
        for tt in range(4):
            t = s * 4 + tt
            b = 0
            em.wait("sync", "vv", xt_guard[tt])
            sd = em.op("sync", lambda E, t=t, tt=tt: E.dma_start(
                out=XTS[tt][:], in_=dd["xr1"][t * 128:(t + 1) * 128, :]),
                inc=f"xB{tt}", k=16)
            em.wait("vector", f"xB{tt}", sd)
            em.wait("vector", "tv", xnb_guard[b])
            ln_vv = _ln_tile(em, P, A, bufs, XTS[tt][:], XNB[b][:], l2gamma=False)
            em.wait("tensor", "vv", ln_vv)
            if psT_guard:
                em.wait("tensor", "vv", psT_guard)
            for c in range(8):
                f = lambda E, c=c: E.transpose(
                    P["psT"][:, c * 128:(c + 1) * 128],
                    XNB[b][:, c * 128:(c + 1) * 128], P["identb"][:])
                if c < 7:
                    em.op("tensor", f)
                else:
                    tv = em.op("tensor", f, inc="tv")
            em.wait("vector", "tv", tv)
            em.wait("vector", "tv", xnfs_guard)
            for c in range(8):
                f = lambda E, c=c, tt=tt: E.tensor_copy(
                    XNFS[:, c * 512 + tt * 128: c * 512 + (tt + 1) * 128],
                    P["psT"][:, c * 128:(c + 1) * 128])
                if c < 7:
                    em.op("vector", f)
                else:
                    vv = em.op("vector", f, inc="vv")
            psT_guard = vv
            xnb_guard[0] = tv
        xnfs_vv = em.cnt["vv"]
        em.wait("tensor", "vv", xnfs_vv)
        em.wait("scalar", "tv", hfm_guard)
        for ic in range(32):
            pb = ic % 4
            if psH_guard[pb]:
                em.wait("tensor", "sv", psH_guard[pb])
            for kc in range(8):
                f = lambda E, kc=kc, ic=ic, pb=pb: E.matmul(
                    ps[pb][:], W1S[:, kc * 4096 + ic * 128: kc * 4096 + (ic + 1) * 128],
                    XNFS[:, kc * 512:(kc + 1) * 512], start=(kc == 0), stop=(kc == 7))
                if kc < 7:
                    em.op("tensor", f)
                else:
                    tv = em.op("tensor", f, inc="tv")
            em.wait("scalar", "tv", tv)
            psH_guard[pb] = em.op("scalar", lambda E, ic=ic, pb=pb: E.activation(
                HFM[:, ic * 512:(ic + 1) * 512], ps[pb][:], AF.Relu,
                bias=B1S[:, ic:ic + 1]), inc="sv")
        h_sv = em.cnt["sv"]
        xnfs_guard = em.cnt["tv"]
        em.wait("tensor", "sv", h_sv)
        for tt in range(4):
            t = s * 4 + tt
            b = 0
            for half in range(2):
                if psO_guard[half]:
                    em.wait("tensor", "vv", psO_guard[half])
                for ic in range(32):
                    f = lambda E, ic=ic, tt=tt, half=half: E.matmul(
                        ps[4 + half][:],
                        HFM[:, ic * 512 + tt * 128: ic * 512 + (tt + 1) * 128],
                        W2S[:, ic * 1024 + half * 512: ic * 1024 + (half + 1) * 512],
                        start=(ic == 0), stop=(ic == 31))
                    if ic < 31:
                        em.op("tensor", f)
                    else:
                        tv = em.op("tensor", f, inc="tv")
                em.wait("vector", "tv", tv)
                if xo_guard[b]:
                    em.wait("vector", f"xoB{b}", xo_guard[b])
                em.op("vector", lambda E, b=b, half=half: E.tensor_add(
                    XO[b][:, half * 512:(half + 1) * 512], ps[4 + half][:],
                    B2B[:, half * 512:(half + 1) * 512]))
                em.drain("vector")
                psO_guard[half] = em.op(
                    "vector", lambda E, b=b, half=half, tt=tt: E.tensor_add(
                        XO[b][:, half * 512:(half + 1) * 512],
                        XO[b][:, half * 512:(half + 1) * 512],
                        XTS[tt][:, half * 512:(half + 1) * 512]), inc="vv")
            xt_guard[tt] = em.cnt["vv"]
            em.wait("gpsimd", "vv", em.cnt["vv"])
            xo_guard[b] = em.op("gpsimd", lambda E, t=t, b=b: E.dma_start(
                out=dd["xr2"][t * 128:(t + 1) * 128, :], in_=XO[b][:]),
                inc=f"xoB{b}", k=16)
        hfm_guard = em.cnt["tv"]
    A.reset()


def emit_C(em, nc, P, din, dd, A):
    em.join()
    ps = P["ps"]

    INPC = A.bf16(16384)
    WPC = {}
    for nm in ("qp", "kp", "vp", "ip", "fp", "op"):
        WPC[nm] = A.bf16(8192)
    XNFS = A.bf16(4096)
    XIF = A.bf16(4096)
    XT = [A.f32(1024), A.f32(1024)]
    XNB = [A.bf16(1024)]
    XNFW = A.f32(1024)
    BCS = XNFW[:, 0:512]
    BN = A.f32(12)
    STAT = A.f32(8)
    DEAD = A.f32(4)
    IBB = A.f32(1024)
    FBB = A.f32(1024)
    OBB = A.f32(1024)
    PRJ = [A.bf16(1024), A.bf16(1024)]
    PRF = [A.f32(1024), A.f32(1024)]
    OGT = A.f32(1024)
    XGT = A.f32(1024)
    GT = [A.bf16(1024), A.bf16(1024)]
    bufs = dict(bn=BN, stat=STAT, xnf=XNFW, dead=DEAD)

    wv = {}
    wv["inp"] = em.op("gpsimd", lambda E: E.dma_start(out=INPC[:], in_=din["inproj_c"]),
                      inc="wC", k=16)
    for nm in ("qp", "kp", "vp", "ip", "fp", "op"):
        wv[nm] = em.op("gpsimd", lambda E, nm=nm: E.dma_start(
            out=WPC[nm][:], in_=din[nm + "_c"]), inc="wC", k=16)
    wC_all = em.cnt["wC"]
    bv = em.op("sync", lambda E: E.dma_start(
        out=IBB[:], in_=din["ib_r"].to_broadcast([128, D])), inc="bC", k=16)
    bv = em.op("sync", lambda E: E.dma_start(
        out=FBB[:], in_=din["fb_r"].to_broadcast([128, D])), inc="bC", k=16)
    bv = em.op("sync", lambda E: E.dma_start(
        out=OBB[:], in_=din["ob_r"].to_broadcast([128, D])), inc="bC", k=16)
    em.wait("tensor", "wC", wC_all)
    em.wait("vector", "bC", bv)

    xt_guard = [0, 0]
    xnb_guard = [0]
    psT_guard = 0
    xnfs_guard = 0
    prj_guard = [0, 0]
    prf_guard = [0, 0]
    gt_guard = [0, 0]
    psp_guard = [0, 0]
    for s in range(NSLAB):
        for tt in range(4):
            t = s * 4 + tt
            b = tt % 2
            em.wait("sync", "vv", xt_guard[b])
            sd = em.op("sync", lambda E, t=t, b=b: E.dma_start(
                out=XT[b][:], in_=dd["xr2"][t * 128:(t + 1) * 128, :]),
                inc=f"xC{b}", k=16)
            em.wait("vector", f"xC{b}", sd)
            em.wait("vector", "tv", xnb_guard[0])
            ln_vv = _ln_tile(em, P, A, bufs, XT[b][:], XNB[0][:], l2gamma=False)
            xt_guard[b] = em.cnt["vv"]
            em.wait("tensor", "vv", ln_vv)
            if psT_guard:
                em.wait("tensor", "vv", psT_guard)
            for c in range(8):
                f = lambda E, c=c: E.transpose(
                    P["psT"][:, c * 128:(c + 1) * 128],
                    XNB[0][:, c * 128:(c + 1) * 128], P["identb"][:])
                em.op("tensor", f) if c < 7 else None
                if c == 7:
                    tv = em.op("tensor", f, inc="tv")
            em.wait("vector", "tv", tv)
            em.wait("vector", "tv", xnfs_guard)
            for c in range(8):
                f = lambda E, c=c, tt=tt: E.tensor_copy(
                    XNFS[:, c * 512 + tt * 128: c * 512 + (tt + 1) * 128],
                    P["psT"][:, c * 128:(c + 1) * 128])
                if c < 7:
                    em.op("vector", f)
                else:
                    vv = em.op("vector", f, inc="vv")
            psT_guard = vv
            xnb_guard[0] = tv
        xnfs_vv = em.cnt["vv"]
        em.wait("tensor", "vv", xnfs_vv)
        for io in range(8):
            pb = io % 2
            if psp_guard[pb]:
                em.wait("tensor", "vv", psp_guard[pb])
            for kc in range(8):
                f = lambda E, kc=kc, io=io, pb=pb: E.matmul(
                    ps[pb][:], INPC[:, kc * 2048 + io * 128: kc * 2048 + (io + 1) * 128],
                    XNFS[:, kc * 512:(kc + 1) * 512], start=(kc == 0), stop=(kc == 7))
                if kc < 7:
                    em.op("tensor", f)
                else:
                    tv = em.op("tensor", f, inc="tv")
            em.wait("vector", "tv", tv)
            psp_guard[pb] = em.op("vector", lambda E, io=io, pb=pb: E.tensor_copy(
                XIF[:, io * 512:(io + 1) * 512], ps[pb][:]), inc="vv")
        xif_vv = em.cnt["vv"]
        em.wait("tensor", "vv", xif_vv)
        ps23e = 0
        for tt in range(4):
            t = s * 4 + tt
            for j, (nm, spname) in enumerate((("qp", "sp_q"), ("kp", "sp_k"),
                                              ("vp", "sp_v"))):
                pb = j % 2
                em.wait("tensor", "vv", ps23e)
                if prj_guard[pb]:
                    em.wait("vector", f"pC{pb}", prj_guard[pb])
                for half in range(2):
                    pp = 2 + half
                    for io in range(8):
                        f = lambda E, io=io, tt=tt, half=half, nm=nm, pp=pp: E.matmul(
                            ps[pp][:], XIF[:, io * 512 + tt * 128: io * 512 + (tt + 1) * 128],
                            WPC[nm][:, io * 1024 + half * 512: io * 1024 + (half + 1) * 512],
                            start=(io == 0), stop=(io == 7))
                        if io < 7 or half < 1:
                            em.op("tensor", f)
                        else:
                            tv = em.op("tensor", f, inc="tv")
                em.wait("vector", "tv", tv)
                em.op("vector", lambda E, pb=pb: E.tensor_copy(
                    PRJ[pb][:, 0:512], ps[2][:]))
                vv = em.op("vector", lambda E, pb=pb: E.tensor_copy(
                    PRJ[pb][:, 512:1024], ps[3][:]), inc="vv")
                ps23e = vv
                em.wait("gpsimd", "vv", vv)
                prj_guard[pb] = em.op("gpsimd", lambda E, t=t, pb=pb, spname=spname:
                    E.dma_start(out=dd[spname][t * 128:(t + 1) * 128, :],
                                in_=PRJ[pb][:]), inc=f"pC{pb}", k=16)
            for j, (nm, spname, BB) in enumerate((("ip", "sp_i", IBB),
                                                  ("fp", "sp_f", FBB))):
                pb = j % 2
                em.wait("tensor", "vv", ps23e)
                if prf_guard[pb]:
                    em.wait("vector", f"pF{pb}", prf_guard[pb])
                for half in range(2):
                    pp = 2 + half
                    for io in range(8):
                        f = lambda E, io=io, tt=tt, half=half, nm=nm, pp=pp: E.matmul(
                            ps[pp][:], XIF[:, io * 512 + tt * 128: io * 512 + (tt + 1) * 128],
                            WPC[nm][:, io * 1024 + half * 512: io * 1024 + (half + 1) * 512],
                            start=(io == 0), stop=(io == 7))
                        if io < 7 or half < 1:
                            em.op("tensor", f)
                        else:
                            tv = em.op("tensor", f, inc="tv")
                em.wait("vector", "tv", tv)
                em.op("vector", lambda E, pb=pb, BB=BB: E.tensor_add(
                    PRF[pb][:, 0:512], ps[2][:], BB[:, 0:512]))
                vv = em.op("vector", lambda E, pb=pb, BB=BB: E.tensor_add(
                    PRF[pb][:, 512:1024], ps[3][:], BB[:, 512:1024]), inc="vv")
                ps23e = vv
                em.wait("gpsimd", "vv", vv)
                prf_guard[pb] = em.op("gpsimd", lambda E, t=t, pb=pb, spname=spname:
                    E.dma_start(out=dd[spname][t * 128:(t + 1) * 128, :],
                                in_=PRF[pb][:]), inc=f"pF{pb}", k=16)
            gb = tt % 2
            em.wait("tensor", "vv", ps23e)
            if gt_guard[gb]:
                em.wait("vector", f"gC{gb}", gt_guard[gb])
            for half in range(2):
                pp = 2 + half
                for io in range(8):
                    f = lambda E, io=io, tt=tt, half=half, pp=pp: E.matmul(
                        ps[pp][:], XIF[:, io * 512 + tt * 128: io * 512 + (tt + 1) * 128],
                        WPC["op"][:, io * 1024 + half * 512: io * 1024 + (half + 1) * 512],
                        start=(io == 0), stop=(io == 7))
                    if io < 7 or half < 1:
                        em.op("tensor", f)
                    else:
                        tv = em.op("tensor", f, inc="tv")
            em.wait("vector", "tv", tv)
            em.op("vector", lambda E: E.tensor_add(OGT[:, 0:512], ps[2][:],
                                                   OBB[:, 0:512]))
            vv = em.op("vector", lambda E: E.tensor_add(OGT[:, 512:1024], ps[3][:],
                                                        OBB[:, 512:1024]), inc="vv")
            ps23e = vv
            for half in range(2):
                pp = 2 + half
                em.wait("tensor", "vv", vv)
                for kc in range(8):
                    f = lambda E, kc=kc, tt=tt, half=half, pp=pp: E.matmul(
                        ps[pp][:], XNFS[:, kc * 512 + tt * 128: kc * 512 + (tt + 1) * 128],
                        INPC[:, kc * 2048 + 1024 + half * 512: kc * 2048 + 1024 + (half + 1) * 512],
                        start=(kc == 0), stop=(kc == 7))
                    if kc < 7 or half < 1:
                        em.op("tensor", f)
                    else:
                        tv = em.op("tensor", f, inc="tv")
            em.wait("vector", "tv", tv)
            em.op("vector", lambda E: E.tensor_copy(XGT[:, 0:512], ps[2][:]))
            vv = em.op("vector", lambda E: E.tensor_copy(XGT[:, 512:1024],
                                                         ps[3][:]), inc="vv")
            ps23e = vv
            em.wait("scalar", "vv", vv)
            em.op("scalar", lambda E: E.activation(OGT[:], OGT[:], AF.Sigmoid))
            sv = em.op("scalar", lambda E: E.activation(XGT[:], XGT[:],
                                                        AF.Sigmoid), inc="sv")
            em.wait("vector", "sv", sv)
            vv = em.op("vector", lambda E, gb=gb: E.tensor_mul(
                GT[gb][:], OGT[:], XGT[:]), inc="vv")
            em.wait("gpsimd", "vv", vv)
            gt_guard[gb] = em.op("gpsimd", lambda E, t=t, gb=gb: E.dma_start(
                out=dd["sp_g"][t * 128:(t + 1) * 128, :], in_=GT[gb][:]),
                inc=f"gC{gb}", k=16)
            xnfs_guard = em.cnt["tv"]
    c1_done = dict(em.cnt)
    A.reset()

    em.join()
    OUTPC = A.bf16(8192)
    QNB = A.bf16(1024)
    KNB = A.bf16(1024)
    RSEB = A.bf16(1024)
    EXQ = A.bf16(1024)
    EXK = A.bf16(1024)
    EXK1 = A.bf16(1024)
    EXB = A.bf16(1024)
    EYE16R = A.f32(256)
    LQ = [A.bf16(1024), A.bf16(1024)]
    LK = [A.bf16(1024), A.bf16(1024)]
    LV = [A.bf16(1024), A.bf16(1024)]
    LI = [A.f32(1024), A.f32(1024)]
    LF = [A.f32(1024), A.f32(1024)]
    LG = [A.bf16(1024), A.bf16(1024)]
    XT2 = [A.f32(1024), A.f32(1024)]
    NL = A.f32(1024)
    LROW = A.f32(1024)
    R1 = A.f32(1024)
    R2 = A.f32(1024)
    K1 = A.bf16(1024)
    K1F = A.bf16(1024)
    A1S = A.f32(1024)
    ARGQ = A.f32(1024)
    ARGK = A.f32(1024)
    ARGB = A.f32(1024)
    SQ = A.f32(1024)
    MS = A.f32(32)
    RSE = SQ
    QN = A.bf16(1024)
    KN = A.bf16(1024)
    QA = A.bf16(1024)
    QI = A.f32(1024)
    KT = A.bf16(1024)
    KB = A.bf16(1024)
    LFW = A.f32(1024)
    QAF = A.bf16(2048)
    QIF = A.f32(2048)
    KTF = A.bf16(2048)
    ASB_ALL = A.bf16(2048)
    HVS = A.f32(1040)
    DEN16 = A.f32(32)
    PFE = A.f32(1040)
    HT = A.f32(1024)
    HG = A.bf16(1024)
    HGF = A.bf16(1024)
    CAU = A.f32(1040)
    NLE = A.f32(16)
    PF16 = A.f32(16)
    DCOL = A.f32(4)
    XO2 = [A.f32(1024)]
    bv2g = em.op("gpsimd", lambda E: E.dma_start(
        out=QNB[:], in_=din["qnw_r"].to_broadcast([128, D])), inc="bC2g", k=16)
    bv2g = em.op("gpsimd", lambda E: E.dma_start(
        out=KNB[:], in_=din["knw_r"].to_broadcast([128, D])), inc="bC2g", k=16)
    em.wait("vector", "bC2g", bv2g)
    wv2 = em.op("gpsimd", lambda E: E.dma_start(out=OUTPC[:], in_=din["outp_c"]),
                inc="wC2", k=16)
    eyev = em.op("sync", lambda E: E.dma_start(out=EYE16R[0:1, :],
                 in_=dd["eye16r"]), inc="bC2", k=16)
    em.wait("vector", "bC2", em.cnt["bC2"])
    em.wait("tensor", "wC2", wv2)
    em.wait("tensor", "bC2", em.cnt["bC2"])

    em.op("vector", lambda E: E.memset(CAU[:], 0.0))
    em.op("vector", lambda E: E.memset(ps[4][:, 0:128], 0.0))
    em.drain("vector")

    ld_guard = {}
    for st in ("q", "k", "v", "i", "f", "g", "x"):
        ld_guard[st] = [0, 0]
    LD = dict(q=(LQ, "sp_q"), k=(LK, "sp_k"), v=(LV, "sp_v"),
              i=(LI, "sp_i"), f=(LF, "sp_f"), g=(LG, "sp_g"))
    sdv = {}
    xo_guard2 = 0
    cau_vv = em.cnt["vv"]
    ps01_rel = 0
    prev_rse_rel = 0
    psPT_guard = 0
    psPTb_guard = 0

    for t in range(NT):
        b = t % 2
        for st, (bufs2, spname) in LD.items():
            em.wait("sync", "vv", ld_guard[st][b])
            sdv[st] = em.op("sync", lambda E, t=t, b=b, bufs2=bufs2, spname=spname:
                E.dma_start(out=bufs2[b][:],
                            in_=dd[spname][t * 128:(t + 1) * 128, :]),
                inc=f"L{st}{b}", k=16)
        em.wait("sync", "vv", ld_guard["x"][b])
        sdv["x"] = em.op("sync", lambda E, t=t, b=b: E.dma_start(
            out=XT2[b][:], in_=dd["xr2"][t * 128:(t + 1) * 128, :]),
            inc=f"Lx{b}", k=16)

        em.wait("scalar", f"Lf{b}", sdv["f"])
        em.wait("scalar", "vv", prev_rse_rel)
        em.op("scalar", lambda E, b=b: E.activation(SQ[:], LF[b][:], AF.Exp,
                                                    scale=-1.0))
        em.drain("scalar")
        sv = em.op("scalar", lambda E: E.activation(LFW[:], SQ[:], AF.Ln,
                                                    bias=1.0), inc="sv")
        em.wait("tensor", "sv", sv)
        em.wait("tensor", "vv", max(cau_vv, ps01_rel))
        for half in range(2):
            tv = em.op("tensor", lambda E, half=half: E.matmul(
                ps[half][:], P["ltincT"][:],
                LFW[:, half * 512:(half + 1) * 512], start=True, stop=True),
                inc="tv")
        em.wait("vector", "tv", tv)
        em.op("vector", lambda E: E.tensor_copy(NL[:, 0:512], ps[0][:]))
        vv = em.op("vector", lambda E: E.tensor_copy(NL[:, 512:1024], ps[1][:]),
                   inc="vv")
        em.wait("tensor", "vv", em.cnt["vv"])
        for j, W in enumerate((P["c1w"], P["c2w"], P["onesc"])):
            for half in range(2):
                tvl = em.op("tensor", lambda E, j=j, half=half, W=W: E.matmul(
                    ps[2 * j + half][0:1, 0:512], W[:, 0:1],
                    LFW[:, half * 512:(half + 1) * 512], start=True, stop=True),
                    inc="tv")
        em.wait("vector", "tv", tvl)
        for j, R in enumerate((R1, R2, LROW)):
            em.op("vector", lambda E, j=j, R=R: E.tensor_copy(
                R[0:1, 0:512], ps[2 * j][0:1, 0:512]))
            vv = em.op("vector", lambda E, j=j, R=R: E.tensor_copy(
                R[0:1, 512:1024], ps[2 * j + 1][0:1, 0:512]), inc="vv")
        em.wait("vector", f"Li{b}", sdv["i"])
        em.wait("tensor", "vv", vv)
        for half in range(2):
            tvb = em.op("tensor", lambda E, half=half: E.matmul(
                ps[2 + half][:], P["onesr"][0:1, :],
                R1[0:1, half * 512:(half + 1) * 512], start=True, stop=True),
                inc="tv")
        em.wait("vector", "tv", tvb)
        for half in range(2):
            em.op("vector", lambda E, half=half: E.tensor_sub(
                ARGQ[0:64, half * 512:(half + 1) * 512],
                ps[2 + half][0:64, :], NL[0:64, half * 512:(half + 1) * 512]))
            em.op("vector", lambda E, half=half, b=b: E.tensor_sub(
                A1S[0:64, half * 512:(half + 1) * 512],
                LI[b][0:64, half * 512:(half + 1) * 512], ps[2 + half][0:64, :]))
        vv = em.op("vector", lambda E: E.engine_nop(), inc="vv")
        em.wait("tensor", "vv", vv)
        for half in range(2):
            tvb = em.op("tensor", lambda E, half=half: E.matmul(
                ps[2 + half][:], P["onesr"][0:1, :],
                R2[0:1, half * 512:(half + 1) * 512], start=True, stop=True),
                inc="tv")
        em.wait("vector", "tv", tvb)
        for half in range(2):
            em.op("vector", lambda E, half=half: E.tensor_sub(
                ARGQ[64:128, half * 512:(half + 1) * 512],
                ps[2 + half][64:128, :], NL[64:128, half * 512:(half + 1) * 512]))
            em.op("vector", lambda E, half=half, b=b: E.tensor_sub(
                ARGK[:, half * 512:(half + 1) * 512],
                LI[b][:, half * 512:(half + 1) * 512], ps[2 + half][:, :]))
        vv = em.op("vector", lambda E: E.engine_nop(), inc="vv")
        em.op("vector", lambda E, b=b: E.tensor_add(ARGB[:], LI[b][:], NL[:]))
        em.drain("vector")
        em.wait("tensor", "vv", vv)
        for half in range(2):
            tvb = em.op("tensor", lambda E, half=half: E.matmul(
                ps[2 + half][:], P["onesr"][0:1, :],
                LROW[0:1, half * 512:(half + 1) * 512], start=True, stop=True),
                inc="tv")
        em.wait("vector", "tv", tvb)
        for half in range(2):
            em.op("vector", lambda E, half=half: E.tensor_sub(
                ARGB[:, half * 512:(half + 1) * 512],
                ARGB[:, half * 512:(half + 1) * 512], ps[2 + half][:, :]))
        em.drain("vector")
        em.op("vector", lambda E: E.tensor_add(ARGK[:], ARGK[:], NL[:]))
        em.op("vector", lambda E: E.tensor_add(A1S[0:64, :], A1S[0:64, :],
                                               NL[0:64, :]))
        vv = em.op("vector", lambda E: E.engine_nop(), inc="vv")
        em.wait("vector", f"Lq{b}", sdv["q"])
        em.op("vector", lambda E, b=b: E.tensor_mul(SQ[:], LQ[b][:], LQ[b][:]))
        em.drain("vector")
        em.op("vector", lambda E: E.tensor_reduce(
            MS[:, 0:16].rearrange("p (h o) -> p h o", o=1),
            SQ[:].rearrange("p (h d) -> p h d", d=64), AX.X, ALU.add))
        em.drain("vector")
        em.wait("vector", f"Lk{b}", sdv["k"])
        em.op("vector", lambda E, b=b: E.tensor_mul(SQ[:], LK[b][:], LK[b][:]))
        em.drain("vector")
        vv = em.op("vector", lambda E: E.tensor_reduce(
            MS[:, 16:32].rearrange("p (h o) -> p h o", o=1),
            SQ[:].rearrange("p (h d) -> p h d", d=64), AX.X, ALU.add), inc="vv")
        em.wait("scalar", "vv", vv)
        em.op("scalar", lambda E: E.activation(MS[:, 0:32], MS[:, 0:32], AF.Ln,
                                               bias=1e-6, scale=1.0 / 64))
        em.drain("scalar")
        sv = em.op("scalar", lambda E: E.activation(MS[:, 0:32], MS[:, 0:32],
                                                    AF.Exp, scale=-0.5), inc="sv")
        em.wait("scalar", "vv", em.cnt["vv"])
        em.op("scalar", lambda E: E.activation(EXQ[:], ARGQ[:], AF.Exp))
        em.op("scalar", lambda E: E.activation(EXK[:], ARGK[:], AF.Exp))
        em.op("scalar", lambda E: E.activation(EXK1[0:64, :], A1S[0:64, :], AF.Exp))
        sv2 = em.op("scalar", lambda E: E.activation(EXB[:], ARGB[:], AF.Exp),
                    inc="sv")
        em.wait("vector", "sv", sv)
        em.op("vector", lambda E: E.tensor_copy(
            RSEB[:].rearrange("p (h d) -> p h d", d=64),
            MS[:, 0:16].unsqueeze(2).to_broadcast([128, 16, 64])))
        em.drain("vector")
        em.op("vector", lambda E, b=b: E.tensor_mul(QN[:], LQ[b][:], RSEB[:]))
        em.drain("vector")
        em.op("vector", lambda E: E.tensor_mul(QN[:], QN[:], QNB[:]))
        em.drain("vector")
        em.op("vector", lambda E: E.tensor_copy(
            RSEB[:].rearrange("p (h d) -> p h d", d=64),
            MS[:, 16:32].unsqueeze(2).to_broadcast([128, 16, 64])))
        em.drain("vector")
        em.op("vector", lambda E, b=b: E.tensor_mul(KN[:], LK[b][:], RSEB[:]))
        em.drain("vector")
        em.op("vector", lambda E: E.tensor_mul(KN[:], KN[:], KNB[:]))
        em.drain("vector")
        em.wait("vector", "sv", sv2)
        em.op("vector", lambda E: E.tensor_mul(QA[:], QN[:], EXQ[:]))
        em.op("vector", lambda E: E.tensor_mul(KT[:], KN[:], EXK[:]))
        em.op("vector", lambda E: E.tensor_mul(K1[0:64, :], KN[0:64, :],
                                               EXK1[0:64, :]))
        kb_vv = em.op("vector", lambda E: E.tensor_mul(KB[:], KN[:], EXB[:]),
                      inc="vv")
        em.wait("scalar", "vv", kb_vv)
        sv3 = em.op("scalar", lambda E: E.activation(SQ[:], NL[:], AF.Exp,
                                                     scale=-1.0), inc="sv")
        em.drain("vector")
        em.wait("vector", "sv", sv3)
        vv = em.op("vector", lambda E: E.tensor_mul(QI[:], QN[:], SQ[:]), inc="vv")
        ld_guard["q"][b] = vv
        ld_guard["k"][b] = vv
        em.wait("tensor", f"Lv{b}", sdv["v"])
        ld_guard["i"][b] = em.cnt["vv"]
        ld_guard["f"][b] = em.cnt["vv"]
        elem_vv = em.cnt["vv"]

        em.wait("tensor", "vv", elem_vv)
        for h in range(16):
            f = lambda E, h=h: E.matmul(
                P["psR"][0:64, 100:116], LROW[0:1, h * 64:(h + 1) * 64],
                EYE16R[0:1, h * 16:(h + 1) * 16], start=(h == 0), stop=(h == 15))
            if h < 15:
                em.op("tensor", f)
            else:
                tv = em.op("tensor", f, inc="tv")
        em.wait("vector", "tv", tv)
        vv = em.op("vector", lambda E: E.tensor_copy(NLE[0:64, :],
                                                     P["psR"][0:64, 100:116]),
                   inc="vv")
        em.wait("scalar", "vv", vv)
        svp = em.op("scalar", lambda E: E.activation(PF16[0:64, :], NLE[0:64, :],
                                                     AF.Exp, scale=-1.0), inc="sv")

        em.wait("tensor", "vv", em.cnt["vv"])
        for nmT, src_, dstF in ((0, QA, QAF), (2, KT, KTF)):
            for rnd in range(4):
                pb = rnd % 2
                if psPTb_guard:
                    em.wait("tensor", "vv", psPTb_guard)
                if psT_guard:
                    em.wait("tensor", "vv", psT_guard)
                for hh in range(4):
                    h = rnd * 4 + hh
                    f = lambda E, h=h, hh=hh, src_=src_, pb=pb: E.transpose(
                        P["psT"][0:64, pb * 512 + hh * 128: pb * 512 + (hh + 1) * 128],
                        src_[:, h * 64:(h + 1) * 64], P["identb"][:])
                    if hh < 3:
                        em.op("tensor", f)
                    else:
                        tv = em.op("tensor", f, inc="tv")
                em.wait("vector", "tv", tv)
                psPTb_guard = em.op("vector", lambda E, rnd=rnd, dstF=dstF, pb=pb:
                    E.tensor_copy(dstF[0:64, rnd * 512:(rnd + 1) * 512],
                                  P["psT"][0:64, pb * 512:(pb + 1) * 512]), inc="vv")
        for nmT, src_, dstF in ((1, QI, QIF),):
            for rnd in range(4):
                pb = rnd % 2
                if psPT_guard:
                    em.wait("tensor", "vv", psPT_guard)
                for hh in range(4):
                    h = rnd * 4 + hh
                    f = lambda E, h=h, hh=hh, src_=src_, pb=pb: E.transpose(
                        ps[2 + pb][0:64, hh * 128:(hh + 1) * 128],
                        src_[:, h * 64:(h + 1) * 64], P["identf"][:])
                    if hh < 3:
                        em.op("tensor", f)
                    else:
                        tv = em.op("tensor", f, inc="tv")
                em.wait("vector", "tv", tv)
                psPT_guard = em.op("vector", lambda E, rnd=rnd, dstF=dstF, pb=pb:
                    E.tensor_copy(dstF[0:64, rnd * 512:(rnd + 1) * 512],
                                  ps[2 + pb][0:64, 0:512]), inc="vv")
        for rnd in range(2):
            pb = rnd % 2
            if psPTb_guard:
                em.wait("tensor", "vv", psPTb_guard)
            for hh in range(8):
                h = rnd * 8 + hh
                f = lambda E, h=h, hh=hh, pb=pb: E.transpose(
                    P["psT"][0:64, pb * 512 + hh * 64: pb * 512 + (hh + 1) * 64],
                    K1[0:64, h * 64:(h + 1) * 64], P["identb"][0:64, 0:64])
                if hh < 7:
                    em.op("tensor", f)
                else:
                    tv = em.op("tensor", f, inc="tv")
            em.wait("vector", "tv", tv)
            psPTb_guard = em.op("vector", lambda E, rnd=rnd, pb=pb:
                E.tensor_copy(K1F[0:64, rnd * 512:(rnd + 1) * 512],
                              P["psT"][0:64, pb * 512:(pb + 1) * 512]), inc="vv")
        tr_vv = em.cnt["vv"]

        em.wait("tensor", "vv", tr_vv)
        at_guard = [0, 0]
        for h in range(16):
            pb = h % 2
            if at_guard[pb]:
                em.wait("tensor", "vv", at_guard[pb])
            em.op("tensor", lambda E, h=h, pb=pb: E.matmul(
                ps[4 + pb][0:64, 0:64],
                K1F[0:64, h * 64:(h + 1) * 64],
                QAF[0:64, h * 128: h * 128 + 64], start=True, stop=True))
            tva = em.op("tensor", lambda E, h=h, pb=pb: E.matmul(
                ps[4 + pb][:, 64:128],
                KTF[0:64, h * 128:(h + 1) * 128],
                QAF[0:64, h * 128 + 64: h * 128 + 128], start=True, stop=True),
                inc="tv")
            em.wait("vector", "tv", tva)
            at_guard[pb] = em.op("vector", lambda E, h=h, pb=pb: E.tensor_mul(
                ASB_ALL[:, h * 128:(h + 1) * 128], ps[4 + pb][:, 0:128],
                P["scanmask_sb"][:]), inc="vv")
        asb_vv = em.cnt["vv"]
        em.wait("tensor", "vv", asb_vv)
        hv_guard4 = [0, 0, 0, 0]
        for h in range(16):
            pb = h % 4
            if hv_guard4[pb]:
                em.wait("tensor", "vv", hv_guard4[pb])
            em.op("tensor", lambda E, h=h, pb=pb: E.matmul(
                ps[pb][:, 0:65], QIF[0:64, h * 128:(h + 1) * 128],
                CAU[0:64, h * 65:(h + 1) * 65], start=True, stop=False))
            em.op("tensor", lambda E, h=h, pb=pb, b=b: E.matmul(
                ps[pb][:, 0:64], ASB_ALL[:, h * 128:(h + 1) * 128],
                LV[b][:, h * 64:(h + 1) * 64], start=False, stop=False))
            tvh = em.op("tensor", lambda E, h=h, pb=pb: E.matmul(
                ps[pb][:, 64:65], ASB_ALL[:, h * 128:(h + 1) * 128],
                P["onescb"][:, 0:1], start=False, stop=True), inc="tv")
            em.wait("vector", "tv", tvh)
            hv_guard4[pb] = em.op("vector", lambda E, h=h, pb=pb: E.tensor_copy(
                HVS[:, h * 65:(h + 1) * 65], ps[pb][:, 0:65]), inc="vv")
        em.drain("vector")
        hv3 = HVS[:].rearrange("p (h c) -> p h c", c=65)
        em.op("vector", lambda E: E.tensor_scalar_max(
            DEN16[:, 0:16], hv3[:, :, 64:65].squeeze(2), 1.0))
        em.drain("vector")
        em.op("vector", lambda E: E.reciprocal(DEN16[:, 16:32], DEN16[:, 0:16]))
        em.drain("vector")
        em.op("vector", lambda E: E.tensor_copy(
            RSE[:].rearrange("p (h d) -> p h d", d=64),
            DEN16[:, 16:32].unsqueeze(2).to_broadcast([128, 16, 64])))
        em.drain("vector")
        prev_rse = em.op("vector", lambda E: E.tensor_mul(
            HT[:].rearrange("p (h d) -> p h d", d=64),
            hv3[:, :, 0:64], RSE[:].rearrange("p (h d) -> p h d", d=64)), inc="vv")
        em.drain("vector")
        em.wait("vector", "sv", svp)
        def state_slot(h):
            if h < 7:
                return P["psR"], h * 65
            if h < 14:
                return ps[4], (h - 7) * 65
            return ps[5], (h - 14) * 65
        em.wait("tensor", "vv", em.cnt["vv"])
        for h in range(16):
            bank, off = state_slot(h)
            em.op("tensor", lambda E, h=h, bank=bank, off=off, b=b: E.matmul(
                bank[0:64, off:off + 64], KB[:, h * 64:(h + 1) * 64],
                LV[b][:, h * 64:(h + 1) * 64], start=True, stop=True))
            f = lambda E, h=h, bank=bank, off=off: E.matmul(
                bank[0:64, off + 64:off + 65], KB[:, h * 64:(h + 1) * 64],
                P["onescb"][:, 0:1], start=True, stop=True)
            if h < 15:
                em.op("tensor", f)
            else:
                tvs = em.op("tensor", f, inc="tv")
        em.wait("vector", "tv", tvs)
        em.op("vector", lambda E: E.tensor_copy(
            PFE[0:64, :].rearrange("p (h d) -> p h d", d=65),
            PF16[0:64, 0:16].unsqueeze(2).to_broadcast([64, 16, 65])))
        em.drain("vector")
        em.op("vector", lambda E: E.tensor_mul(CAU[0:64, :], CAU[0:64, :],
                                               PFE[0:64, 0:1040]))
        em.drain("vector")
        em.op("vector", lambda E: E.tensor_add(CAU[0:64, 0:455], CAU[0:64, 0:455],
                                               P["psR"][0:64, 0:455]))
        em.op("vector", lambda E: E.tensor_add(CAU[0:64, 455:910],
                                               CAU[0:64, 455:910], ps[4][0:64, 0:455]))
        st_vv = em.op("vector", lambda E: E.tensor_add(
            CAU[0:64, 910:1040], CAU[0:64, 910:1040], ps[5][0:64, 0:130]),
            inc="vv")
        em.drain("vector")
        cau_vv = em.cnt["vv"]
        prev_rse_rel = prev_rse
        ld_guard["v"][b] = cau_vv
        hcol_vv = em.cnt["vv"]

        em.wait("vector", f"Lg{b}", sdv["g"])
        vv = em.op("vector", lambda E, b=b: E.tensor_mul(HG[:], HT[:], LG[b][:]),
                   inc="vv")
        ld_guard["g"][b] = vv
        em.wait("tensor", "vv", vv)
        if psT_guard:
            em.wait("tensor", "vv", psT_guard)
        if psPTb_guard:
            em.wait("tensor", "vv", psPTb_guard)
        for c in range(8):
            f = lambda E, c=c: E.transpose(
                P["psT"][:, c * 128:(c + 1) * 128], HG[:, c * 128:(c + 1) * 128],
                P["identb"][:])
            if c < 7:
                em.op("tensor", f)
            else:
                tv = em.op("tensor", f, inc="tv")
        em.wait("vector", "tv", tv)
        vv = None
        for c in range(8):
            f = lambda E, c=c: E.tensor_copy(HGF[:, c * 128:(c + 1) * 128],
                                             P["psT"][:, c * 128:(c + 1) * 128])
            if c < 7:
                em.op("vector", f)
            else:
                vv = em.op("vector", f, inc="vv")
        psT_guard = vv
        em.wait("tensor", "vv", vv)
        for half in range(2):
            for fc in range(8):
                f = lambda E, fc=fc, half=half: E.matmul(
                    ps[half][:], HGF[:, fc * 128:(fc + 1) * 128],
                    OUTPC[:, fc * 1024 + half * 512: fc * 1024 + (half + 1) * 512],
                    start=(fc == 0), stop=(fc == 7))
                if fc < 7:
                    em.op("tensor", f)
                else:
                    tv = em.op("tensor", f, inc="tv")
        em.wait("vector", "tv", tv)
        em.wait("vector", f"Lx{b}", sdv["x"])
        if xo_guard2:
            em.wait("vector", "xoC0", xo_guard2)
        em.op("vector", lambda E, b=b: E.tensor_add(XO2[0][:, 0:512], ps[0][:],
                                                    XT2[b][:, 0:512]))
        vv = em.op("vector", lambda E, b=b: E.tensor_add(
            XO2[0][:, 512:1024], ps[1][:], XT2[b][:, 512:1024]), inc="vv")
        ld_guard["x"][b] = vv
        ps01_rel = vv
        em.wait("gpsimd", "vv", vv)
        xo_guard2 = em.op("gpsimd", lambda E, t=t: E.dma_start(
            out=dd["xr3"][t * 128:(t + 1) * 128, :], in_=XO2[0][:]),
            inc="xoC0", k=16)
    A.reset()


def emit_D(em, nc, P, din, dd, A):
    em.join()
    ps = P["ps"]
    RG = [list(range(NCORE))]

    SLOT2 = A.f32(32)
    GK = A.f32(32)
    ESTAR = A.f32(32)
    POSL = A.f32(32)
    G1 = A.f32(32)
    MYMAPT = A.f32(48)
    IDX32 = P["idx32"]
    mark = A.mark()

    XT3 = [A.f32(1024), A.f32(1024)]
    XNB = [A.bf16(1024)]
    XNFW = A.f32(1024)
    BCS = XNFW[:, 0:512]
    BN = A.f32(12)
    STAT = A.f32(8)
    DEAD = A.f32(4)
    XNFSg = A.bf16(1024)
    WGC = A.bf16(64)
    G8 = A.f32(8)
    OH = A.f32(8)
    T1 = A.f32(8)
    T2 = A.f32(8)
    CNTE = A.f32(8)
    BASE_B = A.f32(8)
    TOTR = A.f32(8)
    IOT8 = A.f32(8)
    SVALS = A.f32(32)
    GCOL = A.f32(4)
    CNTS_SB = A.f32(8)
    BASEG = A.f32(8)
    BASER = A.f32(8)
    BASEB128 = A.f32(8)
    LT8S = A.f32(8)
    OH8S = A.f32(1)
    SELS = A.f32(144)
    MAPT = A.f32(384)
    MYMAP = A.f32(128)
    ZR = A.f32(336)
    bufs = dict(bn=BN, stat=STAT, xnf=XNFW, dead=DEAD)

    wgv = em.op("gpsimd", lambda E: E.dma_start(out=WGC[:], in_=din["wg_c"]),
                inc="wD", k=16)
    sv0 = em.op("sync", lambda E: E.dma_start(out=IOT8[:], in_=dd["iota8"]),
                inc="bD", k=16)
    sv0 = em.op("sync", lambda E: E.dma_start(out=SVALS[:], in_=din["svals"]),
                inc="bD", k=16)
    sv0 = em.op("sync", lambda E: E.dma_start(out=LT8S[0:8, :], in_=dd["lt8sT"]),
                inc="bD", k=16)
    sv0 = em.op("sync", lambda E: E.dma_start(out=OH8S[0:8, :], in_=din["oh8"]),
                inc="bD", k=16)
    sv0 = em.op("sync", lambda E: E.dma_start(out=SELS[0:128, 0:48],
                in_=din["sel_m"][0:128, :]), inc="bD", k=16)
    sv0 = em.op("sync", lambda E: E.dma_start(out=SELS[0:128, 48:96],
                in_=din["sel_m"][128:256, :]), inc="bD", k=16)
    sv0 = em.op("sync", lambda E: E.dma_start(out=SELS[0:72, 96:144],
                in_=din["sel_m"][256:328, :]), inc="bD", k=16)
    em.wait("vector", "bD", sv0)
    em.wait("tensor", "wD", wgv)
    em.wait("tensor", "bD", sv0)
    em.op("vector", lambda E: E.memset(BASE_B[:], 0.0))
    em.op("vector", lambda E: E.memset(ZR[:], 0.0))
    em.drain("vector")
    zr_vv = em.op("vector", lambda E: E.nop() if False else E.memset(DEAD[:], 0.0),
                  inc="vv")
    em.wait("gpsimd", "vv", zr_vv)
    zrg = em.op("gpsimd", lambda E: E.dma_start(
        out=dd["smap_my"].rearrange("(p f) x -> p (f x)", p=128),
        in_=ZR[:, 0:329]), inc="zD", k=16)

    xt_guard = [0, 0]
    xnb_guard = [0]
    psT_guard = 0
    xnfs_guard = 0
    xn_wr = 0
    for t in range(NT):
        b = t % 2
        em.wait("sync", "vv", xt_guard[b])
        sd = em.op("sync", lambda E, t=t, b=b: E.dma_start(
            out=XT3[b][:], in_=dd["xr3"][t * 128:(t + 1) * 128, :]),
            inc=f"xD{b}", k=16)
        em.wait("vector", f"xD{b}", sd)
        em.wait("vector", "tv", xnb_guard[0])
        if xn_wr:
            em.wait("vector", "xnS", xn_wr)
        ln_vv = _ln_tile(em, P, A, bufs, XT3[b][:], XNB[0][:], l2gamma=False)
        xt_guard[b] = em.cnt["vv"]
        em.wait("gpsimd", "vv", ln_vv)
        xn_wr = em.op("gpsimd", lambda E, t=t: E.dma_start(
            out=dd["xn_my"][t * 128:(t + 1) * 128, :], in_=XNB[0][:]),
            inc="xnS", k=16)
        em.wait("tensor", "vv", ln_vv)
        if psT_guard:
            em.wait("tensor", "vv", psT_guard)
        for c in range(8):
            f = lambda E, c=c: E.transpose(
                P["psT"][:, c * 128:(c + 1) * 128],
                XNB[0][:, c * 128:(c + 1) * 128], P["identb"][:])
            if c < 7:
                em.op("tensor", f)
            else:
                tv = em.op("tensor", f, inc="tv")
        em.wait("vector", "tv", tv)
        em.wait("vector", "tv", xnfs_guard)
        for c in range(8):
            f = lambda E, c=c: E.tensor_copy(
                XNFSg[:, c * 128:(c + 1) * 128], P["psT"][:, c * 128:(c + 1) * 128])
            if c < 7:
                em.op("vector", f)
            else:
                vv = em.op("vector", f, inc="vv")
        psT_guard = vv
        xnb_guard[0] = tv
        em.wait("tensor", "vv", vv)
        for kc in range(8):
            f = lambda E, kc=kc: E.matmul(
                ps[4][:, 0:8], XNFSg[:, kc * 128:(kc + 1) * 128],
                WGC[:, kc * 8:(kc + 1) * 8], start=(kc == 0), stop=(kc == 7))
            if kc < 7:
                em.op("tensor", f)
            else:
                tv = em.op("tensor", f, inc="tv")
        xnfs_guard = tv
        em.wait("vector", "tv", tv)
        vv = em.op("vector", lambda E: E.tensor_copy(G8[:], ps[4][:, 0:8]),
                   inc="vv")
        em.drain("vector")
        em.op("vector", lambda E: E.reduce_max(GCOL[:, 0:1], G8[:], axis=AX.X))
        em.drain("vector")
        em.op("vector", lambda E: E.tensor_single_scalar(OH[:], G8[:],
                                                         GCOL[:, 0:1], ALU.is_equal))
        em.op("vector", lambda E: E.tensor_scalar_mul(GCOL[:, 1:2], GCOL[:, 0:1],
                                                      -1.0))
        em.drain("vector")
        em.op("vector", lambda E: E.tensor_mul(T1[:], OH[:], IOT8[:]))
        em.op("vector", lambda E: E.tensor_scalar(T2[:], OH[:], -8.0, 8.0,
                                                  ALU.mult, ALU.add))
        em.drain("vector")
        em.op("vector", lambda E: E.tensor_add(T1[:], T1[:], T2[:]))
        em.drain("vector")
        em.op("vector", lambda E, t=t: E.tensor_reduce(ESTAR[:, t:t + 1], T1[:],
                                                       AX.X, ALU.min))
        em.drain("vector")
        vv = em.op("vector", lambda E, t=t: E.tensor_single_scalar(
            OH[:], IOT8[:], ESTAR[:, t:t + 1], ALU.is_equal), inc="vv")
        em.wait("scalar", "vv", vv)
        sv = em.op("scalar", lambda E: E.activation(T2[:], G8[:], AF.Exp,
                   bias=GCOL[:, 1:2], accum_out=GCOL[:, 2:3]), inc="sv")
        em.wait("vector", "sv", sv)
        em.op("vector", lambda E, t=t: E.reciprocal(G1[:, t:t + 1], GCOL[:, 2:3]))
        em.drain("vector")
        em.wait("tensor", "vv", em.cnt["vv"])
        tv = em.op("tensor", lambda E: E.matmul(ps[5][:, 0:8], P["ltincT"][:],
                   OH[:], start=True, stop=True), inc="tv")
        tv2 = em.op("tensor", lambda E: E.matmul(P["psR"][0:1, 0:8],
                    P["onesc"][:, 0:1], OH[:], start=True, stop=True), inc="tv")
        em.wait("vector", "tv", tv2)
        em.op("vector", lambda E: E.tensor_copy(CNTE[:], ps[5][:, 0:8]))
        em.op("vector", lambda E: E.tensor_copy(TOTR[0:1, :], P["psR"][0:1, 0:8]))
        em.drain("vector")
        em.op("vector", lambda E: E.tensor_add(T2[:], CNTE[:], BASE_B[:]))
        em.drain("vector")
        em.op("vector", lambda E: E.tensor_mul(T2[:], T2[:], OH[:]))
        em.drain("vector")
        vv = em.op("vector", lambda E, t=t: E.tensor_reduce(POSL[:, t:t + 1], T2[:],
                                                            AX.X, ALU.add), inc="vv")
        em.wait("tensor", "vv", vv)
        tv = em.op("tensor", lambda E: E.matmul(P["psR"][:, 8:16], P["onesr"][0:1, :],
                   TOTR[0:1, :], start=True, stop=True), inc="tv")
        em.wait("vector", "tv", tv)
        vv = em.op("vector", lambda E: E.tensor_add(BASE_B[:], BASE_B[:],
                   P["psR"][:, 8:16]), inc="vv")
        em.drain("vector")
    gat_vv = em.cnt["vv"]

    em.wait("gpsimd", "vv", gat_vv)
    cg = em.op("gpsimd", lambda E: E.dma_start(out=dd["cnts_in"],
               in_=BASE_B[0:1, 0:8]), inc="cD", k=16)
    em.wait("gpsimd", "cD", cg)
    cg = em.op("gpsimd", lambda E: E.collective_compute(
        "AllGather", ALU.bypass, RG, ins=[dd["cnts_in"]], outs=[dd["cnts_out"]]),
        inc="gv")
    em.wait("gpsimd", "gv", cg)
    cg = em.op("gpsimd", lambda E: E.dma_start(out=CNTS_SB[0:8, 0:8],
               in_=dd["cnts_out"]), inc="cD", k=16)
    em.wait("gpsimd", "xnS", em.cnt["xnS"])
    xag = em.op("gpsimd", lambda E: E.collective_compute(
        "AllGather", ALU.bypass, RG, ins=[dd["xn_my"]], outs=[dd["xn_all"]]),
        inc="gv")
    em.wait("tensor", "cD", cg)
    tv = em.op("tensor", lambda E: E.matmul(ps[4][0:8, 0:8], LT8S[0:8, 0:8],
               CNTS_SB[0:8, 0:8], start=True, stop=True), inc="tv")
    em.wait("vector", "tv", tv)
    vv = em.op("vector", lambda E: E.tensor_copy(BASEG[0:8, :], ps[4][0:8, 0:8]),
               inc="vv")
    em.wait("tensor", "vv", vv)
    tv = em.op("tensor", lambda E: E.matmul(ps[4][0:1, 0:8], OH8S[0:8, 0:1],
               BASEG[0:8, 0:8], start=True, stop=True), inc="tv")
    em.wait("vector", "tv", tv)
    vv = em.op("vector", lambda E: E.tensor_copy(BASER[0:1, :], ps[4][0:1, 0:8]),
               inc="vv")
    em.wait("tensor", "vv", vv)
    tv = em.op("tensor", lambda E: E.matmul(ps[4][:, 0:8], P["onesr"][0:1, :],
               BASER[0:1, 0:8], start=True, stop=True), inc="tv")
    em.wait("vector", "tv", tv)
    vv = em.op("vector", lambda E: E.tensor_copy(BASEB128[:], ps[4][:, 0:8]),
               inc="vv")

    em.wait("gpsimd", "zD", zrg)
    sc_wr = 0
    for t in range(NT):
        em.op("vector", lambda E, t=t: E.tensor_single_scalar(
            OH[:], IOT8[:], ESTAR[:, t:t + 1], ALU.is_equal))
        em.drain("vector")
        em.op("vector", lambda E: E.tensor_mul(T2[:], OH[:], BASEB128[:]))
        em.drain("vector")
        em.op("vector", lambda E: E.tensor_reduce(GCOL[:, 0:1], T2[:], AX.X,
                                                  ALU.add))
        em.drain("vector")
        em.op("vector", lambda E, t=t: E.tensor_add(GCOL[:, 1:2], GCOL[:, 0:1],
                                                    POSL[:, t:t + 1]))
        em.drain("vector")
        em.op("vector", lambda E: E.tensor_scalar_add(GCOL[:, 1:2], GCOL[:, 1:2],
                                                      -1.0))
        em.drain("vector")
        em.op("vector", lambda E: E.tensor_single_scalar(GCOL[:, 2:3],
              GCOL[:, 1:2], float(CAP), ALU.is_lt))
        em.drain("vector")
        em.op("vector", lambda E, t=t: E.tensor_mul(GK[:, t:t + 1], GCOL[:, 2:3],
                                                    G1[:, t:t + 1]))
        em.op("vector", lambda E, t=t: E.tensor_scalar(
            SLOT2[:, t:t + 1], ESTAR[:, t:t + 1], float(ESTRIDE), GCOL[:, 1:2],
            ALU.mult, ALU.add))
        em.drain("vector")
        em.op("vector", lambda E, t=t: E.tensor_mul(SLOT2[:, t:t + 1],
              SLOT2[:, t:t + 1], GCOL[:, 2:3]))
        em.op("vector", lambda E: E.tensor_scalar(GCOL[:, 3:4], GCOL[:, 2:3],
              -float(NSLOT), float(NSLOT), ALU.mult, ALU.add))
        em.drain("vector")
        em.op("vector", lambda E, t=t: E.tensor_add(GCOL[:, 3:4],
              SLOT2[:, t:t + 1], GCOL[:, 3:4]))
        em.drain("vector")
        if sc_wr:
            em.wait("vector", "scD", sc_wr)
        vv = em.op("vector", lambda E, t=t: E.tensor_copy(P["idx32"][:, 0:1],
                   GCOL[:, 3:4]), inc="vv")
        em.wait("gpsimd", "vv", vv)
        sc_wr = em.op("gpsimd", lambda E, t=t: E.indirect_dma_start(
            out=dd["smap_my"], out_offset=IndirectOffsetOnAxis(P["idx32"][:, 0:1], 0),
            in_=SVALS[:, t:t + 1], in_offset=None), inc="scD", k=16)
    em.wait("gpsimd", "scD", em.cnt["scD"])
    arv = em.op("gpsimd", lambda E: E.collective_compute(
        "AllReduce", ALU.add, RG, ins=[dd["smap_my"]], outs=[dd["smap_red"]]),
        inc="gv")
    em.wait("gpsimd", "gv", arv)
    mg = em.op("gpsimd", lambda E: E.dma_start(out=MAPT[:, 0:128],
               in_=dd["smap_red"].rearrange("(f p) x -> f (p x)", p=128)[0:128, :]),
               inc="cD", k=16)
    mg = em.op("gpsimd", lambda E: E.dma_start(out=MAPT[:, 128:256],
               in_=dd["smap_red"].rearrange("(f p) x -> f (p x)", p=128)[128:256, :]),
               inc="cD", k=16)
    mg = em.op("gpsimd", lambda E: E.dma_start(out=MAPT[0:72, 256:384],
               in_=dd["smap_red"].rearrange("(f p) x -> f (p x)", p=128)[256:328, :]),
               inc="cD", k=16)
    em.wait("tensor", "cD", mg)
    em.op("tensor", lambda E: E.matmul(ps[4][0:48, 0:128], SELS[0:128, 0:48],
          MAPT[0:128, 0:128], start=True, stop=False))
    em.op("tensor", lambda E: E.matmul(ps[4][0:48, 0:128], SELS[0:128, 48:96],
          MAPT[0:128, 128:256], start=False, stop=False))
    tv = em.op("tensor", lambda E: E.matmul(ps[4][0:48, 0:128], SELS[0:72, 96:144],
               MAPT[0:72, 256:384], start=False, stop=True), inc="tv")
    em.wait("vector", "tv", tv)
    vv = em.op("vector", lambda E: E.tensor_copy(MYMAP[0:48, :], ps[4][0:48, 0:128]),
               inc="vv")
    em.wait("tensor", "vv", vv)
    tv = em.op("tensor", lambda E: E.transpose(P["psR"][:, 0:48],
               MYMAP[0:48, 0:128], P["identf"][0:48, 0:48]), inc="tv")
    em.wait("vector", "tv", tv)
    mym_vv = em.op("vector", lambda E: E.tensor_copy(MYMAPT[:, 0:48],
                   P["psR"][:, 0:48]), inc="vv")

    em.join()
    A.reset(mark)
    MW1 = A.bf16(32768)
    MW2 = A.bf16(32768)
    HFM = A.bf16(16384)
    DGA = A.bf16(4096)
    DFM = A.bf16(4096)
    MB1S = A.f32(32)
    MB2B = A.bf16(1024)
    EOB = [A.f32(1024)]
    IDXD = A.f32(8)
    w1v = em.op("gpsimd", lambda E: E.dma_start(out=MW1[:], in_=din["mw1_c"]),
                inc="wD2", k=16)
    w2v = em.op("gpsimd", lambda E: E.dma_start(out=MW2[:], in_=din["mw2_c"]),
                inc="wD2", k=16)
    b1v = em.op("sync", lambda E: E.dma_start(out=MB1S[:], in_=din["mb1_t"]),
                inc="bD2", k=16)
    b2v = em.op("gpsimd", lambda E: E.dma_start(
        out=MB2B[:], in_=din["mb2_r"].to_broadcast([128, D])), inc="wD2", k=16)
    em.wait("tensor", "wD2", em.cnt["wD2"])
    em.wait("scalar", "bD2", b1v)
    em.wait("vector", "wD2", em.cnt["wD2"])
    em.op("vector", lambda E: E.memset(DGA[:], 0.0))
    em.drain("vector")
    em.wait("gpsimd", "gv", xag)
    nslab = ESTRIDE // 512 + 1
    dga_guard = 0
    gD_wr = 0
    psH_guard = [0, 0, 0, 0]
    psO_guard = [0, 0]
    hfm_guard = 0
    dfm_guard = 0
    eo_guard = 0
    for s in range(nslab):
        nsub = 4 if s < 10 else 1
        for sub in range(nsub):
            tcol = s * 4 + sub
            em.wait("vector", "vv", mym_vv)
            if dga_guard:
                em.wait("vector", "tv", dga_guard)
            em.op("vector", lambda E, tcol=tcol: E.tensor_scalar(
                IDXD[:, 0:1], MYMAPT[:, tcol:tcol + 1], -1.0, 0.0,
                ALU.add, ALU.max))
            em.drain("vector")
            if gD_wr:
                em.wait("vector", "gD", gD_wr)
            vv = em.op("vector", lambda E: E.tensor_copy(P["idx32"][:, 1:2],
                       IDXD[:, 0:1]), inc="vv")
            em.drain("vector")
            em.wait("gpsimd", "vv", vv)
            gD_wr = em.op("gpsimd", lambda E, sub=sub: E.indirect_dma_start(
                out=DGA[:, sub * 1024:(sub + 1) * 1024], out_offset=None,
                in_=dd["xn_all"],
                in_offset=IndirectOffsetOnAxis(P["idx32"][:, 1:2], 0)),
                inc="gD", k=16)
        em.wait("tensor", "gD", gD_wr)
        for sub in range(nsub):
            if psT_guard:
                em.wait("tensor", "vv", psT_guard)
            for c in range(8):
                f = lambda E, c=c, sub=sub: E.transpose(
                    P["psT"][:, c * 128:(c + 1) * 128],
                    DGA[:, sub * 1024 + c * 128: sub * 1024 + (c + 1) * 128],
                    P["identb"][:])
                if c < 7:
                    em.op("tensor", f)
                else:
                    tv = em.op("tensor", f, inc="tv")
            em.wait("vector", "tv", tv)
            em.wait("vector", "tv", hfm_guard)
            for c in range(8):
                f = lambda E, c=c, sub=sub: E.tensor_copy(
                    DFM[:, c * 512 + sub * 128: c * 512 + (sub + 1) * 128],
                    P["psT"][:, c * 128:(c + 1) * 128])
                if c < 7:
                    em.op("vector", f)
                else:
                    vv = em.op("vector", f, inc="vv")
            psT_guard = vv
        dga_guard = em.cnt["tv"]
        dfm_vv = em.cnt["vv"]
        em.wait("tensor", "vv", dfm_vv)
        em.wait("scalar", "tv", hfm_guard)
        for ic in range(32):
            pb = ic % 4
            if psH_guard[pb]:
                em.wait("tensor", "sv", psH_guard[pb])
            for kc in range(8):
                f = lambda E, kc=kc, ic=ic, pb=pb: E.matmul(
                    ps[pb][:], MW1[:, kc * 4096 + ic * 128: kc * 4096 + (ic + 1) * 128],
                    DFM[:, kc * 512:(kc + 1) * 512], start=(kc == 0), stop=(kc == 7))
                if kc < 7:
                    em.op("tensor", f)
                else:
                    tv = em.op("tensor", f, inc="tv")
            em.wait("scalar", "tv", tv)
            psH_guard[pb] = em.op("scalar", lambda E, ic=ic, pb=pb: E.activation(
                HFM[:, ic * 512:(ic + 1) * 512], ps[pb][:], AF.Relu,
                bias=MB1S[:, ic:ic + 1]), inc="sv")
        h_sv = em.cnt["sv"]
        dfm_guard = em.cnt["tv"]
        em.wait("tensor", "sv", h_sv)
        for sub in range(nsub):
            tcol = s * 4 + sub
            for half in range(2):
                if psO_guard[half]:
                    em.wait("tensor", "vv", psO_guard[half])
                for ic in range(32):
                    f = lambda E, ic=ic, sub=sub, half=half: E.matmul(
                        ps[4 + half][:],
                        HFM[:, ic * 512 + sub * 128: ic * 512 + (sub + 1) * 128],
                        MW2[:, ic * 1024 + half * 512: ic * 1024 + (half + 1) * 512],
                        start=(ic == 0), stop=(ic == 31))
                    if ic < 31:
                        em.op("tensor", f)
                    else:
                        tv = em.op("tensor", f, inc="tv")
                em.wait("vector", "tv", tv)
                if eo_guard:
                    em.wait("vector", "eoS", eo_guard)
                psO_guard[half] = em.op("vector", lambda E, half=half: E.tensor_add(
                    EOB[0][:, half * 512:(half + 1) * 512], ps[4 + half][:],
                    MB2B[:, half * 512:(half + 1) * 512]), inc="vv")
            em.wait("gpsimd", "vv", em.cnt["vv"])
            eo_guard = em.op("gpsimd", lambda E, tcol=tcol: E.dma_start(
                out=dd["eo_my"][tcol * 128:(tcol + 1) * 128, :], in_=EOB[0][:]),
                inc="eoS", k=16)
        hfm_guard = em.cnt["tv"]
    em.wait("gpsimd", "eoS", em.cnt["eoS"])
    eag = em.op("gpsimd", lambda E: E.collective_compute(
        "AllGather", ALU.bypass, RG, ins=[dd["eo_my"]], outs=[dd["eo_all"]]),
        inc="gv")

    em.join()
    A.reset(mark)
    FXT = [A.f32(1024), A.f32(1024)]
    FG = [A.bf16(1024), A.bf16(1024)]
    YO = [A.f32(1024), A.f32(1024)]
    em.wait("vector", "gv", eag)
    em.op("vector", lambda E: E.memset(FG[0][:], 0.0))
    vv0 = em.op("vector", lambda E: E.memset(FG[1][:], 0.0), inc="vv")
    fxt_guard = [0, 0]
    fg_guard = [0, 0]
    yo_guard = [0, 0]
    for t in range(NT):
        b = t % 2
        em.wait("sync", "vv", fxt_guard[b])
        sd = em.op("sync", lambda E, t=t, b=b: E.dma_start(
            out=FXT[b][:], in_=dd["xr3"][t * 128:(t + 1) * 128, :]),
            inc=f"fD{b}", k=16)
        em.wait("vector", "vv", vv0)
        if fg_guard[b]:
            em.wait("vector", f"gF{b}", fg_guard[b])
        vv = em.op("vector", lambda E, t=t, b=b: E.tensor_copy(
            P["idx32"][:, 2 + b:3 + b], SLOT2[:, t:t + 1]), inc="vv")
        em.wait("gpsimd", "vv", vv)
        fg_guard[b] = em.op("gpsimd", lambda E, b=b: E.indirect_dma_start(
            out=FG[b][:], out_offset=None,
            in_=dd["eo_all"],
            in_offset=IndirectOffsetOnAxis(P["idx32"][:, 2 + b:3 + b], 0)),
            inc=f"gF{b}", k=16)
        em.wait("vector", f"gF{b}", fg_guard[b])
        em.wait("vector", f"fD{b}", sd)
        if yo_guard[b]:
            em.wait("vector", f"yD{b}", yo_guard[b])
        vv = em.op("vector", lambda E, t=t, b=b: E.scalar_tensor_tensor(
            YO[b][:], FG[b][:], GK[:, t:t + 1], FXT[b][:], ALU.mult, ALU.add),
            inc="vv")
        fxt_guard[b] = vv
        em.wait("gpsimd", "vv", vv)
        yo_guard[b] = em.op("gpsimd", lambda E, t=t, b=b: E.dma_start(
            out=dd["y"][t * 128:(t + 1) * 128, :], in_=YO[b][:]),
            inc=f"yD{b}", k=16)
    A.reset()


_DEBUG_STREAM = None


def kernel(**inputs):
    maps = [host_prep(inputs, c) for c in range(NCORE)]
    nc = build_program(debug_stream=_DEBUG_STREAM)
    res = run_bass_kernel_spmd(nc, maps, core_ids=list(range(NCORE)))
    out = np.stack([np.asarray(res.results[c]["y"], dtype=np.float32)
                    for c in range(NCORE)], axis=0)
    return out
